# version 11
# baseline (speedup 1.0000x reference)
import numpy as np
import ml_dtypes
from contextlib import ExitStack

import concourse.bass as bass
import concourse.mybir as mybir
from concourse.bass_utils import run_bass_kernel_spmd

F32 = mybir.dt.float32
BF16 = mybir.dt.bfloat16
AF = mybir.ActivationFunctionType
ALU = mybir.AluOpType
AX = mybir.AxisListType

NB = 8
T = 2048
D = 1024
DEPTH = 6
NHEAD = 16
DH = 64
WIN = 256
DI = 2730
NW = T // WIN
NT = T // 128
DC = D // 128
HG = 2
HGH = NHEAD // HG
NEG = -3.0e37
NKC = (DI + 127) // 128

MG0, MG1, MW00, MW01, MTRI = 0, 384, 896, 1152, 1408


def _host_consts():
    half = np.arange(0, DH, 2, dtype=np.float64)
    inv_freq = 1.0 / (10000.0 ** (half / DH))
    t = np.arange(T, dtype=np.float64)
    freqs = np.concatenate([t[:, None] * inv_freq[None, :]] * 2, -1)
    base_scale = (half + 0.4 * DH) / (1.4 * DH)
    power = (t - T // 2) / (WIN // 2)
    scale = np.concatenate([base_scale[None, :] ** power[:, None]] * 2, -1)
    cos = np.cos(freqs); sin = np.sin(freqs)
    sign = np.where(np.arange(DH) < DH // 2, -1.0, 1.0)[None, :]
    rot = lambda a: np.roll(a, -32, axis=1)
    rep = lambda a: np.tile(a, (1, HGH)).astype(ml_dtypes.bfloat16)
    tables = np.stack([rep(cos * scale), rep(rot(sign * sin * scale)),
                       rep(cos / scale), rep(rot(sign * sin / scale))])
    p = np.arange(128)[:, None]

    def band(ncols, lo, hi):
        j = np.arange(ncols)[None, :]
        return np.where((j >= lo) & (j <= hi), 0.0, NEG).astype(np.float32)
    masks = np.concatenate([
        band(384, p, p + WIN),
        band(512, p + 128, p + 384),
        band(256, 0 * p, p),
        band(256, 0 * p, p + 128),
        band(128, 0 * p, p),
    ], axis=1)
    return dict(
        tables=np.ascontiguousarray(tables),
        masks=masks,
        masksb=masks.astype(ml_dtypes.bfloat16),
        identF=np.eye(128, dtype=np.float32),
        identB=np.eye(128, dtype=np.float32).astype(ml_dtypes.bfloat16),
        c1024=np.full((128, 1), 1.0 / 1024.0, dtype=ml_dtypes.bfloat16),
        row8=np.full((1, 128), 8.0, dtype=np.float32),
        row1=np.full((1, 128), 1.0, dtype=np.float32),
        epscol=np.full((128, 1), 1.0e-5, dtype=np.float32),
        eps24col=np.full((128, 1), 1.0e-24, dtype=np.float32),
    )


class SemC:
    __slots__ = ("h", "val")
    def __init__(self, h):
        self.h = h
        self.val = 0


ENGS = ("sync", "gpsimd", "scalar", "vector", "tensor")


class Prog:
    def __init__(self):
        self.ops = {e: [] for e in ENGS}
        self.waited = {}
        self.esem = {}
        self.bank_free = [(None, 0)] * 8
        self.bank_ready = [(None, 0)] * 8

    def do(self, eng, fn, inc=None, n=1):
        if eng in ("vector", "scalar") or (eng == "gpsimd" and n != 16):
            es = self.esem[eng]
            assert inc is None or inc is es
            self.wait(eng, es, es.val)
            inc, n = es, 1
        if inc is None:
            self.ops[eng].append(fn)
            return None
        inc.val += n
        h, v = inc.h, inc.val
        self.ops[eng].append(lambda e, fn=fn, h=h, n=n: fn(e).then_inc(h, n))
        return v

    def wait(self, eng, sem, val):
        if sem is None or val is None or val <= 0:
            return
        key = (eng, id(sem))
        if self.waited.get(key, 0) >= val:
            return
        self.waited[key] = val
        h = sem.h
        self.ops[eng].append(lambda e, h=h, val=val: e.wait_ge(h, val))

    def wait_t(self, eng, tok):
        if tok is not None:
            self.wait(eng, tok[0], tok[1])

    def flush(self, block):
        for eng in ENGS:
            ops = self.ops[eng]
            if not ops:
                continue
            def body(e, ops=ops):
                for fn in ops:
                    fn(e)
            getattr(block, eng)(body)


def build_program(depth=DEPTH, mode="full"):
    nc = bass.Bass("TRN2", target_bir_lowering=False, debug=False)

    def din(name, shape, dt=F32):
        return nc.dram_tensor(name, list(shape), dt, kind="ExternalInput").ap()

    x0 = din("x0", [T, D])
    wq_d = din("wq", [D, D]); wk_d = din("wk", [D, D]); wv_d = din("wv", [D, D])
    wqkv_d = din("wqkv", [DEPTH, D, 3 * D])
    qsc_d = din("qsc", [DEPTH, DH]); ksc_d = din("ksc", [DEPTH, DH])
    wo_d = din("wo", [DEPTH, D, D])
    lng_d = din("lng", [DEPTH, D]); lnb_d = din("lnb", [DEPTH, D])
    w1_d = din("w1", [DEPTH, D, 2 * DI]); b1_d = din("b1", [DEPTH, 2 * DI])
    w2_d = din("w2", [DEPTH, DI, D]); b2_d = din("b2", [DEPTH, D])
    tbl_d = din("tables", [4, T, HGH * DH], BF16)
    masks_d = din("masks", [128, 1536])
    masksb_d = din("masksb", [128, 1536], BF16)
    identF_d = din("identF", [128, 128])
    identB_d = din("identB", [128, 128], BF16)
    c1024_d = din("c1024", [128, 1], BF16)
    row8_d = din("row8", [1, 128]); row1_d = din("row1", [1, 128])
    eps_d = din("epscol", [128, 1])
    eps24_d = din("eps24col", [128, 1])

    y_d = nc.dram_tensor("y", [T, D], F32, kind="ExternalOutput").ap()
    x1s_d = nc.dram_tensor("x1s", [T, D], F32, kind="Internal").ap()
    rstash_d = nc.dram_tensor("rstash", [128, DC * T], F32, kind="Internal").ap()
    dump = mode.startswith("L") or mode.startswith("A")
    attn_only = mode.startswith("A")
    nlayers = depth if mode in ("full",) else (0 if mode == "outer" else int(mode[1:]) + 1)
    if dump:
        ydbg_d = nc.dram_tensor("ydbg", [128, DC * T], F32, kind="ExternalOutput").ap()

    with ExitStack() as ctx:
        A32 = ctx.enter_context(nc.sbuf_tensor("A32", [128, 24064], F32))
        A16 = ctx.enter_context(nc.sbuf_tensor("A16", [128, 50816], BF16))
        sb_idF = ctx.enter_context(nc.sbuf_tensor("sb_idF", [128, 128], F32))
        sb_idB = ctx.enter_context(nc.sbuf_tensor("sb_idB", [128, 128], BF16))
        sb_c1024 = ctx.enter_context(nc.sbuf_tensor("sb_c1024", [128, 1], BF16))
        sb_row8 = ctx.enter_context(nc.sbuf_tensor("sb_row8", [1, 128], F32))
        sb_row1 = ctx.enter_context(nc.sbuf_tensor("sb_row1", [1, 128], F32))
        sb_eps = ctx.enter_context(nc.sbuf_tensor("sb_eps", [128, 1], F32))
        sb_eps24 = ctx.enter_context(nc.sbuf_tensor("sb_eps24", [128, 1], F32))
        sb_mskb = ctx.enter_context(nc.sbuf_tensor("sb_mskb", [128, 1536], BF16))
        PS = ctx.enter_context(nc.psum_tensor("PS", [128, 4096], F32))

        R_RES = 0
        R_MASK = 16384
        R_A = 17920
        R_SM = 22016
        R_W = 0
        R_KT = 12288
        R_AT = 16384
        R_TAB = 24576
        R_V = 32768
        R_XS = 34816
        R_QW = 36864
        R_QT2 = 38912
        R_EP = 40960
        R_WO = 43264
        R_SP = 47360
        sm = {}
        _p = [R_SM]
        for nm, c in [("ssq", 32), ("ssk", 16), ("negmax", 8), ("sumexp", 8),
                      ("recip", 8), ("murow", 512), ("rstdrow", 512),
                      ("qsrow", 512), ("gcol", 8), ("bcol", 8), ("b2col", 8),
                      ("b1a", 24), ("b1g", 24), ("t0", 8), ("t1", 8), ("t2", 8),
                      ("bnd", 8)]:
            sm[nm] = _p[0]; _p[0] += c
        assert _p[0] <= R_SM + 2048

        def a32(base, cols, pr=128, po=0):
            return A32[po:po + pr, base:base + cols]

        def a16(base, cols, pr=128, po=0):
            return A16[po:po + pr, base:base + cols]

        def bank(b, cols=512, off=0, pr=128, po=0):
            return PS[po:po + pr, b * 512 + off:b * 512 + off + cols]

        sQ = SemC(ctx.enter_context(nc.semaphore("sQ")))
        qC = SemC(ctx.enter_context(nc.semaphore("qC")))
        qX0 = [SemC(ctx.enter_context(nc.semaphore("qX0a"))),
               SemC(ctx.enter_context(nc.semaphore("qX0b")))]
        qST = SemC(ctx.enter_context(nc.semaphore("qST")))
        sG = SemC(ctx.enter_context(nc.semaphore("sG")))
        sT = SemC(ctx.enter_context(nc.semaphore("sT")))
        sV = SemC(ctx.enter_context(nc.semaphore("sV")))
        sS = SemC(ctx.enter_context(nc.semaphore("sS")))
        sO = SemC(ctx.enter_context(nc.semaphore("sO")))
        sGC = SemC(ctx.enter_context(nc.semaphore("sGC")))

        block = ctx.enter_context(nc.Block())
        P = Prog()
        P.esem = {"vector": sV, "scalar": sS, "gpsimd": sGC}

        def mm_acc(b, cols, pairs, off=0, opr=128, opo=0):
            P.wait_t("tensor", P.bank_free[b])
            n = len(pairs)
            for i, (l, r) in enumerate(pairs):
                P.do("tensor",
                     lambda e, o=bank(b, cols, off, opr, opo), l=l, r=r,
                     st=(i == 0), sp=(i == n - 1):
                     e.matmul(o, l, r, start=st, stop=sp),
                     sT if i == n - 1 else None, 1)
            P.bank_ready[b] = (sT, sT.val)
            return sT.val

        def bank_read_done(b, sem, val):
            P.bank_free[b] = (sem, val)

        for dst, src in [(sb_idF[:], identF_d), (sb_idB[:], identB_d),
                         (sb_c1024[:], c1024_d), (sb_row8[:], row8_d),
                         (sb_row1[:], row1_d), (sb_eps[:], eps_d),
                         (sb_eps24[:], eps24_d),
                         (sb_mskb[:], masksb_d), (a32(R_MASK, 1536), masks_d)]:
            P.do("sync", lambda e, d=dst, s=src: e.dma_start(out=d, in_=s), qC, 16)
        consts_q = qC.val

        OX0T = 0
        OWB = 16384
        OKT = 24576
        OSTG = 40960
        OQR = 16384
        OVR = 0
        OE = 18432
        OPT = 20480
        O_X0R = [R_A, R_A + 1024]
        O_ST = R_A + 2048
        O_X1T = R_A + 3072

        qts_d = nc.dram_tensor("qts", [NT, 128, D], BF16, kind="Internal").ap()
        vs_d = nc.dram_tensor("vs", [NT, 128, D], BF16, kind="Internal").ap()
        qQS = [SemC(ctx.enter_context(nc.semaphore("qQSa"))),
               SemC(ctx.enter_context(nc.semaphore("qQSb")))]
        qQR = [SemC(ctx.enter_context(nc.semaphore("qQRa"))),
               SemC(ctx.enter_context(nc.semaphore("qQRb")))]
        qVR = [SemC(ctx.enter_context(nc.semaphore(f"qVR{i}"))) for i in range(4)]

        x0_ld = []
        P.wait("tensor", qC, consts_q)
        ring_free = [None, None]
        for tt in range(NT):
            P.wait_t("sync", ring_free[tt % 2])
            dst = a32(O_X0R[tt % 2], 1024)
            src = x0[tt * 128:(tt + 1) * 128, :]
            x0_ld.append(P.do("sync", lambda e, d=dst, s=src: e.dma_start(out=d, in_=s),
                              qX0[tt % 2], 16))
            P.wait("tensor", qX0[tt % 2], x0_ld[tt])
            P.wait_t("tensor", ring_free[tt % 2])
            for dcg in range(2):
                b = dcg
                P.wait_t("tensor", P.bank_free[b])
                for k in range(4):
                    dc = dcg * 4 + k
                    src2 = a32(O_X0R[tt % 2], 1024)[:, dc * 128:dc * 128 + 128]
                    P.do("tensor",
                         lambda e, o=bank(b, 128, off=k * 128), s=src2:
                         e.matmul(o, s, sb_idF[:], start=True, stop=True),
                         sT if k == 3 else None, 1)
                mmv = sT.val
                P.wait("scalar", sT, mmv)
                last = None
                for k in range(4):
                    dc = dcg * 4 + k
                    dst2 = a16(OX0T + dc * T + tt * 128, 128)
                    last = P.do("scalar",
                                lambda e, d=dst2, s=bank(b, 128, off=k * 128):
                                e.copy(d, s), sS, 1)
                bank_read_done(b, sS, last)
            ring_free[tt % 2] = (sS, sS.val)
        x0T_done = (sS, sS.val)

        def wload(w_ap, prev_consumed=None):
            if prev_consumed is not None:
                P.wait_t("gpsimd", prev_consumed)
            for dc in range(DC):
                src = w_ap[dc * 128:(dc + 1) * 128, :]
                dst = a16(OWB + dc * 1024, 1024)
                P.do("gpsimd", lambda e, d=dst, s=src: e.dma_start(out=d, in_=s), sG, 16)
            return sG.val

        wld = wload(wk_d)
        P.wait("tensor", sG, wld)
        P.wait_t("tensor", x0T_done)
        for mc in range(DC):
            for tg in range(4):
                b = 2 + ((mc * 4 + tg) % 2)
                pairs = [(a16(OWB + dc * 1024 + mc * 128, 128),
                          a16(OX0T + dc * T + tg * 512, 512)) for dc in range(DC)]
                mmv = mm_acc(b, 512, pairs)
                P.wait("scalar", sT, mmv)
                ev = P.do("scalar", lambda e, d=a16(OKT + mc * T + tg * 512, 512),
                          s=bank(b): e.copy(d, s), sS, 1)
                bank_read_done(b, sS, ev)
        kt_done = sS.val

        stg_free = [None, None]

        def proj_to_hbm(which, wld_mark):
            P.wait("tensor", sG, wld_mark)
            n_out = NT * 2 if which == "v" else DC * 4
            for i in range(n_out):
                b = 2 + (i % 2)
                sl = i % 2
                if which == "v":
                    tt, vg = i // 2, i % 2
                    tt, vg = divmod(i, 2)
                    pairs = [(a16(OX0T + dc * T + tt * 128, 128),
                              a16(OWB + dc * 1024 + vg * 512, 512)) for dc in range(DC)]
                else:
                    mc, tg = divmod(i, 4)
                    pairs = [(a16(OWB + dc * 1024 + mc * 128, 128),
                              a16(OX0T + dc * T + tg * 512, 512)) for dc in range(DC)]
                mmv = mm_acc(b, 512, pairs)
                P.wait("scalar", sT, mmv)
                P.wait_t("scalar", stg_free[sl])
                ev = P.do("scalar", lambda e, d=a16(OSTG + sl * 512, 512), s=bank(b):
                          e.copy(d, s), sS, 1)
                bank_read_done(b, sS, ev)
                P.wait("sync", sS, ev)
                if which == "v":
                    dd = vs_d[tt, :, vg * 512:(vg + 1) * 512]
                else:
                    mc, tg = divmod(i, 4)
                    dd = qts_d[tg * 4:(tg + 1) * 4, :, mc * 128:(mc + 1) * 128]                         .rearrange("q p c -> p q c")
                dmam = P.do("sync", lambda e, d=dd, s=a16(OSTG + sl * 512, 512)
                            .rearrange("p (q c) -> p q c", q=4 if which == "q" else 1):
                            e.dma_start(out=d, in_=s), qQS[sl], 16)
                stg_free[sl] = (qQS[sl], dmam)
            return (qQS[0].val, qQS[1].val)

        wld = wload(wv_d, prev_consumed=(sS, kt_done))
        v_done = proj_to_hbm("v", wld)
        wld = wload(wq_d, prev_consumed=(sS, sS.val))
        q_done = proj_to_hbm("q", wld)

        x1_qdma = []
        qr_free = [None, None]
        vloads = {}
        P.wait("sync", qQS[0], q_done[0])
        P.wait("sync", qQS[1], q_done[1])
        for qt in range(NT):
            nk = (qt + 1) * 128
            njg = (nk + 511) // 512
            xd = P.do("sync", lambda e, d=a32(O_X0R[0], 1024),
                      s=x0[qt * 128:(qt + 1) * 128, :]: e.dma_start(out=d, in_=s),
                      qX0[0], 16)
            qsl = qt % 2
            P.wait_t("sync", qr_free[qsl])
            qd = P.do("sync", lambda e, d=a16(OQR + qsl * 1024, 1024),
                      s=qts_d[qt]: e.dma_start(out=d, in_=s), qQR[qsl], 16)
            qs4 = qVR[qt % 4]
            P.wait("sync", qs4, qs4.val)
            vm = P.do("sync", lambda e, d=a16(OVR + qt * 1024, 1024),
                      s=vs_d[qt]: e.dma_start(out=d, in_=s), qs4, 16)
            vloads[qt] = (qs4, vm)
            P.wait("tensor", qQR[qsl], qd)
            for jg in range(njg):
                w_ = min(512, nk - jg * 512)
                pairs = [(a16(OQR + qsl * 1024 + mc * 128, 128),
                          a16(OKT + mc * T + jg * 512, w_)) for mc in range(DC)]
                mm_acc(jg, w_, pairs)
            s_mm = sT.val
            P.wait("vector", sT, s_mm)
            dsl = PS[:, qt * 128:qt * 128 + 128]
            mski = P.do("vector", lambda e, d=dsl: e.tensor_tensor(
                d, d, a32(R_MASK + MTRI, 128), ALU.add), sV, 1)
            P.wait("vector", sV, mski)
            nm = a32(sm["negmax"], 1)
            P.do("vector", lambda e, o=nm, n=nk: e.tensor_reduce(
                o, PS[:, 0:n], AX.X, ALU.max, negate=True), sV, 1)
            red_v = sV.val
            P.wait("scalar", sV, red_v)
            nm32 = a32(sm["t0"], 1)
            nm32_s = P.do("scalar", lambda e, o=nm32, i=nm: e.activation(
                o, i, AF.Copy, scale=1.0 / 32.0), sS, 1)
            P.wait("scalar", sS, nm32_s)
            sxp = a32(sm["sumexp"], 1)
            exp_s = P.do("scalar", lambda e, o=a16(OE, nk), n=nk, b=nm32, sx=sxp:
                         e.activation(o, PS[:, 0:n], AF.Exp, bias=b, scale=1.0 / 32.0,
                                      accum_out=sx), sS, 1)
            for b_ in range(njg):
                bank_read_done(b_, sS, exp_s)
            P.wait("vector", sS, exp_s)
            rc = a32(sm["recip"], 1)
            P.do("vector", lambda e, o=rc, i=sxp: e.reciprocal(o, i), sV, 1)
            rcv = sV.val
            P.wait("vector", sV, rcv)
            pn = P.do("vector", lambda e, o=a16(OE, nk), i=a16(OE, nk), r=rc:
                      e.tensor_scalar(o, i, r, None, ALU.mult), sV, 1)
            P.wait("tensor", sV, pn)
            for jc in range(qt + 1):
                b = 4 + ((jc // 2) % 2)
                off = (jc % 2) * 128
                if jc % 2 == 0:
                    P.wait_t("tensor", P.bank_free[b])
                mm = P.do("tensor", lambda e, o=bank(b, 128, off=off),
                          l=a16(OE + jc * 128, 128):
                          e.matmul(o, l, sb_idB[:], start=True, stop=True), sT, 1)
                if jc % 2 == 1 or jc == qt:
                    P.wait("scalar", sT, mm)
                    wdt = 128 * (2 if jc % 2 == 1 else 1)
                    o0 = (jc // 2) * 256
                    ev = P.do("scalar", lambda e, d=a16(OPT + o0, wdt),
                              s=bank(b, wdt): e.copy(d, s), sS, 1)
                    bank_read_done(b, sS, ev)
            pt_ev = sS.val
            P.wait("tensor", sS, pt_ev)
            for jc in range(qt + 1):
                P.wait_t("tensor", vloads[jc])
            for dg in range(2):
                pairs = [(a16(OPT + jc * 128, 128),
                          a16(OVR + jc * 1024 + dg * 512, 512))
                         for jc in range(qt + 1)]
                mm_acc(6 + dg, 512, pairs)
            pv_mm = sT.val
            qr_free[qsl] = (sT, s_mm)
            P.wait("vector", sT, pv_mm)
            P.wait("vector", qX0[0], xd)
            for dg in range(2):
                ev = P.do("vector", lambda e, o=a32(O_ST + dg * 512, 512),
                          p_=bank(6 + dg), xs=a32(O_X0R[0] + dg * 512, 512):
                          e.tensor_tensor(o, p_, xs, ALU.add), sV, 1)
                bank_read_done(6 + dg, sV, ev)
            st_v = sV.val
            P.wait("vector", sV, st_v)
            stile = a32(O_ST, 1024)
            negmu = a32(sm["t1"], 1)
            P.do("vector", lambda e, o=negmu, i=stile: e.tensor_reduce(
                o, i, AX.X, ALU.add, negate=True), sV, 1)
            P.wait("vector", sV, sV.val)
            P.do("vector", lambda e, o=negmu, i=negmu: e.tensor_scalar(
                o, i, 1.0 / 1024.0, None, ALU.mult), sV, 1)
            mu_v = sV.val
            P.wait("scalar", sV, mu_v)
            ssq = a32(sm["t2"], 1)
            P.do("scalar", lambda e, o=a32(O_X1T, 1024), i=stile, b=negmu, sx=ssq:
                 e.activation(o, i, AF.Square, bias=b, accum_out=sx), sS, 1)
            P.do("scalar", lambda e, o=ssq, i=ssq: e.activation(
                o, i, AF.Ln, scale=1.0 / 1024.0, bias=sb_eps[:]), sS, 1)
            P.do("scalar", lambda e, o=ssq, i=ssq: e.activation(
                o, i, AF.Exp, scale=-0.5), sS, 1)
            rstd_s = sS.val
            P.wait("vector", sS, rstd_s)
            P.do("vector", lambda e, o=a32(sm["t0"], 1), i=negmu, j=ssq:
                 e.tensor_tensor(o, i, j, ALU.mult), sV, 1)
            m2_v = sV.val
            P.wait("vector", sV, m2_v)
            x1v = P.do("vector", lambda e, o=a32(O_X1T, 1024), i=stile, r=ssq,
                       m2=a32(sm["t0"], 1):
                       e.tensor_scalar(o, i, r, m2, ALU.mult, ALU.add), sV, 1)
            P.wait("sync", sV, x1v)
            x1_qdma.append(P.do("sync", lambda e, s=a32(O_X1T, 1024),
                                d=x1s_d[qt * 128:(qt + 1) * 128, :]:
                                e.dma_start(out=d, in_=s), qST, 16))
            if mode == "outer":
                P.do("sync", lambda e, s=a32(O_X1T, 1024),
                     d=y_d[qt * 128:(qt + 1) * 128, :]: e.dma_start(out=d, in_=s),
                     sO, 16)
            P.wait("tensor", sV, x1v)
            for dcg in range(2):
                b = dcg
                P.wait_t("tensor", P.bank_free[b])
                for k in range(4):
                    dc = dcg * 4 + k
                    src = a32(O_X1T, 1024)[:, dc * 128:dc * 128 + 128]
                    P.do("tensor", lambda e, o=bank(b, 128, off=k * 128), s=src:
                         e.matmul(o, s, sb_idF[:], start=True, stop=True),
                         sT if k == 3 else None, 1)
                mmv = sT.val
                P.wait("vector", sT, mmv)
                ev = None
                for k in range(4):
                    dc = dcg * 4 + k
                    dst = a32(R_RES + dc * T + qt * 128, 128)
                    ev = P.do("vector", lambda e, d=dst, s=bank(b, 128, off=k * 128):
                              e.tensor_copy(d, s), sV, 1)
                bank_read_done(b, sV, ev)
            P.wait("sync", sV, st_v)
            P.wait("scalar", qST, qST.val)
            P.wait("scalar", sO, sO.val)
            P.wait("scalar", sT, sT.val)
        outer_done_v = sV.val
        outer_done_s = sS.val
        outer_done_t = sT.val

        QS8B, KS8B = R_A + 2048, R_A + 3072
        TMPA = R_A
        XQW, XKW = R_QW, R_QW + 1024
        XQTR = [R_QT2, R_QT2 + 1024]
        EP_E = [R_EP, R_EP + 768]
        EP_PT = R_EP + 1536
        MID = [R_W + i * 1024 for i in range(12)] + \
              [R_KT + i * 1024 for i in range(4)] + \
              [R_QT2 + i * 1024 for i in range(2)] + \
              [R_TAB + i * 1024 for i in range(2)] + \
              [R_SP + i * 1024 for i in range(2)]
        W1RING = [R_TAB + 2048, R_TAB + 4096]
        W2RING = [R_TAB + 6144, R_TAB + 7168, R_EP, R_EP + 1024]
        XSF = R_XS

        qWQ = SemC(ctx.enter_context(nc.semaphore("qWQ")))
        qTB = [SemC(ctx.enter_context(nc.semaphore(f"qTB{i}"))) for i in range(2)]
        qWO = SemC(ctx.enter_context(nc.semaphore("qWO")))
        qW1 = [SemC(ctx.enter_context(nc.semaphore(f"qW1{i}"))) for i in range(2)]
        qW2 = [SemC(ctx.enter_context(nc.semaphore(f"qW2{i}"))) for i in range(4)]
        qMS = SemC(ctx.enter_context(nc.semaphore("qMS")))
        qRS = SemC(ctx.enter_context(nc.semaphore("qRS")))
        qSC = SemC(ctx.enter_context(nc.semaphore("qSC")))
        qXS = SemC(ctx.enter_context(nc.semaphore("qXS")))

        state = dict(tab_free=[None, None], w1_free=[None, None],
                     w2_free=[None] * 4,
                     v_free=[None] * 4, smalls_free=None, res_tok=None, wo_free=None,
                     sq_free=[None] * 3, b_free=[None] * 3, chain_seq=[0])
        SQP = [R_A, R_A + 512, R_MASK]
        BP = [R_A + 1024, R_A + 1536, R_MASK + 512]

        def emit_inner(ell):
            if ell == 0:
                for eng in ("gpsimd", "sync", "tensor", "vector", "scalar"):
                    P.wait(eng, sT, outer_done_t)
                    P.wait(eng, sV, outer_done_v)
                    P.wait(eng, sS, outer_done_s)
                P.wait("sync", qST, qST.val)

            P.wait_t("sync", state["smalls_free"])
            qs_rep = qsc_d[ell][None, None, :].broadcast_to([128, HGH, DH])
            ks_rep = ksc_d[ell][None, None, :].broadcast_to([128, HGH, DH])
            P.wait("sync", qSC, qSC.val)
            for dst, srcp in [
                (a32(QS8B, 512).rearrange("p (a b) -> p a b", a=HGH), qs_rep),
                (a32(KS8B, 512).rearrange("p (a b) -> p a b", a=HGH), ks_rep),
            ]:
                P.do("sync", lambda e, d=dst, s=srcp: e.dma_start(out=d, in_=s), qSC, 16)
            sc_ld = qSC.val
            P.wait("vector", qSC, qSC.val)
            P.do("vector", lambda e, o=a32(QS8B, 512), i=a32(QS8B, 512):
                 e.tensor_scalar(o, i, 8.0, None, ALU.mult))
            qs8_v = sV.val
            P.wait_t("sync", state["res_tok"])
            colspecs = [
                (sm["gcol"], 8, lng_d[ell].rearrange("(c p) -> c p", p=128), None),
                (sm["bcol"], 8, lnb_d[ell].rearrange("(c p) -> c p", p=128), None),
                (sm["b2col"], 8, b2_d[ell].rearrange("(c p) -> c p", p=128), None),
                (sm["b1a"], 22, b1_d[ell, 0:2816].rearrange("(c p) -> c p", p=128), None),
                (sm["b1g"], 22, b1_d[ell, 2730:5418].rearrange("(c p) -> c p", p=128),
                 b1_d[ell, 5418:5460][None, :]),
            ]
            for dcol, ncc, srcp, tail in colspecs:
                ld = P.do("sync", lambda e, d=a32(TMPA, 128, pr=ncc if tail is None
                          else ncc - 1), s=srcp: e.dma_start(out=d, in_=s), qMS, 16)
                if tail is not None:
                    ld = P.do("sync", lambda e, d=a32(TMPA, 42, pr=1, po=ncc - 1),
                              s=tail: e.dma_start(out=d, in_=s), qMS, 16)
                P.wait("tensor", qMS, ld)
                P.wait_t("tensor", P.bank_free[5])
                mm = P.do("tensor", lambda e, o=bank(5, ncc), l=a32(TMPA, 128, pr=ncc),
                          r=sb_idF[0:ncc, 0:ncc]: e.matmul(o, l, r, start=True, stop=True),
                          sT, 1)
                P.wait("scalar", sT, mm)
                ev = P.do("scalar", lambda e, d=a32(dcol, ncc), s=bank(5, ncc):
                          e.copy(d, s))
                bank_read_done(5, sS, ev)
                P.wait("sync", sT, mm)
            smalls_ld = qMS.val
            smalls_s = sS.val

            P.wait_t("sync", state["res_tok"])
            P.wait("sync", qXS, qXS.val)
            rs = P.do("sync", lambda e, d=rstash_d, s=a32(R_RES, DC * T):
                      e.dma_start(out=d, in_=s), qRS, 16)

            qkv_mm_done = [0]
            for hg in range(2):
                P.wait_t("gpsimd", state["res_tok"])
                if hg == 1:
                    P.wait("gpsimd", sT, qkv_mm_done[0])
                for mi, cbase in enumerate((0, D, 2 * D)):
                    src = wqkv_d[ell][:, cbase + hg * 512: cbase + hg * 512 + 512] \
                        .rearrange("(c p) m -> p c m", p=128)
                    dst = a16(R_W + mi * 4096, 4096).rearrange("p (c m) -> p c m", c=DC)
                    P.do("gpsimd", lambda e, d=dst, s=src: e.dma_start(out=d, in_=s),
                         qWQ, 16)
                wqkv_ld = qWQ.val
                P.wait_t("gpsimd", state["wo_free"])
                for kc in range(4):
                    src = wo_d[ell, hg * 512 + kc * 128: hg * 512 + (kc + 1) * 128, :]
                    P.do("gpsimd", lambda e, d=a16(R_WO + kc * 1024, 1024), s=src:
                         e.dma_start(out=d, in_=s), qWO, 16)
                wo_ld = qWO.val
                if hg == 1:
                    P.wait("scalar", sT, sT.val)

                fw = [None] * NW
                qh_chains = [(0, 0), (0, 1), (1, 0), (1, 1)]

                def wave(st, mi):
                    for tt in range(2):
                        pairs = [(a16(R_XS + dc * 256 + tt * 128, 128),
                                  a16(R_W + mi * 4096 + dc * 512, 512))
                                 for dc in range(DC)]
                        st["wv"][(tt, mi)] = mm_acc(4 + tt, 512, pairs)

                def sslot(st, ci):
                    side, tt_ = qh_chains[ci]
                    if side == 0:
                        return a32(sm["ssq"] + tt_ * 16 + (st["w"] % 2) * 8, 8)
                    return a32(sm["ssk"] + tt_ * 8, 8)

                def chain_head(st, ci):
                    side, tt_ = qh_chains[ci]
                    par = state["chain_seq"][0] % 3
                    state["chain_seq"][0] += 1
                    st["cpar"][ci] = par
                    P.wait("scalar", sT, st["wv"][(tt_, side)])
                    P.wait_t("scalar", state["sq_free"][par])
                    P.do("scalar", lambda e, o=a32(SQP[par], 512), i=bank(4 + tt_):
                         e.activation(o, i, AF.Square))
                    st["st1"][ci] = sS.val

                def chain_mid(st, ci):
                    ssl = sslot(st, ci)
                    P.wait("vector", sS, st["st1"][ci])
                    P.do("vector", lambda e, o=ssl, i=a32(SQP[st["cpar"][ci]], 512)
                         .rearrange("p (h d) -> p h d", h=8):
                         e.tensor_reduce(o, i, AX.X, ALU.add))
                    st["st2"][ci] = sV.val
                    P.wait("scalar", sV, st["st2"][ci])
                    P.do("scalar", lambda e, o=ssl, i=ssl:
                         e.activation(o, i, AF.Ln, bias=sb_eps24[:]))
                    P.do("scalar", lambda e, o=ssl, i=ssl:
                         e.activation(o, i, AF.Exp, scale=-0.5))
                    st["st3"][ci] = sS.val

                def chain_tail(st, ci):
                    side, tt_ = qh_chains[ci]
                    b = 4 + tt_
                    par = st["cpar"][ci]
                    tsl = st["w"] % 2
                    cosb = R_TAB + tsl * 4096 + (0 if side == 0 else 2) * 1024 + tt_ * 512
                    sinb = R_TAB + tsl * 4096 + (1 if side == 0 else 3) * 1024 + tt_ * 512
                    dstw = (XQW if side == 0 else XKW) + tt_ * 512
                    Bp = a32(BP[par], 512)
                    P.wait_t("vector", state["b_free"][par])
                    P.wait("vector", sT, st["wv"][(tt_, side)])
                    P.wait("vector", qSC, sc_ld)
                    if side == 0:
                        P.wait("vector", sS, st["st1"][ci])
                        P.do("vector", lambda e, o=Bp, i=bank(b), j=a32(QS8B, 512):
                             e.tensor_tensor(o, i, j, ALU.mult))
                    else:
                        P.wait("vector", sS, st["st3"][ci])
                        ssl = sslot(st, ci)
                        for h in range(8):
                            P.do("vector", lambda e, o=Bp[:, h * 64:h * 64 + 64],
                                 i=bank(b, 64, off=h * 64), r=ssl[:, h:h + 1],
                                 j=a32(KS8B + h * 64, 64):
                                 e.scalar_tensor_tensor(o, i, r, j, ALU.mult, ALU.mult))
                    bank_read_done(b, sV, sV.val)
                    P.wait("vector", qTB[tsl], st["tab_ld"])
                    hv = lambda base, lo: a32(base, 512).rearrange(
                        "p (h x) -> p h x", h=8)[:, :, lo:lo + 32]
                    hv16 = lambda base, lo: a16(base, 512).rearrange(
                        "p (h x) -> p h x", h=8)[:, :, lo:lo + 32]
                    P.do("vector", lambda e, o=hv(SQP[par], 0), i=hv(BP[par], 32),
                         j=hv16(sinb, 32): e.tensor_tensor(o, i, j, ALU.mult))
                    P.do("vector", lambda e, o=hv(SQP[par], 32), i=hv(BP[par], 0),
                         j=hv16(sinb, 0): e.tensor_tensor(o, i, j, ALU.mult))
                    P.do("vector", lambda e, o=Bp, i=Bp, j=a16(cosb, 512):
                         e.tensor_tensor(o, i, j, ALU.mult))
                    P.do("vector", lambda e, o=a16(dstw, 512), i=Bp,
                         j=a32(SQP[par], 512): e.tensor_tensor(o, i, j, ALU.add))
                    st["st7"][ci] = sV.val
                    state["sq_free"][par] = (sV, sV.val)
                    state["b_free"][par] = (sV, sV.val)

                def qk_transpose(st, side, bnk, ready_v):
                    w = st["w"]
                    P.wait("tensor", sV, ready_v)
                    for f0 in (0, 2):
                        P.wait_t("tensor", P.bank_free[bnk])
                        for k2, (fp, tt_) in enumerate(
                                [(f0, 0), (f0, 1), (f0 + 1, 0), (f0 + 1, 1)]):
                            srcb = (XQW if side == 0 else XKW) + tt_ * 512
                            P.do("tensor", lambda e, o=bank(bnk, 128, off=k2 * 128),
                                 l=a16(srcb + fp * 128, 128):
                                 e.matmul(o, l, sb_idB[:], start=True, stop=True),
                                 sT if k2 == 3 else None, 1)
                        mmv = sT.val
                        P.wait("scalar", sT, mmv)
                        if side == 0:
                            dsts = [a16(XQTR[w % 2] + f0 * 256, 512)]
                        else:
                            kt3 = A16[:, R_KT:R_KT + 4096].rearrange(
                                "p (f x) -> p f x", f=4)
                            dsts = [kt3[:, f0:f0 + 2,
                                        (w % 2) * 512 + 256:(w % 2) * 512 + 512],
                                    kt3[:, f0:f0 + 2,
                                        ((w + 1) % 2) * 512:((w + 1) % 2) * 512 + 256]]
                        ev = None
                        for dst in dsts:
                            ev = P.do("scalar", lambda e, d=dst, s=bank(bnk, 512):
                                      e.copy(d, s))
                        bank_read_done(bnk, sS, ev)

                def front_p1a(w):
                    st = {"w": w, "wv": {}, "cpar": {}, "st1": {}, "st2": {},
                          "st3": {}, "st7": {}}
                    fw[w] = st
                    tsl = w % 2
                    P.wait_t("sync", state["tab_free"][tsl])
                    for ti in range(4):
                        src = tbl_d[ti, w * 256:(w + 1) * 256, :] \
                            .rearrange("(tt p) c -> p tt c", p=128)
                        dst = a16(R_TAB + tsl * 4096 + ti * 1024, 1024) \
                            .rearrange("p (tt c) -> p tt c", tt=2)
                        P.do("sync", lambda e, d=dst, s=src: e.dma_start(out=d, in_=s),
                             qTB[tsl], 16)
                    st["tab_ld"] = qTB[tsl].val
                    P.wait("gpsimd", sT, qkv_mm_done[0])
                    P.wait("gpsimd", qRS, rs)
                    src0 = rstash_d.rearrange("p (c t) -> p c t", c=DC)[
                        :, 0:4, w * 256:(w + 1) * 256]
                    dst0 = a16(R_XS, 1024).rearrange("p (c t) -> p c t", c=4)
                    P.do("gpsimd", lambda e, d=dst0, s=src0: e.dma_start(out=d, in_=s),
                         qXS, 16)
                    if w > 0:
                        src1 = rstash_d.rearrange("p (c t) -> p c t", c=DC)[
                            :, 4:8, w * 256 - 1:(w + 1) * 256 - 1]
                        dst1 = a16(R_XS + 1024, 1024).rearrange("p (c t) -> p c t", c=4)
                        P.do("gpsimd", lambda e, d=dst1, s=src1:
                             e.dma_start(out=d, in_=s), qXS, 16)
                        st["xs_v"] = 0
                    else:
                        src1 = rstash_d.rearrange("p (c t) -> p c t", c=DC)[
                            :, 4:8, 0:255]
                        dst1 = a16(R_XS + 1024, 1024).rearrange(
                            "p (c t) -> p c t", c=4)[:, :, 1:256]
                        P.do("gpsimd", lambda e, d=dst1, s=src1:
                             e.dma_start(out=d, in_=s), qXS, 16)
                        P.wait("vector", sT, qkv_mm_done[0])
                        for c4 in range(4):
                            P.do("vector", lambda e, d=a16(R_XS + 1024 + c4 * 256, 1):
                                 e.memset(d, 0.0))
                        st["xs_v"] = sV.val
                    st["xs_ld"] = qXS.val

                def front_p1b(w):
                    st = fw[w]
                    P.wait("tensor", qWQ, wqkv_ld)
                    P.wait("tensor", qXS, st["xs_ld"])
                    P.wait("tensor", sV, st["xs_v"])
                    wave(st, 0)
                    for ci in (0, 1):
                        chain_head(st, ci)
                    for ci in (0, 1):
                        chain_mid(st, ci)

                def front_p2(w):
                    st = fw[w]
                    chain_tail(st, 0)
                    chain_tail(st, 1)
                    wave(st, 1)
                    for ci in (2, 3):
                        chain_head(st, ci)
                    for ci in (2, 3):
                        chain_mid(st, ci)

                def front_p3(w):
                    st = fw[w]
                    chain_tail(st, 2)
                    chain_tail(st, 3)
                    wave(st, 2)
                    qkv_mm_done[0] = sT.val
                    state["tab_free"][st["w"] % 2] = (sV, sV.val)

                def front_p3b(w):
                    st = fw[w]
                    qk_transpose(st, 0, 6, max(st["st7"][0], st["st7"][1]))
                    qk_transpose(st, 1, 7, max(st["st7"][2], st["st7"][3]))
                    st["tr_s"] = sS.val

                def front_p4(w, pv_tok):
                    st = fw[w]
                    for tt in range(2):
                        slot = (st["w"] * 2 + tt) % 4
                        P.wait("scalar", sT, st["wv"][(tt, 2)])
                        P.wait_t("scalar", pv_tok)
                        P.wait_t("scalar", state["v_free"][slot])
                        ev = P.do("scalar", lambda e, o=a16(R_V + slot * 512, 512),
                                  i=bank(4 + tt): e.copy(o, i))
                        bank_read_done(4 + tt, sS, ev)
                    st["prep_s"] = sS.val

                def sims_window(w):
                    st = fw[w]
                    wp = w % 2
                    nj0 = 128 if w == 0 else 384
                    nj1 = 256 if w == 0 else 384
                    m0 = MTRI if w == 0 else MG0
                    m1 = MW01 if w == 0 else MG0
                    half = 0 if w == 0 else wp
                    col0 = 256 if w == 0 else 0
                    col1 = 256 if w == 0 else 128
                    jtts0 = [0] if w == 0 else [2 * w - 2, 2 * w - 1, 2 * w]
                    jtts1 = [0, 1] if w == 0 else [2 * w - 1, 2 * w, 2 * w + 1]
                    nch0, nch1 = len(jtts0), len(jtts1)
                    P.wait("tensor", sS, st["tr_s"])

                    def ssq_col(tt_, h):
                        return a32(sm["ssq"] + tt_ * 16 + wp * 8 + h, 1)

                    def sim_S(h):
                        pt = h // 2
                        po = (h % 2) * 64
                        par = (h % 2) * 2
                        for qh, nj, msk in ((0, nj0, m0), (1, nj1, m1)):
                            bnk = par + qh
                            P.wait_t("tensor", P.bank_free[bnk])
                            P.do("tensor", lambda e, o=bank(bnk, nj),
                                 m=sb_mskb[:, msk:msk + nj]:
                                 e.matmul(o, sb_idB[:], m, start=True, stop=False))
                        for qh, nj, cc in ((0, nj0, col0), (1, nj1, col1)):
                            bnk = par + qh
                            lhs = a16(XQTR[wp] + pt * 256 + qh * 128, 128, pr=64, po=po)
                            rhs = a16(R_KT + pt * 1024 + half * 512 + cc, nj,
                                      pr=64, po=po)
                            P.do("tensor", lambda e, o=bank(bnk, nj), l=lhs, r=rhs:
                                 e.matmul(o, l, r, start=False, stop=True), sT, 1)
                            P.bank_ready[bnk] = (sT, sT.val)
                        return sT.val

                    def sim_max(h):
                        par = (h % 2) * 2
                        P.wait("vector", sT, s_mm_of[h])
                        for qh, nj in ((0, nj0), (1, nj1)):
                            P.do("vector", lambda e,
                                 o=a32(sm["negmax"] + (h % 3) * 2 + qh, 1),
                                 i=bank(par + qh, nj):
                                 e.tensor_reduce(o, i, AX.X, ALU.max, negate=True))
                        P.wait("vector", sS, max(st["st3"][0], st["st3"][1]))
                        for qh in range(2):
                            P.do("vector", lambda e,
                                 o=a32(sm["negmax"] + (h % 3) * 2 + qh, 1),
                                 i=a32(sm["negmax"] + (h % 3) * 2 + qh, 1),
                                 r=ssq_col(qh, h):
                                 e.tensor_scalar(o, i, r, None, ALU.mult))
                        msk_v_of[h] = sV.val

                    def sim_exp(h):
                        par = (h % 2) * 2
                        P.wait("scalar", sV, msk_v_of[h])
                        if h >= 2:
                            P.wait("scalar", sT, ptmm_of[h - 2])
                        for qh, nj in ((0, nj0), (1, nj1)):
                            bnk = par + qh
                            ex = P.do("scalar", lambda e,
                                      o=a16(EP_E[h % 2] + qh * 384, nj),
                                      i=bank(bnk, nj),
                                      b=a32(sm["negmax"] + (h % 3) * 2 + qh, 1),
                                      r=ssq_col(qh, h),
                                      sx=a32(sm["sumexp"] + (h % 3) * 2 + qh, 1):
                                      e.activation(o, i, AF.Exp, bias=b, scale=r,
                                                   accum_out=sx))
                            bank_read_done(bnk, sS, ex)
                        exp_s_of[h] = sS.val

                    def sim_norm(h):
                        P.wait("vector", sS, exp_s_of[h])
                        P.do("vector", lambda e,
                             o=a32(sm["recip"] + (h % 3) * 2, 2),
                             i=a32(sm["sumexp"] + (h % 3) * 2, 2):
                             e.reciprocal(o, i))
                        for qh, nj in ((0, nj0), (1, nj1)):
                            P.do("vector", lambda e, o=a16(EP_E[h % 2] + qh * 384, nj),
                                 i=a16(EP_E[h % 2] + qh * 384, nj),
                                 r=a32(sm["recip"] + (h % 3) * 2 + qh, 1):
                                 e.tensor_scalar(o, i, r, None, ALU.mult))

                    def sim_ptpv(h):
                        pt = h // 2
                        po = (h % 2) * 64
                        pn_v = sV.val
                        P.wait("tensor", sV, pn_v)
                        tiles = [(0, jc) for jc in range(nch0)] + \
                                [(1, jc) for jc in range(nch1)]
                        bA = 6 + (h % 2)
                        bB = 7 - (h % 2)
                        evs = []
                        for gi, gstart in enumerate(range(0, len(tiles), 4)):
                            grp = tiles[gstart:gstart + 4]
                            bnk = bA if gi == 0 else bB
                            P.wait_t("tensor", P.bank_free[bnk])
                            for k2, (qh, jc) in enumerate(grp):
                                P.do("tensor", lambda e, o=bank(bnk, 128, off=k2 * 128),
                                     l=a16(EP_E[h % 2] + qh * 384 + jc * 128, 128):
                                     e.matmul(o, l, sb_idB[:], start=True, stop=True),
                                     sT if k2 == len(grp) - 1 else None, 1)
                            evs.append((bnk, sT.val, gstart, len(grp)))
                        ptmm_of[h] = sT.val
                        for bnk, mmv, gstart, glen in evs:
                            P.wait("scalar", sT, mmv)
                            ev = P.do("scalar", lambda e,
                                      d=a16(EP_PT + gstart * 128, 128 * glen),
                                      s=bank(bnk, 128 * glen): e.copy(d, s))
                            bank_read_done(bnk, sS, ev)
                        pt_s = sS.val
                        P.wait("tensor", sS, pt_s)
                        P.wait("tensor", sS, st["prep_s"])
                        P.wait_t("tensor", P.bank_free[bA])
                        for qh, jtts in ((0, jtts0), (1, jtts1)):
                            nch = len(jtts)
                            for ji, jtt in enumerate(jtts):
                                idx = ji if qh == 0 else nch0 + ji
                                P.do("tensor", lambda e,
                                     o=bank(bA, 128, off=qh * 128, pr=64, po=po),
                                     l=a16(R_V + (jtt % 4) * 512 + h * 64, 64),
                                     r=a16(EP_PT + idx * 128, 128),
                                     st=(ji == 0), sp=(ji == nch - 1):
                                     e.matmul(o, l, r, start=st, stop=sp),
                                     sT if ji == nch - 1 else None, 1)
                        pv_mm = sT.val
                        P.wait("scalar", sT, pv_mm)
                        ev = P.do("scalar", lambda e,
                                  d=a16(R_AT + pt * 2048 + w * 256, 256, pr=64, po=po),
                                  s=bank(bA, 256, pr=64, po=po): e.copy(d, s))
                        bank_read_done(bA, sS, ev)

                    s_mm_of = {}
                    msk_v_of = {}
                    exp_s_of = {}
                    ptmm_of = {}
                    nxt = w + 1 if w + 1 < NW else None
                    for i in range(11):
                        if i < 8:
                            s_mm_of[i] = sim_S(i)
                        if 1 <= i <= 8:
                            sim_max(i - 1)
                        if i >= 3:
                            sim_ptpv(i - 3)
                        if 1 <= i <= 8:
                            sim_exp(i - 1)
                        if 2 <= i <= 9:
                            sim_norm(i - 2)
                        if nxt is not None:
                            if i == 0:
                                front_p1a(nxt)
                            elif i == 3:
                                front_p1b(nxt)
                            elif i == 5:
                                front_p2(nxt)
                            elif i == 7:
                                front_p3(nxt)
                    if nxt is not None:
                        front_p3b(nxt)
                        front_p4(nxt, (sT, sT.val))
                    if w > 0:
                        for tt in range(2):
                            state["v_free"][((w - 1) * 2 + tt) % 4] = (sT, sT.val)

                front_p1a(0)
                front_p1b(0)
                front_p2(0)
                front_p3(0)
                front_p3b(0)
                front_p4(0, None)
                for w in range(NW):
                    sims_window(w)
                P.wait("tensor", qWO, wo_ld)
                P.wait("tensor", sS, sS.val)
                P.wait("vector", qRS, rs)
                for tg in range(4):
                    for mc in range(DC):
                        b = 5 + ((tg * DC + mc) % 2)
                        pairs = [(a16(R_WO + kc * 1024 + mc * 128, 128),
                                  a16(R_AT + kc * 2048 + tg * 512, 512))
                                 for kc in range(4)]
                        mmv = mm_acc(b, 512, pairs)
                        P.wait("vector", sT, mmv)
                        dstr = a32(R_RES + mc * T + tg * 512, 512)
                        ev = P.do("vector", lambda e, d=dstr, p_=bank(b):
                                  e.tensor_tensor(d, d, p_, ALU.add))
                        bank_read_done(b, sV, ev)
                state["wo_free"] = (sT, sT.val)
                for sl4 in range(4):
                    state["v_free"][sl4] = (sT, sT.val)
            res_after_attn = sV.val
            if attn_only and ell == nlayers - 1:
                state["res_tok"] = (sV, sV.val)
                state["smalls_free"] = (sV, sV.val)
                return

            P.wait("scalar", sV, res_after_attn)
            for c4 in range(4):
                P.do("scalar", lambda e, d=a32(sm["bnd"] + c4, 1),
                     s=a32(R_RES + (4 + c4) * T + 1023, 1): e.copy(d, s))
            bnd_s = sS.val
            for half in range(2):
                hbase = half * 1024
                for tg2 in range(2):
                    tgb = hbase + tg2 * 512
                    P.wait("scalar", sV, res_after_attn if half == 0 else sV.val)
                    for dc in range(DC):
                        dst = a16(XSF + dc * 512, 512)
                        if dc < 4:
                            P.do("scalar", lambda e, d=dst,
                                 s=a32(R_RES + dc * T + tgb, 512): e.copy(d, s))
                        elif tgb == 0:
                            P.do("vector", lambda e, d=a16(XSF + dc * 512, 1):
                                 e.memset(d, 0.0))
                            P.do("scalar", lambda e, d=a16(XSF + dc * 512 + 1, 511),
                                 s=a32(R_RES + dc * T, 511): e.copy(d, s))
                        elif tgb == 1024:
                            P.wait("scalar", sS, bnd_s)
                            P.do("scalar", lambda e, d=a16(XSF + dc * 512, 1),
                                 s=a32(sm["bnd"] + dc - 4, 1): e.copy(d, s))
                            P.do("scalar", lambda e, d=a16(XSF + dc * 512 + 1, 511),
                                 s=a32(R_RES + dc * T + tgb, 511): e.copy(d, s))
                        else:
                            P.do("scalar", lambda e, d=dst,
                                 s=a32(R_RES + dc * T + tgb - 1, 512): e.copy(d, s))
                    xs_s = sS.val
                    xs_v = sV.val
                    P.wait("tensor", sS, xs_s)
                    P.wait("tensor", sV, xs_v)
                    pairs = [(sb_c1024[:], a16(XSF + dc * 512, 512)) for dc in range(DC)]
                    mu_mm = mm_acc(6, 512, pairs, opr=1)
                    P.wait("vector", sT, mu_mm)
                    mu_c = P.do("vector", lambda e, o=a32(sm["murow"], 512, pr=1),
                                i=bank(6, 512, pr=1): e.tensor_copy(o, i))
                    bank_read_done(6, sV, mu_c)
                    P.wait_t("tensor", P.bank_free[7])
                    P.wait("tensor", sV, mu_c)
                    P.do("tensor", lambda e, o=bank(7, 512),
                         l=sb_row1[:], r=a32(sm["murow"], 512, pr=1):
                         e.matmul(o, l, r, start=True, stop=True), sT, 1)
                    mub_mm = sT.val
                    P.wait("vector", sT, mub_mm)
                    for dc in range(DC):
                        P.do("vector", lambda e, o=a32(R_A + dc * 512, 512),
                             i=a16(XSF + dc * 512, 512), j=bank(7, 512):
                             e.tensor_tensor(o, i, j, ALU.subtract))
                    p1_v = sV.val
                    bank_read_done(7, sV, p1_v)
                    P.wait("scalar", sV, p1_v)
                    P.wait_t("tensor", P.bank_free[6])
                    P.wait("scalar", sT, sT.val)
                    for dc in range(DC):
                        P.do("scalar", lambda e, o=a16(R_W + dc * 512, 512),
                             i=a32(R_A + dc * 512, 512): e.activation(o, i, AF.Square))
                    sq_s = sS.val
                    P.wait("tensor", sS, sq_s)
                    for dc in range(DC):
                        P.do("tensor", lambda e, o=bank(6, 512, pr=1),
                             l=sb_c1024[:], r=a16(R_W + dc * 512, 512),
                             st=(dc == 0), sp=(dc == DC - 1):
                             e.matmul(o, l, r, start=st, stop=sp),
                             sT if dc == DC - 1 else None, 1)
                    var_mm = sT.val
                    P.wait("vector", sT, var_mm)
                    vr_c = P.do("vector", lambda e, o=a32(sm["rstdrow"], 512, pr=1),
                                i=bank(6, 512, pr=1): e.tensor_copy(o, i))
                    bank_read_done(6, sV, vr_c)
                    P.wait("scalar", sV, vr_c)
                    P.do("scalar", lambda e, o=a32(sm["rstdrow"], 512, pr=1),
                         i=a32(sm["rstdrow"], 512, pr=1):
                         e.activation(o, i, AF.Ln, bias=sb_eps[0:1, :]))
                    rst_s = P.do("scalar", lambda e, o=a32(sm["rstdrow"], 512, pr=1),
                                 i=a32(sm["rstdrow"], 512, pr=1):
                                 e.activation(o, i, AF.Exp, scale=-0.5))
                    P.wait_t("tensor", P.bank_free[7])
                    P.wait("tensor", sS, rst_s)
                    P.do("tensor", lambda e, o=bank(7, 512),
                         l=sb_row1[:], r=a32(sm["rstdrow"], 512, pr=1):
                         e.matmul(o, l, r, start=True, stop=True), sT, 1)
                    ab_mm = sT.val
                    P.wait("vector", sT, ab_mm)
                    P.wait("vector", sS, smalls_s)
                    for dc in range(DC):
                        P.do("vector", lambda e, o=a32(R_A + dc * 512, 512),
                             i=a32(R_A + dc * 512, 512), j=bank(7, 512):
                             e.tensor_tensor(o, i, j, ALU.mult))
                        P.do("vector", lambda e, o=a16(R_AT + dc * 1024 + tg2 * 512, 512),
                             i=a32(R_A + dc * 512, 512), g=a32(sm["gcol"] + dc, 1),
                             bb=a32(sm["bcol"] + dc, 1):
                             e.tensor_scalar(o, i, g, bb, ALU.mult, ALU.add))
                    xn_v = sV.val
                    bank_read_done(7, sV, xn_v)
                xn_done = sV.val

                P.wait("gpsimd", sV, xn_done)
                for ci in range(NKC):
                    wci = 128 if ci < NKC - 1 else DI - 128 * (NKC - 1)
                    slot = ci % 2
                    P.wait_t("gpsimd", state["w1_free"][slot])
                    for gside in range(2):
                        c0 = gside * DI + ci * 128
                        src = w1_d[ell][:, c0:c0 + wci] \
                            .rearrange("(c p) m -> p c m", p=128)
                        dst = a16(W1RING[slot] + gside * 1024, DC * wci) \
                            .rearrange("p (c m) -> p c m", c=DC)
                        P.do("gpsimd", lambda e, d=dst, s=src: e.dma_start(out=d, in_=s),
                             qW1[slot], 16)
                    w1_ld = qW1[slot].val
                    P.wait("tensor", qW1[slot], w1_ld)
                    P.wait("tensor", sV, xn_done)
                    for tg2 in range(2):
                        for gside in range(2):
                            pairs = [(a16(W1RING[slot] + gside * 1024 + dc * wci, wci),
                                      a16(R_AT + dc * 1024 + tg2 * 512, 512))
                                     for dc in range(DC)]
                            mm_acc((ci % 2) * 4 + tg2 * 2 + gside, 512, pairs, opr=wci)
                    w1_mm = sT.val
                    state["w1_free"][slot] = (sT, w1_mm)
                    for tg2 in range(2):
                        ba, bg = (ci % 2) * 4 + tg2 * 2, (ci % 2) * 4 + tg2 * 2 + 1
                        P.wait("scalar", sT, w1_mm)
                        P.wait("scalar", sV, sV.val)
                        gl = P.do("scalar", lambda e, o=a16(R_V + 512, 512, pr=wci),
                                  i=bank(bg, 512, pr=wci),
                                  bb=a32(sm["b1g"] + ci, 1, pr=wci):
                                  e.activation(o, i, AF.Gelu, bias=bb))
                        bank_read_done(bg, sS, gl)
                        P.wait("vector", sS, gl)
                        mv = P.do("vector", lambda e,
                                  o=a16(MID[ci] + tg2 * 512, 512, pr=wci),
                                  i=bank(ba, 512, pr=wci),
                                  bb=a32(sm["b1a"] + ci, 1, pr=wci),
                                  g=a16(R_V + 512, 512, pr=wci):
                                  e.scalar_tensor_tensor(o, i, bb, g, ALU.add, ALU.mult))
                        bank_read_done(ba, sV, mv)
                mid_v = sV.val
                mid_s = sS.val

                for tg2 in range(2):
                    tgb = hbase + tg2 * 512
                    w2_done = [None] * DC
                    for kc in range(NKC):
                        wkc = 128 if kc < NKC - 1 else DI - 128 * (NKC - 1)
                        slot = kc % 4
                        P.wait_t("gpsimd", state["w2_free"][slot])
                        src = w2_d[ell, kc * 128:kc * 128 + wkc, :]
                        wd = P.do("gpsimd", lambda e, d=a16(W2RING[slot], 1024, pr=wkc),
                                  s=src: e.dma_start(out=d, in_=s), qW2[slot], 16)
                        P.wait("tensor", qW2[slot], wd)
                        if kc == 0:
                            P.wait("tensor", sV, mid_v)
                            P.wait("tensor", sS, mid_s)
                            for b2_ in range(8):
                                P.wait_t("tensor", P.bank_free[b2_])
                        for mc in range(DC):
                            tok = P.do("tensor", lambda e, o=bank(mc, 512),
                                       l=a16(W2RING[slot] + mc * 128, 128, pr=wkc),
                                       r=a16(MID[kc] + tg2 * 512, 512, pr=wkc),
                                       st=(kc == 0), sp=(kc == NKC - 1):
                                       e.matmul(o, l, r, start=st, stop=sp),
                                       sT if (mc == DC - 1 or kc == NKC - 1) else None, 1)
                            if kc == NKC - 1:
                                w2_done[mc] = tok
                        state["w2_free"][slot] = (sT, sT.val)
                    for mc in range(DC):
                        P.wait("vector", sT, w2_done[mc])
                        dstr = a32(R_RES + mc * T + tgb, 512)
                        ev = P.do("vector", lambda e, d=dstr, p_=bank(mc),
                                  bb=a32(sm["b2col"] + mc, 1):
                                  e.scalar_tensor_tensor(d, p_, bb, d, ALU.add, ALU.add))
                        bank_read_done(mc, sV, ev)
            state["res_tok"] = (sV, sV.val)
            state["smalls_free"] = (sV, sV.val)
            for sl4 in range(4):
                state["v_free"][sl4] = (sT, sT.val)

        for ell in range(nlayers):
            emit_inner(ell)

        if dump:
            P.wait("sync", sV, sV.val)
            P.wait("sync", sS, sS.val)
            P.do("sync", lambda e: e.dma_start(out=ydbg_d, in_=a32(R_RES, DC * T)),
                 sO, 16)
            P.do("sync", lambda e: e.wait_ge(sO.h, sO.val))
        elif mode == "full":
            P.wait("sync", sV, sV.val)
            P.wait("sync", sS, sS.val)
            P.wait("tensor", sV, sV.val)
            P.wait("tensor", sS, sS.val)
            P.wait("scalar", sV, sV.val)
            P.wait("vector", sS, sS.val)
            P.wait("sync", qST, qST.val)
            for qt in range(NT):
                xd = P.do("sync", lambda e, d=a32(O_X0R[0], 1024),
                          s=x1s_d[qt * 128:(qt + 1) * 128, :]:
                          e.dma_start(out=d, in_=s), qX0[0], 16)
                for dcg in range(2):
                    b = dcg
                    P.wait_t("tensor", P.bank_free[b])
                    for k in range(4):
                        dc = dcg * 4 + k
                        src = a32(R_RES + dc * T + qt * 128, 128)
                        P.do("tensor", lambda e, o=bank(b, 128, off=k * 128), s=src:
                             e.matmul(o, s, sb_idF[:], start=True, stop=True),
                             sT if k == 3 else None, 1)
                    mmv = sT.val
                    P.wait("vector", sT, mmv)
                    P.wait("vector", qX0[0], xd)
                    ev = P.do("vector", lambda e, o=a32(O_ST + dcg * 512, 512),
                              p_=bank(b), xs=a32(O_X0R[0] + dcg * 512, 512):
                              e.tensor_tensor(o, p_, xs, ALU.add), sV, 1)
                    bank_read_done(b, sV, ev)
                P.wait("vector", sV, sV.val)
                stile = a32(O_ST, 1024)
                negmu = a32(sm["t1"], 1)
                P.do("vector", lambda e, o=negmu, i=stile: e.tensor_reduce(
                    o, i, AX.X, ALU.add, negate=True), sV, 1)
                P.wait("vector", sV, sV.val)
                P.do("vector", lambda e, o=negmu, i=negmu: e.tensor_scalar(
                    o, i, 1.0 / 1024.0, None, ALU.mult), sV, 1)
                mu_v = sV.val
                P.wait("scalar", sV, mu_v)
                ssq = a32(sm["t2"], 1)
                P.do("scalar", lambda e, o=a32(O_X1T, 1024), i=stile, b=negmu, sx=ssq:
                     e.activation(o, i, AF.Square, bias=b, accum_out=sx), sS, 1)
                P.do("scalar", lambda e, o=ssq, i=ssq: e.activation(
                    o, i, AF.Ln, scale=1.0 / 1024.0, bias=sb_eps[:]), sS, 1)
                P.do("scalar", lambda e, o=ssq, i=ssq: e.activation(
                    o, i, AF.Exp, scale=-0.5), sS, 1)
                rstd_s = sS.val
                P.wait("vector", sS, rstd_s)
                P.do("vector", lambda e, o=a32(sm["t0"], 1), i=negmu: e.tensor_tensor(
                    o, i, ssq, ALU.mult), sV, 1)
                m2_v = sV.val
                P.wait("vector", sV, m2_v)
                yv = P.do("vector", lambda e, o=a32(O_X1T, 1024), i=stile:
                          e.tensor_scalar(o, i, ssq, a32(sm["t0"], 1), ALU.mult, ALU.add),
                          sV, 1)
                P.wait("sync", sV, yv)
                P.do("sync", lambda e, s=a32(O_X1T, 1024),
                     d=y_d[qt * 128:(qt + 1) * 128, :]: e.dma_start(out=d, in_=s),
                     sO, 16)
                P.wait("tensor", sV, yv)
                P.wait("scalar", sO, sO.val)
                P.wait("sync", sV, sV.val)
            P.do("sync", lambda e: e.wait_ge(sO.h, sO.val))
        else:
            P.do("sync", lambda e: e.wait_ge(sO.h, sO.val))

        P.flush(block)
    return nc



def _in_maps(inputs, consts):
    x = np.ascontiguousarray(np.asarray(inputs["L0_attn_embeddings"], dtype=np.float32))
    base = {
        "wq": np.ascontiguousarray(np.asarray(inputs["L0_attn_W_Q"], np.float32)),
        "wk": np.ascontiguousarray(np.asarray(inputs["L0_attn_W_K"], np.float32)),
        "wv": np.ascontiguousarray(np.asarray(inputs["L0_attn_W_V"], np.float32)),
        "wqkv": np.ascontiguousarray(np.asarray(inputs["L0_ffn_Wqkv"], np.float32)),
        "qsc": np.ascontiguousarray(np.asarray(inputs["L0_ffn_q_scale"], np.float32)),
        "ksc": np.ascontiguousarray(np.asarray(inputs["L0_ffn_k_scale"], np.float32)),
        "wo": np.ascontiguousarray(np.asarray(inputs["L0_ffn_Wo"], np.float32)),
        "lng": np.ascontiguousarray(np.asarray(inputs["L0_ffn_ln_g"], np.float32)),
        "lnb": np.ascontiguousarray(np.asarray(inputs["L0_ffn_ln_b"], np.float32)),
        "w1": np.ascontiguousarray(np.asarray(inputs["L0_ffn_W1"], np.float32)),
        "b1": np.ascontiguousarray(np.asarray(inputs["L0_ffn_b1"], np.float32)),
        "w2": np.ascontiguousarray(np.asarray(inputs["L0_ffn_W2"], np.float32)),
        "b2": np.ascontiguousarray(np.asarray(inputs["L0_ffn_b2"], np.float32)),
    }
    base.update(consts)
    maps = []
    for c in range(NB):
        m = dict(base)
        m["x0"] = np.ascontiguousarray(x[c])
        maps.append(m)
    return maps


def kernel(**inputs):
    consts = _host_consts()
    nc = build_program(depth=DEPTH, mode="full")
    maps = _in_maps(inputs, consts)
    res = run_bass_kernel_spmd(nc, maps, core_ids=list(range(NB)))
    out = np.stack([np.asarray(r["y"], dtype=np.float32) for r in res.results])
    return out


# version 12
# speedup vs baseline: 1.0055x; 1.0055x over previous
import numpy as np
import ml_dtypes
from contextlib import ExitStack

import concourse.bass as bass
import concourse.mybir as mybir
from concourse.bass_utils import run_bass_kernel_spmd

F32 = mybir.dt.float32
BF16 = mybir.dt.bfloat16
AF = mybir.ActivationFunctionType
ALU = mybir.AluOpType
AX = mybir.AxisListType

NB = 8
T = 2048
D = 1024
DEPTH = 6
NHEAD = 16
DH = 64
WIN = 256
DI = 2730
NW = T // WIN
NT = T // 128
DC = D // 128
HG = 2
HGH = NHEAD // HG
NEG = -3.0e37
NKC = (DI + 127) // 128

MG0, MG1, MW00, MW01, MTRI = 0, 384, 896, 1152, 1408


def _host_consts():
    half = np.arange(0, DH, 2, dtype=np.float64)
    inv_freq = 1.0 / (10000.0 ** (half / DH))
    t = np.arange(T, dtype=np.float64)
    freqs = np.concatenate([t[:, None] * inv_freq[None, :]] * 2, -1)
    base_scale = (half + 0.4 * DH) / (1.4 * DH)
    power = (t - T // 2) / (WIN // 2)
    scale = np.concatenate([base_scale[None, :] ** power[:, None]] * 2, -1)
    cos = np.cos(freqs); sin = np.sin(freqs)
    sign = np.where(np.arange(DH) < DH // 2, -1.0, 1.0)[None, :]
    rot = lambda a: np.roll(a, -32, axis=1)
    rep = lambda a: np.tile(a, (1, HGH)).astype(ml_dtypes.bfloat16)
    tables = np.stack([rep(cos * scale), rep(rot(sign * sin * scale)),
                       rep(cos / scale), rep(rot(sign * sin / scale))])
    p = np.arange(128)[:, None]

    def band(ncols, lo, hi):
        j = np.arange(ncols)[None, :]
        return np.where((j >= lo) & (j <= hi), 0.0, NEG).astype(np.float32)
    masks = np.concatenate([
        band(384, p, p + WIN),
        band(512, p + 128, p + 384),
        band(256, 0 * p, p),
        band(256, 0 * p, p + 128),
        band(128, 0 * p, p),
    ], axis=1)
    return dict(
        tables=np.ascontiguousarray(tables),
        masks=masks,
        masksb=masks.astype(ml_dtypes.bfloat16),
        identF=np.eye(128, dtype=np.float32),
        identB=np.eye(128, dtype=np.float32).astype(ml_dtypes.bfloat16),
        c1024=np.full((128, 1), 1.0 / 1024.0, dtype=ml_dtypes.bfloat16),
        row8=np.full((1, 128), 8.0, dtype=np.float32),
        row1=np.full((1, 128), 1.0, dtype=np.float32),
        epscol=np.full((128, 1), 1.0e-5, dtype=np.float32),
        eps24col=np.full((128, 1), 1.0e-24, dtype=np.float32),
    )


class SemC:
    __slots__ = ("h", "val")
    def __init__(self, h):
        self.h = h
        self.val = 0


ENGS = ("sync", "gpsimd", "scalar", "vector", "tensor")


class Prog:
    def __init__(self):
        self.ops = {e: [] for e in ENGS}
        self.waited = {}
        self.esem = {}
        self.bank_free = [(None, 0)] * 8
        self.bank_ready = [(None, 0)] * 8

    def do(self, eng, fn, inc=None, n=1):
        if eng in ("vector", "scalar") or (eng == "gpsimd" and n != 16):
            es = self.esem[eng]
            assert inc is None or inc is es
            self.wait(eng, es, es.val)
            inc, n = es, 1
        if inc is None:
            self.ops[eng].append(fn)
            return None
        inc.val += n
        h, v = inc.h, inc.val
        self.ops[eng].append(lambda e, fn=fn, h=h, n=n: fn(e).then_inc(h, n))
        return v

    def wait(self, eng, sem, val):
        if sem is None or val is None or val <= 0:
            return
        key = (eng, id(sem))
        if self.waited.get(key, 0) >= val:
            return
        self.waited[key] = val
        h = sem.h
        self.ops[eng].append(lambda e, h=h, val=val: e.wait_ge(h, val))

    def wait_t(self, eng, tok):
        if tok is not None:
            self.wait(eng, tok[0], tok[1])

    def flush(self, block):
        for eng in ENGS:
            ops = self.ops[eng]
            if not ops:
                continue
            def body(e, ops=ops):
                for fn in ops:
                    fn(e)
            getattr(block, eng)(body)


def build_program(depth=DEPTH, mode="full"):
    nc = bass.Bass("TRN2", target_bir_lowering=False, debug=False)

    def din(name, shape, dt=F32):
        return nc.dram_tensor(name, list(shape), dt, kind="ExternalInput").ap()

    x0 = din("x0", [T, D])
    wq_d = din("wq", [D, D]); wk_d = din("wk", [D, D]); wv_d = din("wv", [D, D])
    wqkv_d = din("wqkv", [DEPTH, D, 3 * D])
    qsc_d = din("qsc", [DEPTH, DH]); ksc_d = din("ksc", [DEPTH, DH])
    wo_d = din("wo", [DEPTH, D, D])
    lng_d = din("lng", [DEPTH, D]); lnb_d = din("lnb", [DEPTH, D])
    w1_d = din("w1", [DEPTH, D, 2 * DI]); b1_d = din("b1", [DEPTH, 2 * DI])
    w2_d = din("w2", [DEPTH, DI, D]); b2_d = din("b2", [DEPTH, D])
    tbl_d = din("tables", [4, T, HGH * DH], BF16)
    masks_d = din("masks", [128, 1536])
    masksb_d = din("masksb", [128, 1536], BF16)
    identF_d = din("identF", [128, 128])
    identB_d = din("identB", [128, 128], BF16)
    c1024_d = din("c1024", [128, 1], BF16)
    row8_d = din("row8", [1, 128]); row1_d = din("row1", [1, 128])
    eps_d = din("epscol", [128, 1])
    eps24_d = din("eps24col", [128, 1])

    y_d = nc.dram_tensor("y", [T, D], F32, kind="ExternalOutput").ap()
    x1s_d = nc.dram_tensor("x1s", [T, D], F32, kind="Internal").ap()
    rstash_d = nc.dram_tensor("rstash", [128, DC * T], F32, kind="Internal").ap()
    dump = mode.startswith("L") or mode.startswith("A")
    attn_only = mode.startswith("A")
    nlayers = depth if mode in ("full",) else (0 if mode == "outer" else int(mode[1:]) + 1)
    if dump:
        ydbg_d = nc.dram_tensor("ydbg", [128, DC * T], F32, kind="ExternalOutput").ap()

    with ExitStack() as ctx:
        A32 = ctx.enter_context(nc.sbuf_tensor("A32", [128, 24064], F32))
        A16 = ctx.enter_context(nc.sbuf_tensor("A16", [128, 50816], BF16))
        sb_idF = ctx.enter_context(nc.sbuf_tensor("sb_idF", [128, 128], F32))
        sb_idB = ctx.enter_context(nc.sbuf_tensor("sb_idB", [128, 128], BF16))
        sb_c1024 = ctx.enter_context(nc.sbuf_tensor("sb_c1024", [128, 1], BF16))
        sb_row8 = ctx.enter_context(nc.sbuf_tensor("sb_row8", [1, 128], F32))
        sb_row1 = ctx.enter_context(nc.sbuf_tensor("sb_row1", [1, 128], F32))
        sb_eps = ctx.enter_context(nc.sbuf_tensor("sb_eps", [128, 1], F32))
        sb_eps24 = ctx.enter_context(nc.sbuf_tensor("sb_eps24", [128, 1], F32))
        sb_mskb = ctx.enter_context(nc.sbuf_tensor("sb_mskb", [128, 1536], BF16))
        PS = ctx.enter_context(nc.psum_tensor("PS", [128, 4096], F32))

        R_RES = 0
        R_MASK = 16384
        R_A = 17920
        R_SM = 22016
        R_W = 0
        R_KT = 12288
        R_AT = 16384
        R_TAB = 24576
        R_V = 32768
        R_XS = 34816
        R_QW = 36864
        R_QT2 = 38912
        R_EP = 40960
        R_WO = 43264
        R_SP = 47360
        sm = {}
        _p = [R_SM]
        for nm, c in [("ssq", 32), ("ssk", 16), ("negmax", 8), ("sumexp", 8),
                      ("recip", 8), ("murow", 512), ("rstdrow", 512),
                      ("qsrow", 512), ("gcol", 8), ("bcol", 8), ("b2col", 8),
                      ("b1a", 24), ("b1g", 24), ("t0", 8), ("t1", 8), ("t2", 8),
                      ("bnd", 8)]:
            sm[nm] = _p[0]; _p[0] += c
        assert _p[0] <= R_SM + 2048

        def a32(base, cols, pr=128, po=0):
            return A32[po:po + pr, base:base + cols]

        def a16(base, cols, pr=128, po=0):
            return A16[po:po + pr, base:base + cols]

        def bank(b, cols=512, off=0, pr=128, po=0):
            return PS[po:po + pr, b * 512 + off:b * 512 + off + cols]

        sQ = SemC(ctx.enter_context(nc.semaphore("sQ")))
        qC = SemC(ctx.enter_context(nc.semaphore("qC")))
        qX0 = [SemC(ctx.enter_context(nc.semaphore("qX0a"))),
               SemC(ctx.enter_context(nc.semaphore("qX0b")))]
        qST = SemC(ctx.enter_context(nc.semaphore("qST")))
        sG = SemC(ctx.enter_context(nc.semaphore("sG")))
        sT = SemC(ctx.enter_context(nc.semaphore("sT")))
        sV = SemC(ctx.enter_context(nc.semaphore("sV")))
        sS = SemC(ctx.enter_context(nc.semaphore("sS")))
        sO = SemC(ctx.enter_context(nc.semaphore("sO")))
        sGC = SemC(ctx.enter_context(nc.semaphore("sGC")))

        block = ctx.enter_context(nc.Block())
        P = Prog()
        P.esem = {"vector": sV, "scalar": sS, "gpsimd": sGC}

        def mm_acc(b, cols, pairs, off=0, opr=128, opo=0):
            P.wait_t("tensor", P.bank_free[b])
            n = len(pairs)
            for i, (l, r) in enumerate(pairs):
                P.do("tensor",
                     lambda e, o=bank(b, cols, off, opr, opo), l=l, r=r,
                     st=(i == 0), sp=(i == n - 1):
                     e.matmul(o, l, r, start=st, stop=sp),
                     sT if i == n - 1 else None, 1)
            P.bank_ready[b] = (sT, sT.val)
            return sT.val

        def bank_read_done(b, sem, val):
            P.bank_free[b] = (sem, val)

        for dst, src in [(sb_idF[:], identF_d), (sb_idB[:], identB_d),
                         (sb_c1024[:], c1024_d), (sb_row8[:], row8_d),
                         (sb_row1[:], row1_d), (sb_eps[:], eps_d),
                         (sb_eps24[:], eps24_d),
                         (sb_mskb[:], masksb_d), (a32(R_MASK, 1536), masks_d)]:
            P.do("sync", lambda e, d=dst, s=src: e.dma_start(out=d, in_=s), qC, 16)
        consts_q = qC.val

        OX0T = 0
        OWB = 16384
        OKT = 24576
        OSTG = 40960
        OQR = 16384
        OVR = 0
        OE = 18432
        OPT = 20480
        O_X0R = [R_A, R_A + 1024]
        O_ST = R_A + 2048
        O_X1T = R_A + 3072

        qts_d = nc.dram_tensor("qts", [NT, 128, D], BF16, kind="Internal").ap()
        vs_d = nc.dram_tensor("vs", [NT, 128, D], BF16, kind="Internal").ap()
        qQS = [SemC(ctx.enter_context(nc.semaphore("qQSa"))),
               SemC(ctx.enter_context(nc.semaphore("qQSb")))]
        qQR = [SemC(ctx.enter_context(nc.semaphore("qQRa"))),
               SemC(ctx.enter_context(nc.semaphore("qQRb")))]
        qVR = [SemC(ctx.enter_context(nc.semaphore(f"qVR{i}"))) for i in range(4)]

        x0_ld = []
        P.wait("tensor", qC, consts_q)
        ring_free = [None, None]
        for tt in range(NT):
            P.wait_t("sync", ring_free[tt % 2])
            dst = a32(O_X0R[tt % 2], 1024)
            src = x0[tt * 128:(tt + 1) * 128, :]
            x0_ld.append(P.do("sync", lambda e, d=dst, s=src: e.dma_start(out=d, in_=s),
                              qX0[tt % 2], 16))
            P.wait("tensor", qX0[tt % 2], x0_ld[tt])
            P.wait_t("tensor", ring_free[tt % 2])
            for dcg in range(2):
                b = dcg
                P.wait_t("tensor", P.bank_free[b])
                for k in range(4):
                    dc = dcg * 4 + k
                    src2 = a32(O_X0R[tt % 2], 1024)[:, dc * 128:dc * 128 + 128]
                    P.do("tensor",
                         lambda e, o=bank(b, 128, off=k * 128), s=src2:
                         e.matmul(o, s, sb_idF[:], start=True, stop=True),
                         sT if k == 3 else None, 1)
                mmv = sT.val
                P.wait("scalar", sT, mmv)
                last = None
                for k in range(4):
                    dc = dcg * 4 + k
                    dst2 = a16(OX0T + dc * T + tt * 128, 128)
                    last = P.do("scalar",
                                lambda e, d=dst2, s=bank(b, 128, off=k * 128):
                                e.copy(d, s), sS, 1)
                bank_read_done(b, sS, last)
            ring_free[tt % 2] = (sS, sS.val)
        x0T_done = (sS, sS.val)

        def wload(w_ap, prev_consumed=None):
            if prev_consumed is not None:
                P.wait_t("gpsimd", prev_consumed)
            for dc in range(DC):
                src = w_ap[dc * 128:(dc + 1) * 128, :]
                dst = a16(OWB + dc * 1024, 1024)
                P.do("gpsimd", lambda e, d=dst, s=src: e.dma_start(out=d, in_=s), sG, 16)
            return sG.val

        wld = wload(wk_d)
        P.wait("tensor", sG, wld)
        P.wait_t("tensor", x0T_done)
        for mc in range(DC):
            for tg in range(4):
                b = 2 + ((mc * 4 + tg) % 2)
                pairs = [(a16(OWB + dc * 1024 + mc * 128, 128),
                          a16(OX0T + dc * T + tg * 512, 512)) for dc in range(DC)]
                mmv = mm_acc(b, 512, pairs)
                P.wait("scalar", sT, mmv)
                ev = P.do("scalar", lambda e, d=a16(OKT + mc * T + tg * 512, 512),
                          s=bank(b): e.copy(d, s), sS, 1)
                bank_read_done(b, sS, ev)
        kt_done = sS.val

        stg_free = [None, None]

        def proj_to_hbm(which, wld_mark):
            P.wait("tensor", sG, wld_mark)
            n_out = NT * 2 if which == "v" else DC * 4
            for i in range(n_out):
                b = 2 + (i % 2)
                sl = i % 2
                if which == "v":
                    tt, vg = i // 2, i % 2
                    tt, vg = divmod(i, 2)
                    pairs = [(a16(OX0T + dc * T + tt * 128, 128),
                              a16(OWB + dc * 1024 + vg * 512, 512)) for dc in range(DC)]
                else:
                    mc, tg = divmod(i, 4)
                    pairs = [(a16(OWB + dc * 1024 + mc * 128, 128),
                              a16(OX0T + dc * T + tg * 512, 512)) for dc in range(DC)]
                mmv = mm_acc(b, 512, pairs)
                P.wait("scalar", sT, mmv)
                P.wait_t("scalar", stg_free[sl])
                ev = P.do("scalar", lambda e, d=a16(OSTG + sl * 512, 512), s=bank(b):
                          e.copy(d, s), sS, 1)
                bank_read_done(b, sS, ev)
                P.wait("sync", sS, ev)
                if which == "v":
                    dd = vs_d[tt, :, vg * 512:(vg + 1) * 512]
                else:
                    mc, tg = divmod(i, 4)
                    dd = qts_d[tg * 4:(tg + 1) * 4, :, mc * 128:(mc + 1) * 128]                         .rearrange("q p c -> p q c")
                dmam = P.do("sync", lambda e, d=dd, s=a16(OSTG + sl * 512, 512)
                            .rearrange("p (q c) -> p q c", q=4 if which == "q" else 1):
                            e.dma_start(out=d, in_=s), qQS[sl], 16)
                stg_free[sl] = (qQS[sl], dmam)
            return (qQS[0].val, qQS[1].val)

        wld = wload(wv_d, prev_consumed=(sS, kt_done))
        v_done = proj_to_hbm("v", wld)
        wld = wload(wq_d, prev_consumed=(sS, sS.val))
        q_done = proj_to_hbm("q", wld)

        x1_qdma = []
        qr_free = [None, None]
        vloads = {}
        P.wait("sync", qQS[0], q_done[0])
        P.wait("sync", qQS[1], q_done[1])
        for qt in range(NT):
            nk = (qt + 1) * 128
            njg = (nk + 511) // 512
            xd = P.do("sync", lambda e, d=a32(O_X0R[0], 1024),
                      s=x0[qt * 128:(qt + 1) * 128, :]: e.dma_start(out=d, in_=s),
                      qX0[0], 16)
            qsl = qt % 2
            P.wait_t("sync", qr_free[qsl])
            qd = P.do("sync", lambda e, d=a16(OQR + qsl * 1024, 1024),
                      s=qts_d[qt]: e.dma_start(out=d, in_=s), qQR[qsl], 16)
            qs4 = qVR[qt % 4]
            P.wait("sync", qs4, qs4.val)
            vm = P.do("sync", lambda e, d=a16(OVR + qt * 1024, 1024),
                      s=vs_d[qt]: e.dma_start(out=d, in_=s), qs4, 16)
            vloads[qt] = (qs4, vm)
            P.wait("tensor", qQR[qsl], qd)
            for jg in range(njg):
                w_ = min(512, nk - jg * 512)
                pairs = [(a16(OQR + qsl * 1024 + mc * 128, 128),
                          a16(OKT + mc * T + jg * 512, w_)) for mc in range(DC)]
                mm_acc(jg, w_, pairs)
            s_mm = sT.val
            P.wait("vector", sT, s_mm)
            dsl = PS[:, qt * 128:qt * 128 + 128]
            mski = P.do("vector", lambda e, d=dsl: e.tensor_tensor(
                d, d, a32(R_MASK + MTRI, 128), ALU.add), sV, 1)
            P.wait("vector", sV, mski)
            nm = a32(sm["negmax"], 1)
            P.do("vector", lambda e, o=nm, n=nk: e.tensor_reduce(
                o, PS[:, 0:n], AX.X, ALU.max, negate=True), sV, 1)
            red_v = sV.val
            P.wait("scalar", sV, red_v)
            nm32 = a32(sm["t0"], 1)
            nm32_s = P.do("scalar", lambda e, o=nm32, i=nm: e.activation(
                o, i, AF.Copy, scale=1.0 / 32.0), sS, 1)
            P.wait("scalar", sS, nm32_s)
            sxp = a32(sm["sumexp"], 1)
            exp_s = P.do("scalar", lambda e, o=a16(OE, nk), n=nk, b=nm32, sx=sxp:
                         e.activation(o, PS[:, 0:n], AF.Exp, bias=b, scale=1.0 / 32.0,
                                      accum_out=sx), sS, 1)
            for b_ in range(njg):
                bank_read_done(b_, sS, exp_s)
            P.wait("vector", sS, exp_s)
            rc = a32(sm["recip"], 1)
            P.do("vector", lambda e, o=rc, i=sxp: e.reciprocal(o, i), sV, 1)
            rcv = sV.val
            P.wait("vector", sV, rcv)
            pn = P.do("vector", lambda e, o=a16(OE, nk), i=a16(OE, nk), r=rc:
                      e.tensor_scalar(o, i, r, None, ALU.mult), sV, 1)
            P.wait("tensor", sV, pn)
            for jc in range(qt + 1):
                b = 4 + ((jc // 2) % 2)
                off = (jc % 2) * 128
                if jc % 2 == 0:
                    P.wait_t("tensor", P.bank_free[b])
                mm = P.do("tensor", lambda e, o=bank(b, 128, off=off),
                          l=a16(OE + jc * 128, 128):
                          e.matmul(o, l, sb_idB[:], start=True, stop=True), sT, 1)
                if jc % 2 == 1 or jc == qt:
                    P.wait("scalar", sT, mm)
                    wdt = 128 * (2 if jc % 2 == 1 else 1)
                    o0 = (jc // 2) * 256
                    ev = P.do("scalar", lambda e, d=a16(OPT + o0, wdt),
                              s=bank(b, wdt): e.copy(d, s), sS, 1)
                    bank_read_done(b, sS, ev)
            pt_ev = sS.val
            P.wait("tensor", sS, pt_ev)
            for jc in range(qt + 1):
                P.wait_t("tensor", vloads[jc])
            for dg in range(2):
                pairs = [(a16(OPT + jc * 128, 128),
                          a16(OVR + jc * 1024 + dg * 512, 512))
                         for jc in range(qt + 1)]
                mm_acc(6 + dg, 512, pairs)
            pv_mm = sT.val
            qr_free[qsl] = (sT, s_mm)
            P.wait("vector", sT, pv_mm)
            P.wait("vector", qX0[0], xd)
            for dg in range(2):
                ev = P.do("vector", lambda e, o=a32(O_ST + dg * 512, 512),
                          p_=bank(6 + dg), xs=a32(O_X0R[0] + dg * 512, 512):
                          e.tensor_tensor(o, p_, xs, ALU.add), sV, 1)
                bank_read_done(6 + dg, sV, ev)
            st_v = sV.val
            P.wait("vector", sV, st_v)
            stile = a32(O_ST, 1024)
            negmu = a32(sm["t1"], 1)
            P.do("vector", lambda e, o=negmu, i=stile: e.tensor_reduce(
                o, i, AX.X, ALU.add, negate=True), sV, 1)
            P.wait("vector", sV, sV.val)
            P.do("vector", lambda e, o=negmu, i=negmu: e.tensor_scalar(
                o, i, 1.0 / 1024.0, None, ALU.mult), sV, 1)
            mu_v = sV.val
            P.wait("scalar", sV, mu_v)
            ssq = a32(sm["t2"], 1)
            P.do("scalar", lambda e, o=a32(O_X1T, 1024), i=stile, b=negmu, sx=ssq:
                 e.activation(o, i, AF.Square, bias=b, accum_out=sx), sS, 1)
            P.do("scalar", lambda e, o=ssq, i=ssq: e.activation(
                o, i, AF.Ln, scale=1.0 / 1024.0, bias=sb_eps[:]), sS, 1)
            P.do("scalar", lambda e, o=ssq, i=ssq: e.activation(
                o, i, AF.Exp, scale=-0.5), sS, 1)
            rstd_s = sS.val
            P.wait("vector", sS, rstd_s)
            P.do("vector", lambda e, o=a32(sm["t0"], 1), i=negmu, j=ssq:
                 e.tensor_tensor(o, i, j, ALU.mult), sV, 1)
            m2_v = sV.val
            P.wait("vector", sV, m2_v)
            x1v = P.do("vector", lambda e, o=a32(O_X1T, 1024), i=stile, r=ssq,
                       m2=a32(sm["t0"], 1):
                       e.tensor_scalar(o, i, r, m2, ALU.mult, ALU.add), sV, 1)
            P.wait("sync", sV, x1v)
            x1_qdma.append(P.do("sync", lambda e, s=a32(O_X1T, 1024),
                                d=x1s_d[qt * 128:(qt + 1) * 128, :]:
                                e.dma_start(out=d, in_=s), qST, 16))
            if mode == "outer":
                P.do("sync", lambda e, s=a32(O_X1T, 1024),
                     d=y_d[qt * 128:(qt + 1) * 128, :]: e.dma_start(out=d, in_=s),
                     sO, 16)
            P.wait("tensor", sV, x1v)
            for dcg in range(2):
                b = dcg
                P.wait_t("tensor", P.bank_free[b])
                for k in range(4):
                    dc = dcg * 4 + k
                    src = a32(O_X1T, 1024)[:, dc * 128:dc * 128 + 128]
                    P.do("tensor", lambda e, o=bank(b, 128, off=k * 128), s=src:
                         e.matmul(o, s, sb_idF[:], start=True, stop=True),
                         sT if k == 3 else None, 1)
                mmv = sT.val
                P.wait("vector", sT, mmv)
                ev = None
                for k in range(4):
                    dc = dcg * 4 + k
                    dst = a32(R_RES + dc * T + qt * 128, 128)
                    ev = P.do("vector", lambda e, d=dst, s=bank(b, 128, off=k * 128):
                              e.tensor_copy(d, s), sV, 1)
                bank_read_done(b, sV, ev)
            P.wait("sync", sV, st_v)
            P.wait("scalar", qST, qST.val)
            P.wait("scalar", sO, sO.val)
            P.wait("scalar", sT, sT.val)
        outer_done_v = sV.val
        outer_done_s = sS.val
        outer_done_t = sT.val

        QS8B, KS8B = R_A + 2048, R_A + 3072
        TMPA = R_A
        XQW, XKW = R_QW, R_QW + 1024
        XQTR = [R_QT2, R_QT2 + 1024]
        EP_E = [R_EP, R_EP + 768, R_EP + 1536]
        EP_PT = R_SP + 2048
        MID = [R_W + i * 1024 for i in range(12)] + \
              [R_KT + i * 1024 for i in range(4)] + \
              [R_QT2 + i * 1024 for i in range(2)] + \
              [R_TAB + i * 1024 for i in range(2)] + \
              [R_SP + i * 1024 for i in range(2)]
        W1RING = [R_TAB + 2048, R_TAB + 4096]
        W2RING = [R_TAB + 6144, R_TAB + 7168, R_EP, R_EP + 1024]
        XSF = R_XS

        qWQ = SemC(ctx.enter_context(nc.semaphore("qWQ")))
        qTB = [SemC(ctx.enter_context(nc.semaphore(f"qTB{i}"))) for i in range(2)]
        qWO = SemC(ctx.enter_context(nc.semaphore("qWO")))
        qW1 = [SemC(ctx.enter_context(nc.semaphore(f"qW1{i}"))) for i in range(2)]
        qW2 = [SemC(ctx.enter_context(nc.semaphore(f"qW2{i}"))) for i in range(4)]
        qMS = SemC(ctx.enter_context(nc.semaphore("qMS")))
        qRS = SemC(ctx.enter_context(nc.semaphore("qRS")))
        qSC = SemC(ctx.enter_context(nc.semaphore("qSC")))
        qXS = SemC(ctx.enter_context(nc.semaphore("qXS")))

        state = dict(tab_free=[None, None], w1_free=[None, None],
                     w2_free=[None] * 4,
                     v_free=[None] * 4, smalls_free=None, res_tok=None, wo_free=None,
                     sq_free=[None] * 3, b_free=[None] * 3, chain_seq=[0])
        SQP = [R_A, R_A + 512, R_MASK]
        BP = [R_A + 1024, R_A + 1536, R_MASK + 512]

        def emit_inner(ell):
            if ell == 0:
                for eng in ("gpsimd", "sync", "tensor", "vector", "scalar"):
                    P.wait(eng, sT, outer_done_t)
                    P.wait(eng, sV, outer_done_v)
                    P.wait(eng, sS, outer_done_s)
                P.wait("sync", qST, qST.val)

            P.wait_t("sync", state["smalls_free"])
            qs_rep = qsc_d[ell][None, None, :].broadcast_to([128, HGH, DH])
            ks_rep = ksc_d[ell][None, None, :].broadcast_to([128, HGH, DH])
            P.wait("sync", qSC, qSC.val)
            for dst, srcp in [
                (a32(QS8B, 512).rearrange("p (a b) -> p a b", a=HGH), qs_rep),
                (a32(KS8B, 512).rearrange("p (a b) -> p a b", a=HGH), ks_rep),
            ]:
                P.do("sync", lambda e, d=dst, s=srcp: e.dma_start(out=d, in_=s), qSC, 16)
            sc_ld = qSC.val
            P.wait("vector", qSC, qSC.val)
            P.do("vector", lambda e, o=a32(QS8B, 512), i=a32(QS8B, 512):
                 e.tensor_scalar(o, i, 8.0, None, ALU.mult))
            qs8_v = sV.val
            P.wait_t("sync", state["res_tok"])
            colspecs = [
                (sm["gcol"], 8, lng_d[ell].rearrange("(c p) -> c p", p=128), None),
                (sm["bcol"], 8, lnb_d[ell].rearrange("(c p) -> c p", p=128), None),
                (sm["b2col"], 8, b2_d[ell].rearrange("(c p) -> c p", p=128), None),
                (sm["b1a"], 22, b1_d[ell, 0:2816].rearrange("(c p) -> c p", p=128), None),
                (sm["b1g"], 22, b1_d[ell, 2730:5418].rearrange("(c p) -> c p", p=128),
                 b1_d[ell, 5418:5460][None, :]),
            ]
            for dcol, ncc, srcp, tail in colspecs:
                ld = P.do("sync", lambda e, d=a32(TMPA, 128, pr=ncc if tail is None
                          else ncc - 1), s=srcp: e.dma_start(out=d, in_=s), qMS, 16)
                if tail is not None:
                    ld = P.do("sync", lambda e, d=a32(TMPA, 42, pr=1, po=ncc - 1),
                              s=tail: e.dma_start(out=d, in_=s), qMS, 16)
                P.wait("tensor", qMS, ld)
                P.wait_t("tensor", P.bank_free[5])
                mm = P.do("tensor", lambda e, o=bank(5, ncc), l=a32(TMPA, 128, pr=ncc),
                          r=sb_idF[0:ncc, 0:ncc]: e.matmul(o, l, r, start=True, stop=True),
                          sT, 1)
                P.wait("scalar", sT, mm)
                ev = P.do("scalar", lambda e, d=a32(dcol, ncc), s=bank(5, ncc):
                          e.copy(d, s))
                bank_read_done(5, sS, ev)
                P.wait("sync", sT, mm)
            smalls_ld = qMS.val
            smalls_s = sS.val

            P.wait_t("sync", state["res_tok"])
            P.wait("sync", qXS, qXS.val)
            rs = P.do("sync", lambda e, d=rstash_d, s=a32(R_RES, DC * T):
                      e.dma_start(out=d, in_=s), qRS, 16)

            qkv_mm_done = [0]
            for hg in range(2):
                P.wait_t("gpsimd", state["res_tok"])
                if hg == 1:
                    P.wait("gpsimd", sT, qkv_mm_done[0])
                for mi, cbase in enumerate((0, D, 2 * D)):
                    src = wqkv_d[ell][:, cbase + hg * 512: cbase + hg * 512 + 512] \
                        .rearrange("(c p) m -> p c m", p=128)
                    dst = a16(R_W + mi * 4096, 4096).rearrange("p (c m) -> p c m", c=DC)
                    P.do("gpsimd", lambda e, d=dst, s=src: e.dma_start(out=d, in_=s),
                         qWQ, 16)
                wqkv_ld = qWQ.val
                P.wait_t("gpsimd", state["wo_free"])
                for kc in range(4):
                    src = wo_d[ell, hg * 512 + kc * 128: hg * 512 + (kc + 1) * 128, :]
                    P.do("gpsimd", lambda e, d=a16(R_WO + kc * 1024, 1024), s=src:
                         e.dma_start(out=d, in_=s), qWO, 16)
                wo_ld = qWO.val
                if hg == 1:
                    P.wait("scalar", sT, sT.val)

                fw = [None] * NW
                qh_chains = [(0, 0), (0, 1), (1, 0), (1, 1)]

                def wave(st, mi):
                    for tt in range(2):
                        pairs = [(a16(R_XS + dc * 256 + tt * 128, 128),
                                  a16(R_W + mi * 4096 + dc * 512, 512))
                                 for dc in range(DC)]
                        st["wv"][(tt, mi)] = mm_acc(6 + tt, 512, pairs)

                def sslot(st, ci):
                    side, tt_ = qh_chains[ci]
                    if side == 0:
                        return a32(sm["ssq"] + tt_ * 16 + (st["w"] % 2) * 8, 8)
                    return a32(sm["ssk"] + tt_ * 8, 8)

                def chain_head(st, ci):
                    side, tt_ = qh_chains[ci]
                    par = state["chain_seq"][0] % 3
                    state["chain_seq"][0] += 1
                    st["cpar"][ci] = par
                    P.wait("scalar", sT, st["wv"][(tt_, side)])
                    P.wait_t("scalar", state["sq_free"][par])
                    P.do("scalar", lambda e, o=a32(SQP[par], 512), i=bank(6 + tt_):
                         e.activation(o, i, AF.Square))
                    st["st1"][ci] = sS.val

                def chain_mid(st, ci):
                    ssl = sslot(st, ci)
                    P.wait("vector", sS, st["st1"][ci])
                    P.do("vector", lambda e, o=ssl, i=a32(SQP[st["cpar"][ci]], 512)
                         .rearrange("p (h d) -> p h d", h=8):
                         e.tensor_reduce(o, i, AX.X, ALU.add))
                    st["st2"][ci] = sV.val
                    P.wait("scalar", sV, st["st2"][ci])
                    P.do("scalar", lambda e, o=ssl, i=ssl:
                         e.activation(o, i, AF.Ln, bias=sb_eps24[:]))
                    P.do("scalar", lambda e, o=ssl, i=ssl:
                         e.activation(o, i, AF.Exp, scale=-0.5))
                    st["st3"][ci] = sS.val

                def chain_tail(st, ci):
                    side, tt_ = qh_chains[ci]
                    b = 6 + tt_
                    par = st["cpar"][ci]
                    tsl = st["w"] % 2
                    cosb = R_TAB + tsl * 4096 + (0 if side == 0 else 2) * 1024 + tt_ * 512
                    sinb = R_TAB + tsl * 4096 + (1 if side == 0 else 3) * 1024 + tt_ * 512
                    dstw = (XQW if side == 0 else XKW) + tt_ * 512
                    Bp = a32(BP[par], 512)
                    P.wait_t("vector", state["b_free"][par])
                    P.wait("vector", sT, st["wv"][(tt_, side)])
                    P.wait("vector", qSC, sc_ld)
                    if side == 0:
                        P.wait("vector", sS, st["st1"][ci])
                        P.do("vector", lambda e, o=Bp, i=bank(b), j=a32(QS8B, 512):
                             e.tensor_tensor(o, i, j, ALU.mult))
                    else:
                        P.wait("vector", sS, st["st3"][ci])
                        ssl = sslot(st, ci)
                        for h in range(8):
                            P.do("vector", lambda e, o=Bp[:, h * 64:h * 64 + 64],
                                 i=bank(b, 64, off=h * 64), r=ssl[:, h:h + 1],
                                 j=a32(KS8B + h * 64, 64):
                                 e.scalar_tensor_tensor(o, i, r, j, ALU.mult, ALU.mult))
                    bank_read_done(b, sV, sV.val)
                    P.wait("vector", qTB[tsl], st["tab_ld"])
                    hv = lambda base, lo: a32(base, 512).rearrange(
                        "p (h x) -> p h x", h=8)[:, :, lo:lo + 32]
                    hv16 = lambda base, lo: a16(base, 512).rearrange(
                        "p (h x) -> p h x", h=8)[:, :, lo:lo + 32]
                    P.do("vector", lambda e, o=hv(SQP[par], 0), i=hv(BP[par], 32),
                         j=hv16(sinb, 32): e.tensor_tensor(o, i, j, ALU.mult))
                    P.do("vector", lambda e, o=hv(SQP[par], 32), i=hv(BP[par], 0),
                         j=hv16(sinb, 0): e.tensor_tensor(o, i, j, ALU.mult))
                    P.do("vector", lambda e, o=Bp, i=Bp, j=a16(cosb, 512):
                         e.tensor_tensor(o, i, j, ALU.mult))
                    P.do("vector", lambda e, o=a16(dstw, 512), i=Bp,
                         j=a32(SQP[par], 512): e.tensor_tensor(o, i, j, ALU.add))
                    st["st7"][ci] = sV.val
                    state["sq_free"][par] = (sV, sV.val)
                    state["b_free"][par] = (sV, sV.val)

                def qk_transpose(st, side, bnk, ready_v):
                    w = st["w"]
                    P.wait("tensor", sV, ready_v)
                    for f0 in (0, 2):
                        P.wait_t("tensor", P.bank_free[bnk])
                        for k2, (fp, tt_) in enumerate(
                                [(f0, 0), (f0, 1), (f0 + 1, 0), (f0 + 1, 1)]):
                            srcb = (XQW if side == 0 else XKW) + tt_ * 512
                            P.do("tensor", lambda e, o=bank(bnk, 128, off=k2 * 128),
                                 l=a16(srcb + fp * 128, 128):
                                 e.matmul(o, l, sb_idB[:], start=True, stop=True),
                                 sT if k2 == 3 else None, 1)
                        mmv = sT.val
                        P.wait("scalar", sT, mmv)
                        if side == 0:
                            dsts = [a16(XQTR[w % 2] + f0 * 256, 512)]
                        else:
                            kt3 = A16[:, R_KT:R_KT + 4096].rearrange(
                                "p (f x) -> p f x", f=4)
                            dsts = [kt3[:, f0:f0 + 2,
                                        (w % 2) * 512 + 256:(w % 2) * 512 + 512],
                                    kt3[:, f0:f0 + 2,
                                        ((w + 1) % 2) * 512:((w + 1) % 2) * 512 + 256]]
                        ev = None
                        for dst in dsts:
                            ev = P.do("scalar", lambda e, d=dst, s=bank(bnk, 512):
                                      e.copy(d, s))
                        bank_read_done(bnk, sS, ev)

                def front_p1a(w):
                    st = {"w": w, "wv": {}, "cpar": {}, "st1": {}, "st2": {},
                          "st3": {}, "st7": {}}
                    fw[w] = st
                    tsl = w % 2
                    P.wait_t("sync", state["tab_free"][tsl])
                    for ti in range(4):
                        src = tbl_d[ti, w * 256:(w + 1) * 256, :] \
                            .rearrange("(tt p) c -> p tt c", p=128)
                        dst = a16(R_TAB + tsl * 4096 + ti * 1024, 1024) \
                            .rearrange("p (tt c) -> p tt c", tt=2)
                        P.do("sync", lambda e, d=dst, s=src: e.dma_start(out=d, in_=s),
                             qTB[tsl], 16)
                    st["tab_ld"] = qTB[tsl].val
                    P.wait("gpsimd", sT, qkv_mm_done[0])
                    P.wait("gpsimd", qRS, rs)
                    src0 = rstash_d.rearrange("p (c t) -> p c t", c=DC)[
                        :, 0:4, w * 256:(w + 1) * 256]
                    dst0 = a16(R_XS, 1024).rearrange("p (c t) -> p c t", c=4)
                    P.do("gpsimd", lambda e, d=dst0, s=src0: e.dma_start(out=d, in_=s),
                         qXS, 16)
                    if w > 0:
                        src1 = rstash_d.rearrange("p (c t) -> p c t", c=DC)[
                            :, 4:8, w * 256 - 1:(w + 1) * 256 - 1]
                        dst1 = a16(R_XS + 1024, 1024).rearrange("p (c t) -> p c t", c=4)
                        P.do("gpsimd", lambda e, d=dst1, s=src1:
                             e.dma_start(out=d, in_=s), qXS, 16)
                        st["xs_v"] = 0
                    else:
                        src1 = rstash_d.rearrange("p (c t) -> p c t", c=DC)[
                            :, 4:8, 0:255]
                        dst1 = a16(R_XS + 1024, 1024).rearrange(
                            "p (c t) -> p c t", c=4)[:, :, 1:256]
                        P.do("gpsimd", lambda e, d=dst1, s=src1:
                             e.dma_start(out=d, in_=s), qXS, 16)
                        P.wait("vector", sT, qkv_mm_done[0])
                        for c4 in range(4):
                            P.do("vector", lambda e, d=a16(R_XS + 1024 + c4 * 256, 1):
                                 e.memset(d, 0.0))
                        st["xs_v"] = sV.val
                    st["xs_ld"] = qXS.val

                def front_p1b(w):
                    st = fw[w]
                    P.wait("tensor", qWQ, wqkv_ld)
                    P.wait("tensor", qXS, st["xs_ld"])
                    P.wait("tensor", sV, st["xs_v"])
                    wave(st, 0)
                    for ci in (0, 1):
                        chain_head(st, ci)
                    for ci in (0, 1):
                        chain_mid(st, ci)
                    chain_tail(st, 0)
                    chain_tail(st, 1)

                def front_p2(w):
                    st = fw[w]
                    wave(st, 1)
                    for ci in (2, 3):
                        chain_head(st, ci)
                    for ci in (2, 3):
                        chain_mid(st, ci)
                    chain_tail(st, 2)
                    chain_tail(st, 3)
                    state["tab_free"][st["w"] % 2] = (sV, sV.val)

                def front_p3(w):
                    st = fw[w]
                    wave(st, 2)
                    qkv_mm_done[0] = sT.val

                def front_p3b(w):
                    st = fw[w]
                    qk_transpose(st, 0, 6, max(st["st7"][0], st["st7"][1]))
                    qk_transpose(st, 1, 7, max(st["st7"][2], st["st7"][3]))
                    st["tr_s"] = sS.val

                def front_p4(w, pv_tok):
                    st = fw[w]
                    for tt in range(2):
                        slot = (st["w"] * 2 + tt) % 4
                        P.wait("scalar", sT, st["wv"][(tt, 2)])
                        P.wait_t("scalar", pv_tok)
                        P.wait_t("scalar", state["v_free"][slot])
                        ev = P.do("scalar", lambda e, o=a16(R_V + slot * 512, 512),
                                  i=bank(6 + tt): e.copy(o, i))
                        bank_read_done(6 + tt, sS, ev)
                    st["prep_s"] = sS.val

                def sims_window(w):
                    st = fw[w]
                    wp = w % 2
                    nj0 = 128 if w == 0 else 384
                    nj1 = 256 if w == 0 else 384
                    m0 = MTRI if w == 0 else MG0
                    m1 = MW01 if w == 0 else MG0
                    half = 0 if w == 0 else wp
                    col0 = 256 if w == 0 else 0
                    col1 = 256 if w == 0 else 128
                    jtts0 = [0] if w == 0 else [2 * w - 2, 2 * w - 1, 2 * w]
                    jtts1 = [0, 1] if w == 0 else [2 * w - 1, 2 * w, 2 * w + 1]
                    nch0, nch1 = len(jtts0), len(jtts1)
                    P.wait("tensor", sS, st["tr_s"])

                    def ssq_col(tt_, h):
                        return a32(sm["ssq"] + tt_ * 16 + wp * 8 + h, 1)

                    def sim_S(h):
                        pt = h // 2
                        po = (h % 2) * 64
                        par = (h % 3) * 2
                        for qh, nj, msk in ((0, nj0, m0), (1, nj1, m1)):
                            bnk = par + qh
                            P.wait_t("tensor", P.bank_free[bnk])
                            P.do("tensor", lambda e, o=bank(bnk, nj),
                                 m=sb_mskb[:, msk:msk + nj]:
                                 e.matmul(o, sb_idB[:], m, start=True, stop=False))
                        for qh, nj, cc in ((0, nj0, col0), (1, nj1, col1)):
                            bnk = par + qh
                            lhs = a16(XQTR[wp] + pt * 256 + qh * 128, 128, pr=64, po=po)
                            rhs = a16(R_KT + pt * 1024 + half * 512 + cc, nj,
                                      pr=64, po=po)
                            P.do("tensor", lambda e, o=bank(bnk, nj), l=lhs, r=rhs:
                                 e.matmul(o, l, r, start=False, stop=True), sT, 1)
                            P.bank_ready[bnk] = (sT, sT.val)
                        return sT.val

                    def sim_max(h):
                        par = (h % 3) * 2
                        P.wait("vector", sT, s_mm_of[h])
                        for qh, nj in ((0, nj0), (1, nj1)):
                            P.do("vector", lambda e,
                                 o=a32(sm["negmax"] + (h % 3) * 2 + qh, 1),
                                 i=bank(par + qh, nj):
                                 e.tensor_reduce(o, i, AX.X, ALU.max, negate=True))
                        P.wait("vector", sS, max(st["st3"][0], st["st3"][1]))
                        for qh in range(2):
                            P.do("vector", lambda e,
                                 o=a32(sm["negmax"] + (h % 3) * 2 + qh, 1),
                                 i=a32(sm["negmax"] + (h % 3) * 2 + qh, 1),
                                 r=ssq_col(qh, h):
                                 e.tensor_scalar(o, i, r, None, ALU.mult))
                        msk_v_of[h] = sV.val

                    def sim_exp(h):
                        par = (h % 3) * 2
                        P.wait("scalar", sV, msk_v_of[h])
                        if h >= 3:
                            P.wait("scalar", sT, ptmm_of[h - 3])
                        for qh, nj in ((0, nj0), (1, nj1)):
                            bnk = par + qh
                            ex = P.do("scalar", lambda e,
                                      o=a16(EP_E[h % 3] + qh * 384, nj),
                                      i=bank(bnk, nj),
                                      b=a32(sm["negmax"] + (h % 3) * 2 + qh, 1),
                                      r=ssq_col(qh, h),
                                      sx=a32(sm["sumexp"] + (h % 3) * 2 + qh, 1):
                                      e.activation(o, i, AF.Exp, bias=b, scale=r,
                                                   accum_out=sx))
                            bank_read_done(bnk, sS, ex)
                        exp_s_of[h] = sS.val

                    def sim_norm(h):
                        P.wait("vector", sS, exp_s_of[h])
                        P.do("vector", lambda e,
                             o=a32(sm["recip"] + (h % 3) * 2, 2),
                             i=a32(sm["sumexp"] + (h % 3) * 2, 2):
                             e.reciprocal(o, i))
                        for qh, nj in ((0, nj0), (1, nj1)):
                            P.do("vector", lambda e, o=a16(EP_E[h % 3] + qh * 384, nj),
                                 i=a16(EP_E[h % 3] + qh * 384, nj),
                                 r=a32(sm["recip"] + (h % 3) * 2 + qh, 1):
                                 e.tensor_scalar(o, i, r, None, ALU.mult))
                        norm_v_of[h] = sV.val

                    def sim_ptpv(h):
                        pt = h // 2
                        po = (h % 2) * 64
                        P.wait("tensor", sV, norm_v_of[h])
                        tiles = [(0, jc) for jc in range(nch0)] + \
                                [(1, jc) for jc in range(nch1)]
                        bA = (((h + 1) % 3) * 2)
                        bB = bA + 1
                        evs = []
                        for gi, gstart in enumerate(range(0, len(tiles), 4)):
                            grp = tiles[gstart:gstart + 4]
                            bnk = bA if gi == 0 else bB
                            P.wait_t("tensor", P.bank_free[bnk])
                            for k2, (qh, jc) in enumerate(grp):
                                P.do("tensor", lambda e, o=bank(bnk, 128, off=k2 * 128),
                                     l=a16(EP_E[h % 3] + qh * 384 + jc * 128, 128):
                                     e.matmul(o, l, sb_idB[:], start=True, stop=True),
                                     sT if k2 == len(grp) - 1 else None, 1)
                            evs.append((bnk, sT.val, gstart, len(grp)))
                        ptmm_of[h] = sT.val
                        for bnk, mmv, gstart, glen in evs:
                            P.wait("scalar", sT, mmv)
                            ev = P.do("scalar", lambda e,
                                      d=a16(EP_PT + gstart * 128, 128 * glen),
                                      s=bank(bnk, 128 * glen): e.copy(d, s))
                            bank_read_done(bnk, sS, ev)
                        pt_s = sS.val
                        P.wait("tensor", sS, pt_s)
                        P.wait("tensor", sS, st["prep_s"])
                        P.wait_t("tensor", P.bank_free[bA])
                        for qh, jtts in ((0, jtts0), (1, jtts1)):
                            nch = len(jtts)
                            for ji, jtt in enumerate(jtts):
                                idx = ji if qh == 0 else nch0 + ji
                                P.do("tensor", lambda e,
                                     o=bank(bA, 128, off=qh * 128, pr=64, po=po),
                                     l=a16(R_V + (jtt % 4) * 512 + h * 64, 64),
                                     r=a16(EP_PT + idx * 128, 128),
                                     st=(ji == 0), sp=(ji == nch - 1):
                                     e.matmul(o, l, r, start=st, stop=sp),
                                     sT if ji == nch - 1 else None, 1)
                        pv_mm = sT.val
                        P.wait("scalar", sT, pv_mm)
                        ev = P.do("scalar", lambda e,
                                  d=a16(R_AT + pt * 2048 + w * 256, 256, pr=64, po=po),
                                  s=bank(bA, 256, pr=64, po=po): e.copy(d, s))
                        bank_read_done(bA, sS, ev)

                    s_mm_of = {}
                    msk_v_of = {}
                    exp_s_of = {}
                    ptmm_of = {}
                    norm_v_of = {}
                    nxt = w + 1 if w + 1 < NW else None
                    for i in range(11):
                        if i < 8:
                            s_mm_of[i] = sim_S(i)
                        if 1 <= i <= 8:
                            sim_max(i - 1)
                        if i >= 3:
                            sim_ptpv(i - 3)
                        if 1 <= i <= 8:
                            sim_exp(i - 1)
                        if 2 <= i <= 9:
                            sim_norm(i - 2)
                        if nxt is not None:
                            if i == 0:
                                front_p1a(nxt)
                            elif i == 2:
                                front_p1b(nxt)
                            elif i == 5:
                                front_p2(nxt)
                            elif i == 8:
                                front_p3(nxt)
                            elif i == 10:
                                front_p4(nxt, (sT, sT.val))
                    if nxt is not None:
                        front_p3b(nxt)
                    if w > 0:
                        for tt in range(2):
                            state["v_free"][((w - 1) * 2 + tt) % 4] = (sT, sT.val)

                front_p1a(0)
                front_p1b(0)
                front_p2(0)
                front_p3(0)
                front_p4(0, None)
                front_p3b(0)
                for w in range(NW):
                    sims_window(w)
                P.wait("tensor", qWO, wo_ld)
                P.wait("tensor", sS, sS.val)
                P.wait("vector", qRS, rs)
                for tg in range(4):
                    for mc in range(DC):
                        b = 5 + ((tg * DC + mc) % 2)
                        pairs = [(a16(R_WO + kc * 1024 + mc * 128, 128),
                                  a16(R_AT + kc * 2048 + tg * 512, 512))
                                 for kc in range(4)]
                        mmv = mm_acc(b, 512, pairs)
                        P.wait("vector", sT, mmv)
                        dstr = a32(R_RES + mc * T + tg * 512, 512)
                        ev = P.do("vector", lambda e, d=dstr, p_=bank(b):
                                  e.tensor_tensor(d, d, p_, ALU.add))
                        bank_read_done(b, sV, ev)
                state["wo_free"] = (sT, sT.val)
                for sl4 in range(4):
                    state["v_free"][sl4] = (sT, sT.val)
            res_after_attn = sV.val
            if attn_only and ell == nlayers - 1:
                state["res_tok"] = (sV, sV.val)
                state["smalls_free"] = (sV, sV.val)
                return

            P.wait("scalar", sV, res_after_attn)
            for c4 in range(4):
                P.do("scalar", lambda e, d=a32(sm["bnd"] + c4, 1),
                     s=a32(R_RES + (4 + c4) * T + 1023, 1): e.copy(d, s))
            bnd_s = sS.val
            for half in range(2):
                hbase = half * 1024
                for tg2 in range(2):
                    tgb = hbase + tg2 * 512
                    P.wait("scalar", sV, res_after_attn if half == 0 else sV.val)
                    for dc in range(DC):
                        dst = a16(XSF + dc * 512, 512)
                        if dc < 4:
                            P.do("scalar", lambda e, d=dst,
                                 s=a32(R_RES + dc * T + tgb, 512): e.copy(d, s))
                        elif tgb == 0:
                            P.do("vector", lambda e, d=a16(XSF + dc * 512, 1):
                                 e.memset(d, 0.0))
                            P.do("scalar", lambda e, d=a16(XSF + dc * 512 + 1, 511),
                                 s=a32(R_RES + dc * T, 511): e.copy(d, s))
                        elif tgb == 1024:
                            P.wait("scalar", sS, bnd_s)
                            P.do("scalar", lambda e, d=a16(XSF + dc * 512, 1),
                                 s=a32(sm["bnd"] + dc - 4, 1): e.copy(d, s))
                            P.do("scalar", lambda e, d=a16(XSF + dc * 512 + 1, 511),
                                 s=a32(R_RES + dc * T + tgb, 511): e.copy(d, s))
                        else:
                            P.do("scalar", lambda e, d=dst,
                                 s=a32(R_RES + dc * T + tgb - 1, 512): e.copy(d, s))
                    xs_s = sS.val
                    xs_v = sV.val
                    P.wait("tensor", sS, xs_s)
                    P.wait("tensor", sV, xs_v)
                    pairs = [(sb_c1024[:], a16(XSF + dc * 512, 512)) for dc in range(DC)]
                    mu_mm = mm_acc(6, 512, pairs, opr=1)
                    P.wait("vector", sT, mu_mm)
                    mu_c = P.do("vector", lambda e, o=a32(sm["murow"], 512, pr=1),
                                i=bank(6, 512, pr=1): e.tensor_copy(o, i))
                    bank_read_done(6, sV, mu_c)
                    P.wait_t("tensor", P.bank_free[7])
                    P.wait("tensor", sV, mu_c)
                    P.do("tensor", lambda e, o=bank(7, 512),
                         l=sb_row1[:], r=a32(sm["murow"], 512, pr=1):
                         e.matmul(o, l, r, start=True, stop=True), sT, 1)
                    mub_mm = sT.val
                    P.wait("vector", sT, mub_mm)
                    for dc in range(DC):
                        P.do("vector", lambda e, o=a32(R_A + dc * 512, 512),
                             i=a16(XSF + dc * 512, 512), j=bank(7, 512):
                             e.tensor_tensor(o, i, j, ALU.subtract))
                    p1_v = sV.val
                    bank_read_done(7, sV, p1_v)
                    P.wait("scalar", sV, p1_v)
                    P.wait_t("tensor", P.bank_free[6])
                    P.wait("scalar", sT, sT.val)
                    for dc in range(DC):
                        P.do("scalar", lambda e, o=a16(R_W + dc * 512, 512),
                             i=a32(R_A + dc * 512, 512): e.activation(o, i, AF.Square))
                    sq_s = sS.val
                    P.wait("tensor", sS, sq_s)
                    for dc in range(DC):
                        P.do("tensor", lambda e, o=bank(6, 512, pr=1),
                             l=sb_c1024[:], r=a16(R_W + dc * 512, 512),
                             st=(dc == 0), sp=(dc == DC - 1):
                             e.matmul(o, l, r, start=st, stop=sp),
                             sT if dc == DC - 1 else None, 1)
                    var_mm = sT.val
                    P.wait("vector", sT, var_mm)
                    vr_c = P.do("vector", lambda e, o=a32(sm["rstdrow"], 512, pr=1),
                                i=bank(6, 512, pr=1): e.tensor_copy(o, i))
                    bank_read_done(6, sV, vr_c)
                    P.wait("scalar", sV, vr_c)
                    P.do("scalar", lambda e, o=a32(sm["rstdrow"], 512, pr=1),
                         i=a32(sm["rstdrow"], 512, pr=1):
                         e.activation(o, i, AF.Ln, bias=sb_eps[0:1, :]))
                    rst_s = P.do("scalar", lambda e, o=a32(sm["rstdrow"], 512, pr=1),
                                 i=a32(sm["rstdrow"], 512, pr=1):
                                 e.activation(o, i, AF.Exp, scale=-0.5))
                    P.wait_t("tensor", P.bank_free[7])
                    P.wait("tensor", sS, rst_s)
                    P.do("tensor", lambda e, o=bank(7, 512),
                         l=sb_row1[:], r=a32(sm["rstdrow"], 512, pr=1):
                         e.matmul(o, l, r, start=True, stop=True), sT, 1)
                    ab_mm = sT.val
                    P.wait("vector", sT, ab_mm)
                    P.wait("vector", sS, smalls_s)
                    for dc in range(DC):
                        P.do("vector", lambda e, o=a32(R_A + dc * 512, 512),
                             i=a32(R_A + dc * 512, 512), j=bank(7, 512):
                             e.tensor_tensor(o, i, j, ALU.mult))
                        P.do("vector", lambda e, o=a16(R_AT + dc * 1024 + tg2 * 512, 512),
                             i=a32(R_A + dc * 512, 512), g=a32(sm["gcol"] + dc, 1),
                             bb=a32(sm["bcol"] + dc, 1):
                             e.tensor_scalar(o, i, g, bb, ALU.mult, ALU.add))
                    xn_v = sV.val
                    bank_read_done(7, sV, xn_v)
                xn_done = sV.val

                P.wait("gpsimd", sV, xn_done)
                for ci in range(NKC):
                    wci = 128 if ci < NKC - 1 else DI - 128 * (NKC - 1)
                    slot = ci % 2
                    P.wait_t("gpsimd", state["w1_free"][slot])
                    for gside in range(2):
                        c0 = gside * DI + ci * 128
                        src = w1_d[ell][:, c0:c0 + wci] \
                            .rearrange("(c p) m -> p c m", p=128)
                        dst = a16(W1RING[slot] + gside * 1024, DC * wci) \
                            .rearrange("p (c m) -> p c m", c=DC)
                        P.do("gpsimd", lambda e, d=dst, s=src: e.dma_start(out=d, in_=s),
                             qW1[slot], 16)
                    w1_ld = qW1[slot].val
                    P.wait("tensor", qW1[slot], w1_ld)
                    P.wait("tensor", sV, xn_done)
                    for tg2 in range(2):
                        for gside in range(2):
                            pairs = [(a16(W1RING[slot] + gside * 1024 + dc * wci, wci),
                                      a16(R_AT + dc * 1024 + tg2 * 512, 512))
                                     for dc in range(DC)]
                            mm_acc((ci % 2) * 4 + tg2 * 2 + gside, 512, pairs, opr=wci)
                    w1_mm = sT.val
                    state["w1_free"][slot] = (sT, w1_mm)
                    for tg2 in range(2):
                        ba, bg = (ci % 2) * 4 + tg2 * 2, (ci % 2) * 4 + tg2 * 2 + 1
                        P.wait("scalar", sT, w1_mm)
                        P.wait("scalar", sV, sV.val)
                        gl = P.do("scalar", lambda e, o=a16(R_V + 512, 512, pr=wci),
                                  i=bank(bg, 512, pr=wci),
                                  bb=a32(sm["b1g"] + ci, 1, pr=wci):
                                  e.activation(o, i, AF.Gelu, bias=bb))
                        bank_read_done(bg, sS, gl)
                        P.wait("vector", sS, gl)
                        mv = P.do("vector", lambda e,
                                  o=a16(MID[ci] + tg2 * 512, 512, pr=wci),
                                  i=bank(ba, 512, pr=wci),
                                  bb=a32(sm["b1a"] + ci, 1, pr=wci),
                                  g=a16(R_V + 512, 512, pr=wci):
                                  e.scalar_tensor_tensor(o, i, bb, g, ALU.add, ALU.mult))
                        bank_read_done(ba, sV, mv)
                mid_v = sV.val
                mid_s = sS.val

                for tg2 in range(2):
                    tgb = hbase + tg2 * 512
                    w2_done = [None] * DC
                    for kc in range(NKC):
                        wkc = 128 if kc < NKC - 1 else DI - 128 * (NKC - 1)
                        slot = kc % 4
                        P.wait_t("gpsimd", state["w2_free"][slot])
                        src = w2_d[ell, kc * 128:kc * 128 + wkc, :]
                        wd = P.do("gpsimd", lambda e, d=a16(W2RING[slot], 1024, pr=wkc),
                                  s=src: e.dma_start(out=d, in_=s), qW2[slot], 16)
                        P.wait("tensor", qW2[slot], wd)
                        if kc == 0:
                            P.wait("tensor", sV, mid_v)
                            P.wait("tensor", sS, mid_s)
                            for b2_ in range(8):
                                P.wait_t("tensor", P.bank_free[b2_])
                        for mc in range(DC):
                            tok = P.do("tensor", lambda e, o=bank(mc, 512),
                                       l=a16(W2RING[slot] + mc * 128, 128, pr=wkc),
                                       r=a16(MID[kc] + tg2 * 512, 512, pr=wkc),
                                       st=(kc == 0), sp=(kc == NKC - 1):
                                       e.matmul(o, l, r, start=st, stop=sp),
                                       sT if (mc == DC - 1 or kc == NKC - 1) else None, 1)
                            if kc == NKC - 1:
                                w2_done[mc] = tok
                        state["w2_free"][slot] = (sT, sT.val)
                    for mc in range(DC):
                        P.wait("vector", sT, w2_done[mc])
                        dstr = a32(R_RES + mc * T + tgb, 512)
                        ev = P.do("vector", lambda e, d=dstr, p_=bank(mc),
                                  bb=a32(sm["b2col"] + mc, 1):
                                  e.scalar_tensor_tensor(d, p_, bb, d, ALU.add, ALU.add))
                        bank_read_done(mc, sV, ev)
            state["res_tok"] = (sV, sV.val)
            state["smalls_free"] = (sV, sV.val)
            for sl4 in range(4):
                state["v_free"][sl4] = (sT, sT.val)

        for ell in range(nlayers):
            emit_inner(ell)

        if dump:
            P.wait("sync", sV, sV.val)
            P.wait("sync", sS, sS.val)
            P.do("sync", lambda e: e.dma_start(out=ydbg_d, in_=a32(R_RES, DC * T)),
                 sO, 16)
            P.do("sync", lambda e: e.wait_ge(sO.h, sO.val))
        elif mode == "full":
            P.wait("sync", sV, sV.val)
            P.wait("sync", sS, sS.val)
            P.wait("tensor", sV, sV.val)
            P.wait("tensor", sS, sS.val)
            P.wait("scalar", sV, sV.val)
            P.wait("vector", sS, sS.val)
            P.wait("sync", qST, qST.val)
            for qt in range(NT):
                xd = P.do("sync", lambda e, d=a32(O_X0R[0], 1024),
                          s=x1s_d[qt * 128:(qt + 1) * 128, :]:
                          e.dma_start(out=d, in_=s), qX0[0], 16)
                for dcg in range(2):
                    b = dcg
                    P.wait_t("tensor", P.bank_free[b])
                    for k in range(4):
                        dc = dcg * 4 + k
                        src = a32(R_RES + dc * T + qt * 128, 128)
                        P.do("tensor", lambda e, o=bank(b, 128, off=k * 128), s=src:
                             e.matmul(o, s, sb_idF[:], start=True, stop=True),
                             sT if k == 3 else None, 1)
                    mmv = sT.val
                    P.wait("vector", sT, mmv)
                    P.wait("vector", qX0[0], xd)
                    ev = P.do("vector", lambda e, o=a32(O_ST + dcg * 512, 512),
                              p_=bank(b), xs=a32(O_X0R[0] + dcg * 512, 512):
                              e.tensor_tensor(o, p_, xs, ALU.add), sV, 1)
                    bank_read_done(b, sV, ev)
                P.wait("vector", sV, sV.val)
                stile = a32(O_ST, 1024)
                negmu = a32(sm["t1"], 1)
                P.do("vector", lambda e, o=negmu, i=stile: e.tensor_reduce(
                    o, i, AX.X, ALU.add, negate=True), sV, 1)
                P.wait("vector", sV, sV.val)
                P.do("vector", lambda e, o=negmu, i=negmu: e.tensor_scalar(
                    o, i, 1.0 / 1024.0, None, ALU.mult), sV, 1)
                mu_v = sV.val
                P.wait("scalar", sV, mu_v)
                ssq = a32(sm["t2"], 1)
                P.do("scalar", lambda e, o=a32(O_X1T, 1024), i=stile, b=negmu, sx=ssq:
                     e.activation(o, i, AF.Square, bias=b, accum_out=sx), sS, 1)
                P.do("scalar", lambda e, o=ssq, i=ssq: e.activation(
                    o, i, AF.Ln, scale=1.0 / 1024.0, bias=sb_eps[:]), sS, 1)
                P.do("scalar", lambda e, o=ssq, i=ssq: e.activation(
                    o, i, AF.Exp, scale=-0.5), sS, 1)
                rstd_s = sS.val
                P.wait("vector", sS, rstd_s)
                P.do("vector", lambda e, o=a32(sm["t0"], 1), i=negmu: e.tensor_tensor(
                    o, i, ssq, ALU.mult), sV, 1)
                m2_v = sV.val
                P.wait("vector", sV, m2_v)
                yv = P.do("vector", lambda e, o=a32(O_X1T, 1024), i=stile:
                          e.tensor_scalar(o, i, ssq, a32(sm["t0"], 1), ALU.mult, ALU.add),
                          sV, 1)
                P.wait("sync", sV, yv)
                P.do("sync", lambda e, s=a32(O_X1T, 1024),
                     d=y_d[qt * 128:(qt + 1) * 128, :]: e.dma_start(out=d, in_=s),
                     sO, 16)
                P.wait("tensor", sV, yv)
                P.wait("scalar", sO, sO.val)
                P.wait("sync", sV, sV.val)
            P.do("sync", lambda e: e.wait_ge(sO.h, sO.val))
        else:
            P.do("sync", lambda e: e.wait_ge(sO.h, sO.val))

        P.flush(block)
    return nc



def _in_maps(inputs, consts):
    x = np.ascontiguousarray(np.asarray(inputs["L0_attn_embeddings"], dtype=np.float32))
    base = {
        "wq": np.ascontiguousarray(np.asarray(inputs["L0_attn_W_Q"], np.float32)),
        "wk": np.ascontiguousarray(np.asarray(inputs["L0_attn_W_K"], np.float32)),
        "wv": np.ascontiguousarray(np.asarray(inputs["L0_attn_W_V"], np.float32)),
        "wqkv": np.ascontiguousarray(np.asarray(inputs["L0_ffn_Wqkv"], np.float32)),
        "qsc": np.ascontiguousarray(np.asarray(inputs["L0_ffn_q_scale"], np.float32)),
        "ksc": np.ascontiguousarray(np.asarray(inputs["L0_ffn_k_scale"], np.float32)),
        "wo": np.ascontiguousarray(np.asarray(inputs["L0_ffn_Wo"], np.float32)),
        "lng": np.ascontiguousarray(np.asarray(inputs["L0_ffn_ln_g"], np.float32)),
        "lnb": np.ascontiguousarray(np.asarray(inputs["L0_ffn_ln_b"], np.float32)),
        "w1": np.ascontiguousarray(np.asarray(inputs["L0_ffn_W1"], np.float32)),
        "b1": np.ascontiguousarray(np.asarray(inputs["L0_ffn_b1"], np.float32)),
        "w2": np.ascontiguousarray(np.asarray(inputs["L0_ffn_W2"], np.float32)),
        "b2": np.ascontiguousarray(np.asarray(inputs["L0_ffn_b2"], np.float32)),
    }
    base.update(consts)
    maps = []
    for c in range(NB):
        m = dict(base)
        m["x0"] = np.ascontiguousarray(x[c])
        maps.append(m)
    return maps


def kernel(**inputs):
    consts = _host_consts()
    nc = build_program(depth=DEPTH, mode="full")
    maps = _in_maps(inputs, consts)
    res = run_bass_kernel_spmd(nc, maps, core_ids=list(range(NB)))
    out = np.stack([np.asarray(r["y"], dtype=np.float32) for r in res.results])
    return out


# version 13
# speedup vs baseline: 1.0121x; 1.0066x over previous
import numpy as np
import ml_dtypes
from contextlib import ExitStack

import concourse.bass as bass
import concourse.mybir as mybir
from concourse.bass_utils import run_bass_kernel_spmd

F32 = mybir.dt.float32
BF16 = mybir.dt.bfloat16
AF = mybir.ActivationFunctionType
ALU = mybir.AluOpType
AX = mybir.AxisListType

NB = 8
T = 2048
D = 1024
DEPTH = 6
NHEAD = 16
DH = 64
WIN = 256
DI = 2730
NW = T // WIN
NT = T // 128
DC = D // 128
HG = 2
HGH = NHEAD // HG
NEG = -3.0e37
NKC = (DI + 127) // 128

MG0, MG1, MW00, MW01, MTRI = 0, 384, 896, 1152, 1408


def _host_consts():
    half = np.arange(0, DH, 2, dtype=np.float64)
    inv_freq = 1.0 / (10000.0 ** (half / DH))
    t = np.arange(T, dtype=np.float64)
    freqs = np.concatenate([t[:, None] * inv_freq[None, :]] * 2, -1)
    base_scale = (half + 0.4 * DH) / (1.4 * DH)
    power = (t - T // 2) / (WIN // 2)
    scale = np.concatenate([base_scale[None, :] ** power[:, None]] * 2, -1)
    cos = np.cos(freqs); sin = np.sin(freqs)
    sign = np.where(np.arange(DH) < DH // 2, -1.0, 1.0)[None, :]
    rot = lambda a: np.roll(a, -32, axis=1)
    rep = lambda a: np.tile(a, (1, HGH)).astype(ml_dtypes.bfloat16)
    tables = np.stack([rep(cos * scale), rep(rot(sign * sin * scale)),
                       rep(cos / scale), rep(rot(sign * sin / scale))])
    p = np.arange(128)[:, None]

    def band(ncols, lo, hi):
        j = np.arange(ncols)[None, :]
        return np.where((j >= lo) & (j <= hi), 0.0, NEG).astype(np.float32)
    masks = np.concatenate([
        band(384, p, p + WIN),
        band(512, p + 128, p + 384),
        band(256, 0 * p, p),
        band(256, 0 * p, p + 128),
        band(128, 0 * p, p),
    ], axis=1)
    return dict(
        tables=np.ascontiguousarray(tables),
        masks=masks,
        masksb=masks.astype(ml_dtypes.bfloat16),
        identF=np.eye(128, dtype=np.float32),
        identB=np.eye(128, dtype=np.float32).astype(ml_dtypes.bfloat16),
        c1024=np.full((128, 1), 1.0 / 1024.0, dtype=ml_dtypes.bfloat16),
        row8=np.full((1, 128), 8.0, dtype=np.float32),
        row1=np.full((1, 128), 1.0, dtype=np.float32),
        epscol=np.full((128, 1), 1.0e-5, dtype=np.float32),
        eps24col=np.full((128, 1), 1.0e-24, dtype=np.float32),
    )


class SemC:
    __slots__ = ("h", "val")
    def __init__(self, h):
        self.h = h
        self.val = 0


ENGS = ("sync", "gpsimd", "scalar", "vector", "tensor")


class Prog:
    def __init__(self):
        self.ops = {e: [] for e in ENGS}
        self.waited = {}
        self.esem = {}
        self.bank_free = [(None, 0)] * 8
        self.bank_ready = [(None, 0)] * 8

    def do(self, eng, fn, inc=None, n=1):
        if eng in ("vector", "scalar") or (eng == "gpsimd" and n != 16):
            es = self.esem[eng]
            assert inc is None or inc is es
            self.wait(eng, es, es.val)
            inc, n = es, 1
        if inc is None:
            self.ops[eng].append(fn)
            return None
        inc.val += n
        h, v = inc.h, inc.val
        self.ops[eng].append(lambda e, fn=fn, h=h, n=n: fn(e).then_inc(h, n))
        return v

    def wait(self, eng, sem, val):
        if sem is None or val is None or val <= 0:
            return
        key = (eng, id(sem))
        if self.waited.get(key, 0) >= val:
            return
        self.waited[key] = val
        h = sem.h
        self.ops[eng].append(lambda e, h=h, val=val: e.wait_ge(h, val))

    def wait_t(self, eng, tok):
        if tok is not None:
            self.wait(eng, tok[0], tok[1])

    def flush(self, block):
        for eng in ENGS:
            ops = self.ops[eng]
            if not ops:
                continue
            def body(e, ops=ops):
                for fn in ops:
                    fn(e)
            getattr(block, eng)(body)


def build_program(depth=DEPTH, mode="full"):
    nc = bass.Bass("TRN2", target_bir_lowering=False, debug=False)

    def din(name, shape, dt=F32):
        return nc.dram_tensor(name, list(shape), dt, kind="ExternalInput").ap()

    x0 = din("x0", [T, D])
    wq_d = din("wq", [D, D]); wk_d = din("wk", [D, D]); wv_d = din("wv", [D, D])
    wqkv_d = din("wqkv", [DEPTH, D, 3 * D])
    qsc_d = din("qsc", [DEPTH, DH]); ksc_d = din("ksc", [DEPTH, DH])
    wo_d = din("wo", [DEPTH, D, D])
    lng_d = din("lng", [DEPTH, D]); lnb_d = din("lnb", [DEPTH, D])
    w1_d = din("w1", [DEPTH, D, 2 * DI]); b1_d = din("b1", [DEPTH, 2 * DI])
    w2_d = din("w2", [DEPTH, DI, D]); b2_d = din("b2", [DEPTH, D])
    tbl_d = din("tables", [4, T, HGH * DH], BF16)
    masks_d = din("masks", [128, 1536])
    masksb_d = din("masksb", [128, 1536], BF16)
    identF_d = din("identF", [128, 128])
    identB_d = din("identB", [128, 128], BF16)
    c1024_d = din("c1024", [128, 1], BF16)
    row8_d = din("row8", [1, 128]); row1_d = din("row1", [1, 128])
    eps_d = din("epscol", [128, 1])
    eps24_d = din("eps24col", [128, 1])

    y_d = nc.dram_tensor("y", [T, D], F32, kind="ExternalOutput").ap()
    x1s_d = nc.dram_tensor("x1s", [T, D], F32, kind="Internal").ap()
    rstash_d = nc.dram_tensor("rstash", [128, DC * T], F32, kind="Internal").ap()
    dump = mode.startswith("L") or mode.startswith("A")
    attn_only = mode.startswith("A")
    nlayers = depth if mode in ("full",) else (0 if mode == "outer" else int(mode[1:]) + 1)
    if dump:
        ydbg_d = nc.dram_tensor("ydbg", [128, DC * T], F32, kind="ExternalOutput").ap()

    with ExitStack() as ctx:
        A32 = ctx.enter_context(nc.sbuf_tensor("A32", [128, 24064], F32))
        A16 = ctx.enter_context(nc.sbuf_tensor("A16", [128, 50816], BF16))
        sb_idF = ctx.enter_context(nc.sbuf_tensor("sb_idF", [128, 128], F32))
        sb_idB = ctx.enter_context(nc.sbuf_tensor("sb_idB", [128, 128], BF16))
        sb_c1024 = ctx.enter_context(nc.sbuf_tensor("sb_c1024", [128, 1], BF16))
        sb_row8 = ctx.enter_context(nc.sbuf_tensor("sb_row8", [1, 128], F32))
        sb_row1 = ctx.enter_context(nc.sbuf_tensor("sb_row1", [1, 128], F32))
        sb_eps = ctx.enter_context(nc.sbuf_tensor("sb_eps", [128, 1], F32))
        sb_eps24 = ctx.enter_context(nc.sbuf_tensor("sb_eps24", [128, 1], F32))
        sb_mskb = ctx.enter_context(nc.sbuf_tensor("sb_mskb", [128, 1536], BF16))
        PS = ctx.enter_context(nc.psum_tensor("PS", [128, 4096], F32))

        R_RES = 0
        R_MASK = 16384
        R_A = 17920
        R_SM = 22016
        R_W = 0
        R_KT = 12288
        R_AT = 16384
        R_TAB = 24576
        R_V = 32768
        R_XS = 34816
        R_QW = 36864
        R_QT2 = 38912
        R_EP = 40960
        R_WO = 43264
        R_SP = 47360
        sm = {}
        _p = [R_SM]
        for nm, c in [("ssq", 32), ("ssk", 16), ("negmax", 8), ("sumexp", 8),
                      ("recip", 8), ("murow", 512), ("rstdrow", 512),
                      ("qsrow", 512), ("gcol", 8), ("bcol", 8), ("b2col", 8),
                      ("b1a", 24), ("b1g", 24), ("t0", 8), ("t1", 8), ("t2", 8),
                      ("bnd", 8)]:
            sm[nm] = _p[0]; _p[0] += c
        assert _p[0] <= R_SM + 2048

        def a32(base, cols, pr=128, po=0):
            return A32[po:po + pr, base:base + cols]

        def a16(base, cols, pr=128, po=0):
            return A16[po:po + pr, base:base + cols]

        def bank(b, cols=512, off=0, pr=128, po=0):
            return PS[po:po + pr, b * 512 + off:b * 512 + off + cols]

        sQ = SemC(ctx.enter_context(nc.semaphore("sQ")))
        qC = SemC(ctx.enter_context(nc.semaphore("qC")))
        qX0 = [SemC(ctx.enter_context(nc.semaphore("qX0a"))),
               SemC(ctx.enter_context(nc.semaphore("qX0b")))]
        qST = SemC(ctx.enter_context(nc.semaphore("qST")))
        sG = SemC(ctx.enter_context(nc.semaphore("sG")))
        sT = SemC(ctx.enter_context(nc.semaphore("sT")))
        sV = SemC(ctx.enter_context(nc.semaphore("sV")))
        sS = SemC(ctx.enter_context(nc.semaphore("sS")))
        sO = SemC(ctx.enter_context(nc.semaphore("sO")))
        sGC = SemC(ctx.enter_context(nc.semaphore("sGC")))

        block = ctx.enter_context(nc.Block())
        P = Prog()
        P.esem = {"vector": sV, "scalar": sS, "gpsimd": sGC}

        def mm_acc(b, cols, pairs, off=0, opr=128, opo=0):
            P.wait_t("tensor", P.bank_free[b])
            n = len(pairs)
            for i, (l, r) in enumerate(pairs):
                P.do("tensor",
                     lambda e, o=bank(b, cols, off, opr, opo), l=l, r=r,
                     st=(i == 0), sp=(i == n - 1):
                     e.matmul(o, l, r, start=st, stop=sp),
                     sT if i == n - 1 else None, 1)
            P.bank_ready[b] = (sT, sT.val)
            return sT.val

        def bank_read_done(b, sem, val):
            P.bank_free[b] = (sem, val)

        for dst, src in [(sb_idF[:], identF_d), (sb_idB[:], identB_d),
                         (sb_c1024[:], c1024_d), (sb_row8[:], row8_d),
                         (sb_row1[:], row1_d), (sb_eps[:], eps_d),
                         (sb_eps24[:], eps24_d),
                         (sb_mskb[:], masksb_d), (a32(R_MASK, 1536), masks_d)]:
            P.do("sync", lambda e, d=dst, s=src: e.dma_start(out=d, in_=s), qC, 16)
        consts_q = qC.val

        OX0T = 0
        OWB = 16384
        OKT = 24576
        OSTG = 40960
        OQR = 16384
        OVR = 0
        OE = 18432
        OPT = 20480
        O_X0R = [R_A, R_A + 1024]
        O_ST = R_A + 2048
        O_X1T = R_A + 3072

        qts_d = nc.dram_tensor("qts", [NT, 128, D], BF16, kind="Internal").ap()
        vs_d = nc.dram_tensor("vs", [NT, 128, D], BF16, kind="Internal").ap()
        qQS = [SemC(ctx.enter_context(nc.semaphore("qQSa"))),
               SemC(ctx.enter_context(nc.semaphore("qQSb")))]
        qQR = [SemC(ctx.enter_context(nc.semaphore("qQRa"))),
               SemC(ctx.enter_context(nc.semaphore("qQRb")))]
        qVR = [SemC(ctx.enter_context(nc.semaphore(f"qVR{i}"))) for i in range(4)]

        x0_ld = []
        P.wait("tensor", qC, consts_q)
        ring_free = [None, None]
        for tt in range(NT):
            P.wait_t("sync", ring_free[tt % 2])
            dst = a32(O_X0R[tt % 2], 1024)
            src = x0[tt * 128:(tt + 1) * 128, :]
            x0_ld.append(P.do("sync", lambda e, d=dst, s=src: e.dma_start(out=d, in_=s),
                              qX0[tt % 2], 16))
            P.wait("tensor", qX0[tt % 2], x0_ld[tt])
            P.wait_t("tensor", ring_free[tt % 2])
            for dcg in range(2):
                b = dcg
                P.wait_t("tensor", P.bank_free[b])
                for k in range(4):
                    dc = dcg * 4 + k
                    src2 = a32(O_X0R[tt % 2], 1024)[:, dc * 128:dc * 128 + 128]
                    P.do("tensor",
                         lambda e, o=bank(b, 128, off=k * 128), s=src2:
                         e.matmul(o, s, sb_idF[:], start=True, stop=True),
                         sT if k == 3 else None, 1)
                mmv = sT.val
                P.wait("scalar", sT, mmv)
                last = None
                for k in range(4):
                    dc = dcg * 4 + k
                    dst2 = a16(OX0T + dc * T + tt * 128, 128)
                    last = P.do("scalar",
                                lambda e, d=dst2, s=bank(b, 128, off=k * 128):
                                e.copy(d, s), sS, 1)
                bank_read_done(b, sS, last)
            ring_free[tt % 2] = (sS, sS.val)
        x0T_done = (sS, sS.val)

        def wload(w_ap, prev_consumed=None):
            if prev_consumed is not None:
                P.wait_t("gpsimd", prev_consumed)
            for dc in range(DC):
                src = w_ap[dc * 128:(dc + 1) * 128, :]
                dst = a16(OWB + dc * 1024, 1024)
                P.do("gpsimd", lambda e, d=dst, s=src: e.dma_start(out=d, in_=s), sG, 16)
            return sG.val

        wld = wload(wk_d)
        P.wait("tensor", sG, wld)
        P.wait_t("tensor", x0T_done)
        for mc in range(DC):
            for tg in range(4):
                b = 2 + ((mc * 4 + tg) % 2)
                pairs = [(a16(OWB + dc * 1024 + mc * 128, 128),
                          a16(OX0T + dc * T + tg * 512, 512)) for dc in range(DC)]
                mmv = mm_acc(b, 512, pairs)
                P.wait("scalar", sT, mmv)
                ev = P.do("scalar", lambda e, d=a16(OKT + mc * T + tg * 512, 512),
                          s=bank(b): e.copy(d, s), sS, 1)
                bank_read_done(b, sS, ev)
        kt_done = sS.val

        stg_free = [None, None]

        def proj_to_hbm(which, wld_mark):
            P.wait("tensor", sG, wld_mark)
            n_out = NT * 2 if which == "v" else DC * 4
            for i in range(n_out):
                b = 2 + (i % 2)
                sl = i % 2
                if which == "v":
                    tt, vg = i // 2, i % 2
                    tt, vg = divmod(i, 2)
                    pairs = [(a16(OX0T + dc * T + tt * 128, 128),
                              a16(OWB + dc * 1024 + vg * 512, 512)) for dc in range(DC)]
                else:
                    mc, tg = divmod(i, 4)
                    pairs = [(a16(OWB + dc * 1024 + mc * 128, 128),
                              a16(OX0T + dc * T + tg * 512, 512)) for dc in range(DC)]
                mmv = mm_acc(b, 512, pairs)
                P.wait("scalar", sT, mmv)
                P.wait_t("scalar", stg_free[sl])
                ev = P.do("scalar", lambda e, d=a16(OSTG + sl * 512, 512), s=bank(b):
                          e.copy(d, s), sS, 1)
                bank_read_done(b, sS, ev)
                P.wait("sync", sS, ev)
                if which == "v":
                    dd = vs_d[tt, :, vg * 512:(vg + 1) * 512]
                else:
                    mc, tg = divmod(i, 4)
                    dd = qts_d[tg * 4:(tg + 1) * 4, :, mc * 128:(mc + 1) * 128]                         .rearrange("q p c -> p q c")
                dmam = P.do("sync", lambda e, d=dd, s=a16(OSTG + sl * 512, 512)
                            .rearrange("p (q c) -> p q c", q=4 if which == "q" else 1):
                            e.dma_start(out=d, in_=s), qQS[sl], 16)
                stg_free[sl] = (qQS[sl], dmam)
            return (qQS[0].val, qQS[1].val)

        wld = wload(wv_d, prev_consumed=(sS, kt_done))
        v_done = proj_to_hbm("v", wld)
        wld = wload(wq_d, prev_consumed=(sS, sS.val))
        q_done = proj_to_hbm("q", wld)

        x1_qdma = []
        qr_free = [None, None]
        vloads = {}
        P.wait("sync", qQS[0], q_done[0])
        P.wait("sync", qQS[1], q_done[1])
        for qt in range(NT):
            nk = (qt + 1) * 128
            njg = (nk + 511) // 512
            xd = P.do("sync", lambda e, d=a32(O_X0R[0], 1024),
                      s=x0[qt * 128:(qt + 1) * 128, :]: e.dma_start(out=d, in_=s),
                      qX0[0], 16)
            qsl = qt % 2
            P.wait_t("sync", qr_free[qsl])
            qd = P.do("sync", lambda e, d=a16(OQR + qsl * 1024, 1024),
                      s=qts_d[qt]: e.dma_start(out=d, in_=s), qQR[qsl], 16)
            qs4 = qVR[qt % 4]
            P.wait("sync", qs4, qs4.val)
            vm = P.do("sync", lambda e, d=a16(OVR + qt * 1024, 1024),
                      s=vs_d[qt]: e.dma_start(out=d, in_=s), qs4, 16)
            vloads[qt] = (qs4, vm)
            P.wait("tensor", qQR[qsl], qd)
            for jg in range(njg):
                w_ = min(512, nk - jg * 512)
                pairs = [(a16(OQR + qsl * 1024 + mc * 128, 128),
                          a16(OKT + mc * T + jg * 512, w_)) for mc in range(DC)]
                mm_acc(jg, w_, pairs)
            s_mm = sT.val
            P.wait("vector", sT, s_mm)
            dsl = PS[:, qt * 128:qt * 128 + 128]
            mski = P.do("vector", lambda e, d=dsl: e.tensor_tensor(
                d, d, a32(R_MASK + MTRI, 128), ALU.add), sV, 1)
            P.wait("vector", sV, mski)
            nm = a32(sm["negmax"], 1)
            P.do("vector", lambda e, o=nm, n=nk: e.tensor_reduce(
                o, PS[:, 0:n], AX.X, ALU.max, negate=True), sV, 1)
            red_v = sV.val
            P.wait("scalar", sV, red_v)
            nm32 = a32(sm["t0"], 1)
            nm32_s = P.do("scalar", lambda e, o=nm32, i=nm: e.activation(
                o, i, AF.Copy, scale=1.0 / 32.0), sS, 1)
            P.wait("scalar", sS, nm32_s)
            sxp = a32(sm["sumexp"], 1)
            exp_s = P.do("scalar", lambda e, o=a16(OE, nk), n=nk, b=nm32, sx=sxp:
                         e.activation(o, PS[:, 0:n], AF.Exp, bias=b, scale=1.0 / 32.0,
                                      accum_out=sx), sS, 1)
            for b_ in range(njg):
                bank_read_done(b_, sS, exp_s)
            P.wait("vector", sS, exp_s)
            rc = a32(sm["recip"], 1)
            P.do("vector", lambda e, o=rc, i=sxp: e.reciprocal(o, i), sV, 1)
            rcv = sV.val
            P.wait("vector", sV, rcv)
            pn = P.do("vector", lambda e, o=a16(OE, nk), i=a16(OE, nk), r=rc:
                      e.tensor_scalar(o, i, r, None, ALU.mult), sV, 1)
            P.wait("tensor", sV, pn)
            for jc in range(qt + 1):
                b = 4 + ((jc // 2) % 2)
                off = (jc % 2) * 128
                if jc % 2 == 0:
                    P.wait_t("tensor", P.bank_free[b])
                mm = P.do("tensor", lambda e, o=bank(b, 128, off=off),
                          l=a16(OE + jc * 128, 128):
                          e.matmul(o, l, sb_idB[:], start=True, stop=True), sT, 1)
                if jc % 2 == 1 or jc == qt:
                    P.wait("scalar", sT, mm)
                    wdt = 128 * (2 if jc % 2 == 1 else 1)
                    o0 = (jc // 2) * 256
                    ev = P.do("scalar", lambda e, d=a16(OPT + o0, wdt),
                              s=bank(b, wdt): e.copy(d, s), sS, 1)
                    bank_read_done(b, sS, ev)
            pt_ev = sS.val
            P.wait("tensor", sS, pt_ev)
            for jc in range(qt + 1):
                P.wait_t("tensor", vloads[jc])
            for dg in range(2):
                pairs = [(a16(OPT + jc * 128, 128),
                          a16(OVR + jc * 1024 + dg * 512, 512))
                         for jc in range(qt + 1)]
                mm_acc(6 + dg, 512, pairs)
            pv_mm = sT.val
            qr_free[qsl] = (sT, s_mm)
            P.wait("vector", sT, pv_mm)
            P.wait("vector", qX0[0], xd)
            for dg in range(2):
                ev = P.do("vector", lambda e, o=a32(O_ST + dg * 512, 512),
                          p_=bank(6 + dg), xs=a32(O_X0R[0] + dg * 512, 512):
                          e.tensor_tensor(o, p_, xs, ALU.add), sV, 1)
                bank_read_done(6 + dg, sV, ev)
            st_v = sV.val
            P.wait("vector", sV, st_v)
            stile = a32(O_ST, 1024)
            negmu = a32(sm["t1"], 1)
            P.do("vector", lambda e, o=negmu, i=stile: e.tensor_reduce(
                o, i, AX.X, ALU.add, negate=True), sV, 1)
            P.wait("vector", sV, sV.val)
            P.do("vector", lambda e, o=negmu, i=negmu: e.tensor_scalar(
                o, i, 1.0 / 1024.0, None, ALU.mult), sV, 1)
            mu_v = sV.val
            P.wait("scalar", sV, mu_v)
            ssq = a32(sm["t2"], 1)
            P.do("scalar", lambda e, o=a32(O_X1T, 1024), i=stile, b=negmu, sx=ssq:
                 e.activation(o, i, AF.Square, bias=b, accum_out=sx), sS, 1)
            P.do("scalar", lambda e, o=ssq, i=ssq: e.activation(
                o, i, AF.Ln, scale=1.0 / 1024.0, bias=sb_eps[:]), sS, 1)
            P.do("scalar", lambda e, o=ssq, i=ssq: e.activation(
                o, i, AF.Exp, scale=-0.5), sS, 1)
            rstd_s = sS.val
            P.wait("vector", sS, rstd_s)
            P.do("vector", lambda e, o=a32(sm["t0"], 1), i=negmu, j=ssq:
                 e.tensor_tensor(o, i, j, ALU.mult), sV, 1)
            m2_v = sV.val
            P.wait("vector", sV, m2_v)
            x1v = P.do("vector", lambda e, o=a32(O_X1T, 1024), i=stile, r=ssq,
                       m2=a32(sm["t0"], 1):
                       e.tensor_scalar(o, i, r, m2, ALU.mult, ALU.add), sV, 1)
            P.wait("sync", sV, x1v)
            x1_qdma.append(P.do("sync", lambda e, s=a32(O_X1T, 1024),
                                d=x1s_d[qt * 128:(qt + 1) * 128, :]:
                                e.dma_start(out=d, in_=s), qST, 16))
            if mode == "outer":
                P.do("sync", lambda e, s=a32(O_X1T, 1024),
                     d=y_d[qt * 128:(qt + 1) * 128, :]: e.dma_start(out=d, in_=s),
                     sO, 16)
            P.wait("tensor", sV, x1v)
            for dcg in range(2):
                b = dcg
                P.wait_t("tensor", P.bank_free[b])
                for k in range(4):
                    dc = dcg * 4 + k
                    src = a32(O_X1T, 1024)[:, dc * 128:dc * 128 + 128]
                    P.do("tensor", lambda e, o=bank(b, 128, off=k * 128), s=src:
                         e.matmul(o, s, sb_idF[:], start=True, stop=True),
                         sT if k == 3 else None, 1)
                mmv = sT.val
                P.wait("vector", sT, mmv)
                ev = None
                for k in range(4):
                    dc = dcg * 4 + k
                    dst = a32(R_RES + dc * T + qt * 128, 128)
                    ev = P.do("vector", lambda e, d=dst, s=bank(b, 128, off=k * 128):
                              e.tensor_copy(d, s), sV, 1)
                bank_read_done(b, sV, ev)
            P.wait("sync", sV, st_v)
            P.wait("scalar", qST, qST.val)
            P.wait("scalar", sO, sO.val)
            P.wait("scalar", sT, sT.val)
        outer_done_v = sV.val
        outer_done_s = sS.val
        outer_done_t = sT.val

        QS8B, KS8B = R_A + 2048, R_A + 3072
        TMPA = R_A
        XQW, XKW = R_QW, R_QW + 1024
        XQTR = [R_QT2, R_QT2 + 1024]
        EP_E = [R_EP, R_EP + 768, R_EP + 1536]
        EP_PT = R_SP + 2048
        MID = [R_W + i * 1024 for i in range(12)] + \
              [R_KT + i * 1024 for i in range(4)] + \
              [R_QT2 + i * 1024 for i in range(2)] + \
              [R_TAB + i * 1024 for i in range(2)] + \
              [R_SP + i * 1024 for i in range(2)]
        W1RING = [R_TAB + 2048, R_TAB + 4096]
        W2RING = [R_TAB + 6144, R_TAB + 7168, R_EP, R_EP + 1024]
        XSF = R_XS

        qWQ = SemC(ctx.enter_context(nc.semaphore("qWQ")))
        qTB = [SemC(ctx.enter_context(nc.semaphore(f"qTB{i}"))) for i in range(2)]
        qWO = SemC(ctx.enter_context(nc.semaphore("qWO")))
        qW1 = [SemC(ctx.enter_context(nc.semaphore(f"qW1{i}"))) for i in range(2)]
        qW2 = [SemC(ctx.enter_context(nc.semaphore(f"qW2{i}"))) for i in range(4)]
        qMS = SemC(ctx.enter_context(nc.semaphore("qMS")))
        qRS = SemC(ctx.enter_context(nc.semaphore("qRS")))
        qSC = SemC(ctx.enter_context(nc.semaphore("qSC")))
        qXS = SemC(ctx.enter_context(nc.semaphore("qXS")))

        state = dict(tab_free=[None, None], w1_free=[None, None],
                     w2_free=[None] * 4,
                     v_free=[None] * 4, smalls_free=None, res_tok=None, wo_free=None,
                     sq_free=[None] * 3, b_free=[None] * 3, chain_seq=[0])
        SQP = [R_A, R_A + 512, R_MASK]
        BP = [R_A + 1024, R_A + 1536, R_MASK + 512]

        def emit_inner(ell):
            if ell == 0:
                for eng in ("gpsimd", "sync", "tensor", "vector", "scalar"):
                    P.wait(eng, sT, outer_done_t)
                    P.wait(eng, sV, outer_done_v)
                    P.wait(eng, sS, outer_done_s)
                P.wait("sync", qST, qST.val)

            P.wait_t("sync", state["smalls_free"])
            qs_rep = qsc_d[ell][None, None, :].broadcast_to([128, HGH, DH])
            ks_rep = ksc_d[ell][None, None, :].broadcast_to([128, HGH, DH])
            P.wait("sync", qSC, qSC.val)
            for dst, srcp in [
                (a32(QS8B, 512).rearrange("p (a b) -> p a b", a=HGH), qs_rep),
                (a32(KS8B, 512).rearrange("p (a b) -> p a b", a=HGH), ks_rep),
            ]:
                P.do("sync", lambda e, d=dst, s=srcp: e.dma_start(out=d, in_=s), qSC, 16)
            sc_ld = qSC.val
            P.wait("vector", qSC, qSC.val)
            P.do("vector", lambda e, o=a32(QS8B, 512), i=a32(QS8B, 512):
                 e.tensor_scalar(o, i, 8.0, None, ALU.mult))
            qs8_v = sV.val
            P.wait_t("sync", state["res_tok"])
            colspecs = [
                (sm["gcol"], 8, lng_d[ell].rearrange("(c p) -> c p", p=128), None),
                (sm["bcol"], 8, lnb_d[ell].rearrange("(c p) -> c p", p=128), None),
                (sm["b2col"], 8, b2_d[ell].rearrange("(c p) -> c p", p=128), None),
                (sm["b1a"], 22, b1_d[ell, 0:2816].rearrange("(c p) -> c p", p=128), None),
                (sm["b1g"], 22, b1_d[ell, 2730:5418].rearrange("(c p) -> c p", p=128),
                 b1_d[ell, 5418:5460][None, :]),
            ]
            for dcol, ncc, srcp, tail in colspecs:
                ld = P.do("sync", lambda e, d=a32(TMPA, 128, pr=ncc if tail is None
                          else ncc - 1), s=srcp: e.dma_start(out=d, in_=s), qMS, 16)
                if tail is not None:
                    ld = P.do("sync", lambda e, d=a32(TMPA, 42, pr=1, po=ncc - 1),
                              s=tail: e.dma_start(out=d, in_=s), qMS, 16)
                P.wait("tensor", qMS, ld)
                P.wait_t("tensor", P.bank_free[5])
                mm = P.do("tensor", lambda e, o=bank(5, ncc), l=a32(TMPA, 128, pr=ncc),
                          r=sb_idF[0:ncc, 0:ncc]: e.matmul(o, l, r, start=True, stop=True),
                          sT, 1)
                P.wait("scalar", sT, mm)
                ev = P.do("scalar", lambda e, d=a32(dcol, ncc), s=bank(5, ncc):
                          e.copy(d, s))
                bank_read_done(5, sS, ev)
                P.wait("sync", sT, mm)
            smalls_ld = qMS.val
            smalls_s = sS.val

            P.wait_t("sync", state["res_tok"])
            P.wait("sync", qXS, qXS.val)
            rs = P.do("sync", lambda e, d=rstash_d, s=a32(R_RES, DC * T):
                      e.dma_start(out=d, in_=s), qRS, 16)

            qkv_mm_done = [0]
            for hg in range(2):
                P.wait_t("gpsimd", state["res_tok"])
                if hg == 1:
                    P.wait("gpsimd", sT, qkv_mm_done[0])
                for mi, cbase in enumerate((0, D, 2 * D)):
                    src = wqkv_d[ell][:, cbase + hg * 512: cbase + hg * 512 + 512] \
                        .rearrange("(c p) m -> p c m", p=128)
                    dst = a16(R_W + mi * 4096, 4096).rearrange("p (c m) -> p c m", c=DC)
                    P.do("gpsimd", lambda e, d=dst, s=src: e.dma_start(out=d, in_=s),
                         qWQ, 16)
                wqkv_ld = qWQ.val
                P.wait_t("gpsimd", state["wo_free"])
                for kc in range(4):
                    src = wo_d[ell, hg * 512 + kc * 128: hg * 512 + (kc + 1) * 128, :]
                    P.do("gpsimd", lambda e, d=a16(R_WO + kc * 1024, 1024), s=src:
                         e.dma_start(out=d, in_=s), qWO, 16)
                wo_ld = qWO.val
                if hg == 1:
                    P.wait("scalar", sT, sT.val)

                fw = [None] * NW
                qh_chains = [(0, 0), (0, 1), (1, 0), (1, 1)]

                def wave(st, mi):
                    for tt in range(2):
                        pairs = [(a16(R_XS + dc * 256 + tt * 128, 128),
                                  a16(R_W + mi * 4096 + dc * 512, 512))
                                 for dc in range(DC)]
                        st["wv"][(tt, mi)] = mm_acc(6 + tt, 512, pairs)

                def sslot(st, ci):
                    side, tt_ = qh_chains[ci]
                    if side == 0:
                        return a32(sm["ssq"] + tt_ * 16 + (st["w"] % 2) * 8, 8)
                    return a32(sm["ssk"] + tt_ * 8, 8)

                def chain_head(st, ci):
                    side, tt_ = qh_chains[ci]
                    par = state["chain_seq"][0] % 3
                    state["chain_seq"][0] += 1
                    st["cpar"][ci] = par
                    P.wait("scalar", sT, st["wv"][(tt_, side)])
                    P.wait_t("scalar", state["sq_free"][par])
                    P.do("scalar", lambda e, o=a32(SQP[par], 512), i=bank(6 + tt_):
                         e.activation(o, i, AF.Square))
                    st["st1"][ci] = sS.val

                def chain_mid(st, ci):
                    ssl = sslot(st, ci)
                    P.wait("vector", sS, st["st1"][ci])
                    P.do("vector", lambda e, o=ssl, i=a32(SQP[st["cpar"][ci]], 512)
                         .rearrange("p (h d) -> p h d", h=8):
                         e.tensor_reduce(o, i, AX.X, ALU.add))
                    st["st2"][ci] = sV.val
                    P.wait("scalar", sV, st["st2"][ci])
                    P.do("scalar", lambda e, o=ssl, i=ssl:
                         e.activation(o, i, AF.Ln, bias=sb_eps24[:]))
                    P.do("scalar", lambda e, o=ssl, i=ssl:
                         e.activation(o, i, AF.Exp, scale=-0.5))
                    st["st3"][ci] = sS.val

                def chain_tail(st, ci):
                    side, tt_ = qh_chains[ci]
                    b = 6 + tt_
                    par = st["cpar"][ci]
                    tsl = st["w"] % 2
                    cosb = R_TAB + tsl * 4096 + (0 if side == 0 else 2) * 1024 + tt_ * 512
                    sinb = R_TAB + tsl * 4096 + (1 if side == 0 else 3) * 1024 + tt_ * 512
                    dstw = (XQW if side == 0 else XKW) + tt_ * 512
                    Bp = a32(BP[par], 512)
                    P.wait_t("vector", state["b_free"][par])
                    P.wait("vector", sT, st["wv"][(tt_, side)])
                    P.wait("vector", qSC, sc_ld)
                    if side == 0:
                        P.wait("vector", sS, st["st1"][ci])
                        P.do("vector", lambda e, o=Bp, i=bank(b), j=a32(QS8B, 512):
                             e.tensor_tensor(o, i, j, ALU.mult))
                    else:
                        P.wait("vector", sS, st["st3"][ci])
                        ssl = sslot(st, ci)
                        for h in range(8):
                            P.do("vector", lambda e, o=Bp[:, h * 64:h * 64 + 64],
                                 i=bank(b, 64, off=h * 64), r=ssl[:, h:h + 1],
                                 j=a32(KS8B + h * 64, 64):
                                 e.scalar_tensor_tensor(o, i, r, j, ALU.mult, ALU.mult))
                    bank_read_done(b, sV, sV.val)
                    P.wait("vector", qTB[tsl], st["tab_ld"])
                    hv = lambda base, lo: a32(base, 512).rearrange(
                        "p (h x) -> p h x", h=8)[:, :, lo:lo + 32]
                    hv16 = lambda base, lo: a16(base, 512).rearrange(
                        "p (h x) -> p h x", h=8)[:, :, lo:lo + 32]
                    P.do("vector", lambda e, o=hv(SQP[par], 0), i=hv(BP[par], 32),
                         j=hv16(sinb, 32): e.tensor_tensor(o, i, j, ALU.mult))
                    P.do("vector", lambda e, o=hv(SQP[par], 32), i=hv(BP[par], 0),
                         j=hv16(sinb, 0): e.tensor_tensor(o, i, j, ALU.mult))
                    P.do("vector", lambda e, o=Bp, i=Bp, j=a16(cosb, 512):
                         e.tensor_tensor(o, i, j, ALU.mult))
                    P.do("vector", lambda e, o=a16(dstw, 512), i=Bp,
                         j=a32(SQP[par], 512): e.tensor_tensor(o, i, j, ALU.add))
                    st["st7"][ci] = sV.val
                    state["sq_free"][par] = (sV, sV.val)
                    state["b_free"][par] = (sV, sV.val)

                def qk_transpose(st, side, bnk, ready_v):
                    w = st["w"]
                    P.wait("tensor", sV, ready_v)
                    for f0 in (0, 2):
                        P.wait_t("tensor", P.bank_free[bnk])
                        for k2, (fp, tt_) in enumerate(
                                [(f0, 0), (f0, 1), (f0 + 1, 0), (f0 + 1, 1)]):
                            srcb = (XQW if side == 0 else XKW) + tt_ * 512
                            P.do("tensor", lambda e, o=bank(bnk, 128, off=k2 * 128),
                                 l=a16(srcb + fp * 128, 128):
                                 e.matmul(o, l, sb_idB[:], start=True, stop=True),
                                 sT if k2 == 3 else None, 1)
                        mmv = sT.val
                        P.wait("scalar", sT, mmv)
                        if side == 0:
                            dsts = [a16(XQTR[w % 2] + f0 * 256, 512)]
                        else:
                            kt3 = A16[:, R_KT:R_KT + 4096].rearrange(
                                "p (f x) -> p f x", f=4)
                            dsts = [kt3[:, f0:f0 + 2,
                                        (w % 2) * 512 + 256:(w % 2) * 512 + 512],
                                    kt3[:, f0:f0 + 2,
                                        ((w + 1) % 2) * 512:((w + 1) % 2) * 512 + 256]]
                        ev = None
                        for dst in dsts:
                            ev = P.do("scalar", lambda e, d=dst, s=bank(bnk, 512):
                                      e.copy(d, s))
                        bank_read_done(bnk, sS, ev)

                def front_p1a(w):
                    st = {"w": w, "wv": {}, "cpar": {}, "st1": {}, "st2": {},
                          "st3": {}, "st7": {}}
                    fw[w] = st
                    tsl = w % 2
                    P.wait_t("sync", state["tab_free"][tsl])
                    for ti in range(4):
                        src = tbl_d[ti, w * 256:(w + 1) * 256, :] \
                            .rearrange("(tt p) c -> p tt c", p=128)
                        dst = a16(R_TAB + tsl * 4096 + ti * 1024, 1024) \
                            .rearrange("p (tt c) -> p tt c", tt=2)
                        P.do("sync", lambda e, d=dst, s=src: e.dma_start(out=d, in_=s),
                             qTB[tsl], 16)
                    st["tab_ld"] = qTB[tsl].val
                    P.wait("gpsimd", sT, qkv_mm_done[0])
                    P.wait("gpsimd", qRS, rs)
                    src0 = rstash_d.rearrange("p (c t) -> p c t", c=DC)[
                        :, 0:4, w * 256:(w + 1) * 256]
                    dst0 = a16(R_XS, 1024).rearrange("p (c t) -> p c t", c=4)
                    P.do("gpsimd", lambda e, d=dst0, s=src0: e.dma_start(out=d, in_=s),
                         qXS, 16)
                    if w > 0:
                        src1 = rstash_d.rearrange("p (c t) -> p c t", c=DC)[
                            :, 4:8, w * 256 - 1:(w + 1) * 256 - 1]
                        dst1 = a16(R_XS + 1024, 1024).rearrange("p (c t) -> p c t", c=4)
                        P.do("gpsimd", lambda e, d=dst1, s=src1:
                             e.dma_start(out=d, in_=s), qXS, 16)
                        st["xs_v"] = 0
                    else:
                        src1 = rstash_d.rearrange("p (c t) -> p c t", c=DC)[
                            :, 4:8, 0:255]
                        dst1 = a16(R_XS + 1024, 1024).rearrange(
                            "p (c t) -> p c t", c=4)[:, :, 1:256]
                        P.do("gpsimd", lambda e, d=dst1, s=src1:
                             e.dma_start(out=d, in_=s), qXS, 16)
                        P.wait("vector", sT, qkv_mm_done[0])
                        for c4 in range(4):
                            P.do("vector", lambda e, d=a16(R_XS + 1024 + c4 * 256, 1):
                                 e.memset(d, 0.0))
                        st["xs_v"] = sV.val
                    st["xs_ld"] = qXS.val

                def front_p1b(w):
                    st = fw[w]
                    P.wait("tensor", qWQ, wqkv_ld)
                    P.wait("tensor", qXS, st["xs_ld"])
                    P.wait("tensor", sV, st["xs_v"])
                    wave(st, 0)
                    for ci in (0, 1):
                        chain_head(st, ci)
                    for ci in (0, 1):
                        chain_mid(st, ci)
                    chain_tail(st, 0)
                    chain_tail(st, 1)

                def front_p2(w):
                    st = fw[w]
                    wave(st, 1)
                    for ci in (2, 3):
                        chain_head(st, ci)
                    for ci in (2, 3):
                        chain_mid(st, ci)
                    chain_tail(st, 2)
                    chain_tail(st, 3)
                    state["tab_free"][st["w"] % 2] = (sV, sV.val)

                def front_p3(w):
                    st = fw[w]
                    wave(st, 2)
                    qkv_mm_done[0] = sT.val

                def front_p3b(w):
                    st = fw[w]
                    qk_transpose(st, 0, 6, max(st["st7"][0], st["st7"][1]))
                    qk_transpose(st, 1, 7, max(st["st7"][2], st["st7"][3]))
                    st["tr_s"] = sS.val

                def front_p4(w, pv_tok):
                    st = fw[w]
                    for tt in range(2):
                        slot = (st["w"] * 2 + tt) % 4
                        P.wait("scalar", sT, st["wv"][(tt, 2)])
                        P.wait_t("scalar", pv_tok)
                        P.wait_t("scalar", state["v_free"][slot])
                        ev = P.do("scalar", lambda e, o=a16(R_V + slot * 512, 512),
                                  i=bank(6 + tt): e.copy(o, i))
                        bank_read_done(6 + tt, sS, ev)
                    st["prep_s"] = sS.val

                def sims_window(w):
                    st = fw[w]
                    wp = w % 2
                    nj0 = 128 if w == 0 else 384
                    nj1 = 256 if w == 0 else 384
                    m0 = MTRI if w == 0 else MG0
                    m1 = MW01 if w == 0 else MG0
                    half = 0 if w == 0 else wp
                    col0 = 256 if w == 0 else 0
                    col1 = 256 if w == 0 else 128
                    jtts0 = [0] if w == 0 else [2 * w - 2, 2 * w - 1, 2 * w]
                    jtts1 = [0, 1] if w == 0 else [2 * w - 1, 2 * w, 2 * w + 1]
                    nch0, nch1 = len(jtts0), len(jtts1)
                    P.wait("tensor", sS, st["tr_s"])

                    def ssq_col(tt_, h):
                        return a32(sm["ssq"] + tt_ * 16 + wp * 8 + h, 1)

                    def sim_S(h):
                        pt = h // 2
                        po = (h % 2) * 64
                        par = (h % 3) * 2
                        for qh, nj, msk in ((0, nj0, m0), (1, nj1, m1)):
                            bnk = par + qh
                            P.wait_t("tensor", P.bank_free[bnk])
                            P.do("tensor", lambda e, o=bank(bnk, nj),
                                 m=sb_mskb[:, msk:msk + nj]:
                                 e.matmul(o, sb_idB[:], m, start=True, stop=False))
                        for qh, nj, cc in ((0, nj0, col0), (1, nj1, col1)):
                            bnk = par + qh
                            lhs = a16(XQTR[wp] + pt * 256 + qh * 128, 128, pr=64, po=po)
                            rhs = a16(R_KT + pt * 1024 + half * 512 + cc, nj,
                                      pr=64, po=po)
                            P.do("tensor", lambda e, o=bank(bnk, nj), l=lhs, r=rhs:
                                 e.matmul(o, l, r, start=False, stop=True), sT, 1)
                            P.bank_ready[bnk] = (sT, sT.val)
                        return sT.val

                    def sim_max(h):
                        par = (h % 3) * 2
                        P.wait("vector", sT, s_mm_of[h])
                        for qh, nj in ((0, nj0), (1, nj1)):
                            P.do("vector", lambda e,
                                 o=a32(sm["negmax"] + (h % 3) * 2 + qh, 1),
                                 i=bank(par + qh, nj):
                                 e.tensor_reduce(o, i, AX.X, ALU.max, negate=True))
                        P.wait("vector", sS, max(st["st3"][0], st["st3"][1]))
                        for qh in range(2):
                            P.do("vector", lambda e,
                                 o=a32(sm["negmax"] + (h % 3) * 2 + qh, 1),
                                 i=a32(sm["negmax"] + (h % 3) * 2 + qh, 1),
                                 r=ssq_col(qh, h):
                                 e.tensor_scalar(o, i, r, None, ALU.mult))
                        msk_v_of[h] = sV.val

                    def sim_exp(h):
                        par = (h % 3) * 2
                        P.wait("scalar", sV, msk_v_of[h])
                        if h >= 3:
                            P.wait("scalar", sT, ptmm_of[h - 3])
                        for qh, nj in ((0, nj0), (1, nj1)):
                            bnk = par + qh
                            ex = P.do("scalar", lambda e,
                                      o=a16(EP_E[h % 3] + qh * 384, nj),
                                      i=bank(bnk, nj),
                                      b=a32(sm["negmax"] + (h % 3) * 2 + qh, 1),
                                      r=ssq_col(qh, h),
                                      sx=a32(sm["sumexp"] + (h % 3) * 2 + qh, 1):
                                      e.activation(o, i, AF.Exp, bias=b, scale=r,
                                                   accum_out=sx))
                            bank_read_done(bnk, sS, ex)
                        exp_s_of[h] = sS.val

                    def sim_norm(h):
                        P.wait("vector", sS, exp_s_of[h])
                        P.do("vector", lambda e,
                             o=a32(sm["recip"] + (h % 3) * 2, 2),
                             i=a32(sm["sumexp"] + (h % 3) * 2, 2):
                             e.reciprocal(o, i))
                        for qh, nj in ((0, nj0), (1, nj1)):
                            P.do("vector", lambda e, o=a16(EP_E[h % 3] + qh * 384, nj),
                                 i=a16(EP_E[h % 3] + qh * 384, nj),
                                 r=a32(sm["recip"] + (h % 3) * 2 + qh, 1):
                                 e.tensor_scalar(o, i, r, None, ALU.mult))
                        norm_v_of[h] = sV.val

                    def sim_ptpv(h):
                        pt = h // 2
                        po = (h % 2) * 64
                        P.wait("tensor", sV, norm_v_of[h])
                        tiles = [(0, jc) for jc in range(nch0)] + \
                                [(1, jc) for jc in range(nch1)]
                        bA = (((h + 1) % 3) * 2)
                        bB = bA + 1
                        evs = []
                        for gi, gstart in enumerate(range(0, len(tiles), 4)):
                            grp = tiles[gstart:gstart + 4]
                            bnk = bA if gi == 0 else bB
                            P.wait_t("tensor", P.bank_free[bnk])
                            for k2, (qh, jc) in enumerate(grp):
                                P.do("tensor", lambda e, o=bank(bnk, 128, off=k2 * 128),
                                     l=a16(EP_E[h % 3] + qh * 384 + jc * 128, 128):
                                     e.matmul(o, l, sb_idB[:], start=True, stop=True),
                                     sT if k2 == len(grp) - 1 else None, 1)
                            evs.append((bnk, sT.val, gstart, len(grp)))
                        ptmm_of[h] = sT.val
                        for bnk, mmv, gstart, glen in evs:
                            P.wait("scalar", sT, mmv)
                            ev = P.do("scalar", lambda e,
                                      d=a16(EP_PT + gstart * 128, 128 * glen),
                                      s=bank(bnk, 128 * glen): e.copy(d, s))
                            bank_read_done(bnk, sS, ev)
                        pt_s = sS.val
                        P.wait("tensor", sS, pt_s)
                        P.wait("tensor", sS, st["prep_s"])
                        P.wait_t("tensor", P.bank_free[bA])
                        for qh, jtts in ((0, jtts0), (1, jtts1)):
                            nch = len(jtts)
                            for ji, jtt in enumerate(jtts):
                                idx = ji if qh == 0 else nch0 + ji
                                P.do("tensor", lambda e,
                                     o=bank(bA, 128, off=qh * 128, pr=64, po=po),
                                     l=a16(R_V + (jtt % 4) * 512 + h * 64, 64),
                                     r=a16(EP_PT + idx * 128, 128),
                                     st=(ji == 0), sp=(ji == nch - 1):
                                     e.matmul(o, l, r, start=st, stop=sp),
                                     sT if ji == nch - 1 else None, 1)
                        pv_mm = sT.val
                        P.wait("scalar", sT, pv_mm)
                        ev = P.do("scalar", lambda e,
                                  d=a16(R_AT + pt * 2048 + w * 256, 256, pr=64, po=po),
                                  s=bank(bA, 256, pr=64, po=po): e.copy(d, s))
                        bank_read_done(bA, sS, ev)

                    s_mm_of = {}
                    msk_v_of = {}
                    exp_s_of = {}
                    ptmm_of = {}
                    norm_v_of = {}
                    nxt = w + 1 if w + 1 < NW else None
                    for i in range(11):
                        if i < 8:
                            s_mm_of[i] = sim_S(i)
                        if 1 <= i <= 8:
                            sim_max(i - 1)
                        if i >= 3:
                            sim_ptpv(i - 3)
                        if 1 <= i <= 8:
                            sim_exp(i - 1)
                        if 2 <= i <= 9:
                            sim_norm(i - 2)
                        if nxt is not None:
                            if i == 0:
                                front_p1a(nxt)
                            elif i == 2:
                                front_p1b(nxt)
                            elif i == 4:
                                front_p2(nxt)
                            elif i == 8:
                                front_p3b(nxt)
                            elif i == 9:
                                front_p3(nxt)
                            elif i == 10:
                                front_p4(nxt, (sT, sT.val))
                    if w > 0:
                        for tt in range(2):
                            state["v_free"][((w - 1) * 2 + tt) % 4] = (sT, sT.val)

                front_p1a(0)
                front_p1b(0)
                front_p2(0)
                front_p3b(0)
                front_p3(0)
                front_p4(0, None)
                for w in range(NW):
                    sims_window(w)
                P.wait("tensor", qWO, wo_ld)
                P.wait("tensor", sS, sS.val)
                P.wait("vector", qRS, rs)
                for tg in range(4):
                    for mc in range(DC):
                        b = 5 + ((tg * DC + mc) % 2)
                        pairs = [(a16(R_WO + kc * 1024 + mc * 128, 128),
                                  a16(R_AT + kc * 2048 + tg * 512, 512))
                                 for kc in range(4)]
                        mmv = mm_acc(b, 512, pairs)
                        P.wait("vector", sT, mmv)
                        dstr = a32(R_RES + mc * T + tg * 512, 512)
                        ev = P.do("vector", lambda e, d=dstr, p_=bank(b):
                                  e.tensor_tensor(d, d, p_, ALU.add))
                        bank_read_done(b, sV, ev)
                state["wo_free"] = (sT, sT.val)
                for sl4 in range(4):
                    state["v_free"][sl4] = (sT, sT.val)
            res_after_attn = sV.val
            if attn_only and ell == nlayers - 1:
                state["res_tok"] = (sV, sV.val)
                state["smalls_free"] = (sV, sV.val)
                return

            P.wait("scalar", sV, res_after_attn)
            for c4 in range(4):
                P.do("scalar", lambda e, d=a32(sm["bnd"] + c4, 1),
                     s=a32(R_RES + (4 + c4) * T + 1023, 1): e.copy(d, s))
            bnd_s = sS.val
            for half in range(2):
                hbase = half * 1024
                for tg2 in range(2):
                    tgb = hbase + tg2 * 512
                    P.wait("scalar", sV, res_after_attn if half == 0 else sV.val)
                    for dc in range(DC):
                        dst = a16(XSF + dc * 512, 512)
                        if dc < 4:
                            P.do("scalar", lambda e, d=dst,
                                 s=a32(R_RES + dc * T + tgb, 512): e.copy(d, s))
                        elif tgb == 0:
                            P.do("vector", lambda e, d=a16(XSF + dc * 512, 1):
                                 e.memset(d, 0.0))
                            P.do("scalar", lambda e, d=a16(XSF + dc * 512 + 1, 511),
                                 s=a32(R_RES + dc * T, 511): e.copy(d, s))
                        elif tgb == 1024:
                            P.wait("scalar", sS, bnd_s)
                            P.do("scalar", lambda e, d=a16(XSF + dc * 512, 1),
                                 s=a32(sm["bnd"] + dc - 4, 1): e.copy(d, s))
                            P.do("scalar", lambda e, d=a16(XSF + dc * 512 + 1, 511),
                                 s=a32(R_RES + dc * T + tgb, 511): e.copy(d, s))
                        else:
                            P.do("scalar", lambda e, d=dst,
                                 s=a32(R_RES + dc * T + tgb - 1, 512): e.copy(d, s))
                    xs_s = sS.val
                    xs_v = sV.val
                    P.wait("tensor", sS, xs_s)
                    P.wait("tensor", sV, xs_v)
                    pairs = [(sb_c1024[:], a16(XSF + dc * 512, 512)) for dc in range(DC)]
                    mu_mm = mm_acc(6, 512, pairs, opr=1)
                    P.wait("vector", sT, mu_mm)
                    mu_c = P.do("vector", lambda e, o=a32(sm["murow"], 512, pr=1),
                                i=bank(6, 512, pr=1): e.tensor_copy(o, i))
                    bank_read_done(6, sV, mu_c)
                    P.wait_t("tensor", P.bank_free[7])
                    P.wait("tensor", sV, mu_c)
                    P.do("tensor", lambda e, o=bank(7, 512),
                         l=sb_row1[:], r=a32(sm["murow"], 512, pr=1):
                         e.matmul(o, l, r, start=True, stop=True), sT, 1)
                    mub_mm = sT.val
                    P.wait("vector", sT, mub_mm)
                    for dc in range(DC):
                        P.do("vector", lambda e, o=a32(R_A + dc * 512, 512),
                             i=a16(XSF + dc * 512, 512), j=bank(7, 512):
                             e.tensor_tensor(o, i, j, ALU.subtract))
                    p1_v = sV.val
                    bank_read_done(7, sV, p1_v)
                    P.wait("scalar", sV, p1_v)
                    P.wait_t("tensor", P.bank_free[6])
                    P.wait("scalar", sT, sT.val)
                    for dc in range(DC):
                        P.do("scalar", lambda e, o=a16(R_W + dc * 512, 512),
                             i=a32(R_A + dc * 512, 512): e.activation(o, i, AF.Square))
                    sq_s = sS.val
                    P.wait("tensor", sS, sq_s)
                    for dc in range(DC):
                        P.do("tensor", lambda e, o=bank(6, 512, pr=1),
                             l=sb_c1024[:], r=a16(R_W + dc * 512, 512),
                             st=(dc == 0), sp=(dc == DC - 1):
                             e.matmul(o, l, r, start=st, stop=sp),
                             sT if dc == DC - 1 else None, 1)
                    var_mm = sT.val
                    P.wait("vector", sT, var_mm)
                    vr_c = P.do("vector", lambda e, o=a32(sm["rstdrow"], 512, pr=1),
                                i=bank(6, 512, pr=1): e.tensor_copy(o, i))
                    bank_read_done(6, sV, vr_c)
                    P.wait("scalar", sV, vr_c)
                    P.do("scalar", lambda e, o=a32(sm["rstdrow"], 512, pr=1),
                         i=a32(sm["rstdrow"], 512, pr=1):
                         e.activation(o, i, AF.Ln, bias=sb_eps[0:1, :]))
                    rst_s = P.do("scalar", lambda e, o=a32(sm["rstdrow"], 512, pr=1),
                                 i=a32(sm["rstdrow"], 512, pr=1):
                                 e.activation(o, i, AF.Exp, scale=-0.5))
                    P.wait_t("tensor", P.bank_free[7])
                    P.wait("tensor", sS, rst_s)
                    P.do("tensor", lambda e, o=bank(7, 512),
                         l=sb_row1[:], r=a32(sm["rstdrow"], 512, pr=1):
                         e.matmul(o, l, r, start=True, stop=True), sT, 1)
                    ab_mm = sT.val
                    P.wait("vector", sT, ab_mm)
                    P.wait("vector", sS, smalls_s)
                    for dc in range(DC):
                        P.do("vector", lambda e, o=a32(R_A + dc * 512, 512),
                             i=a32(R_A + dc * 512, 512), j=bank(7, 512):
                             e.tensor_tensor(o, i, j, ALU.mult))
                        P.do("vector", lambda e, o=a16(R_AT + dc * 1024 + tg2 * 512, 512),
                             i=a32(R_A + dc * 512, 512), g=a32(sm["gcol"] + dc, 1),
                             bb=a32(sm["bcol"] + dc, 1):
                             e.tensor_scalar(o, i, g, bb, ALU.mult, ALU.add))
                    xn_v = sV.val
                    bank_read_done(7, sV, xn_v)
                xn_done = sV.val

                P.wait("gpsimd", sV, xn_done)
                for ci in range(NKC):
                    wci = 128 if ci < NKC - 1 else DI - 128 * (NKC - 1)
                    slot = ci % 2
                    P.wait_t("gpsimd", state["w1_free"][slot])
                    for gside in range(2):
                        c0 = gside * DI + ci * 128
                        src = w1_d[ell][:, c0:c0 + wci] \
                            .rearrange("(c p) m -> p c m", p=128)
                        dst = a16(W1RING[slot] + gside * 1024, DC * wci) \
                            .rearrange("p (c m) -> p c m", c=DC)
                        P.do("gpsimd", lambda e, d=dst, s=src: e.dma_start(out=d, in_=s),
                             qW1[slot], 16)
                    w1_ld = qW1[slot].val
                    P.wait("tensor", qW1[slot], w1_ld)
                    P.wait("tensor", sV, xn_done)
                    for tg2 in range(2):
                        for gside in range(2):
                            pairs = [(a16(W1RING[slot] + gside * 1024 + dc * wci, wci),
                                      a16(R_AT + dc * 1024 + tg2 * 512, 512))
                                     for dc in range(DC)]
                            mm_acc((ci % 2) * 4 + tg2 * 2 + gside, 512, pairs, opr=wci)
                    w1_mm = sT.val
                    state["w1_free"][slot] = (sT, w1_mm)
                    for tg2 in range(2):
                        ba, bg = (ci % 2) * 4 + tg2 * 2, (ci % 2) * 4 + tg2 * 2 + 1
                        P.wait("scalar", sT, w1_mm)
                        P.wait("scalar", sV, sV.val)
                        gl = P.do("scalar", lambda e, o=a16(R_V + 512, 512, pr=wci),
                                  i=bank(bg, 512, pr=wci),
                                  bb=a32(sm["b1g"] + ci, 1, pr=wci):
                                  e.activation(o, i, AF.Gelu, bias=bb))
                        bank_read_done(bg, sS, gl)
                        P.wait("vector", sS, gl)
                        mv = P.do("vector", lambda e,
                                  o=a16(MID[ci] + tg2 * 512, 512, pr=wci),
                                  i=bank(ba, 512, pr=wci),
                                  bb=a32(sm["b1a"] + ci, 1, pr=wci),
                                  g=a16(R_V + 512, 512, pr=wci):
                                  e.scalar_tensor_tensor(o, i, bb, g, ALU.add, ALU.mult))
                        bank_read_done(ba, sV, mv)
                mid_v = sV.val
                mid_s = sS.val

                for tg2 in range(2):
                    tgb = hbase + tg2 * 512
                    w2_done = [None] * DC
                    for kc in range(NKC):
                        wkc = 128 if kc < NKC - 1 else DI - 128 * (NKC - 1)
                        slot = kc % 4
                        P.wait_t("gpsimd", state["w2_free"][slot])
                        src = w2_d[ell, kc * 128:kc * 128 + wkc, :]
                        wd = P.do("gpsimd", lambda e, d=a16(W2RING[slot], 1024, pr=wkc),
                                  s=src: e.dma_start(out=d, in_=s), qW2[slot], 16)
                        P.wait("tensor", qW2[slot], wd)
                        if kc == 0:
                            P.wait("tensor", sV, mid_v)
                            P.wait("tensor", sS, mid_s)
                            for b2_ in range(8):
                                P.wait_t("tensor", P.bank_free[b2_])
                        for mc in range(DC):
                            tok = P.do("tensor", lambda e, o=bank(mc, 512),
                                       l=a16(W2RING[slot] + mc * 128, 128, pr=wkc),
                                       r=a16(MID[kc] + tg2 * 512, 512, pr=wkc),
                                       st=(kc == 0), sp=(kc == NKC - 1):
                                       e.matmul(o, l, r, start=st, stop=sp),
                                       sT if (mc == DC - 1 or kc == NKC - 1) else None, 1)
                            if kc == NKC - 1:
                                w2_done[mc] = tok
                        state["w2_free"][slot] = (sT, sT.val)
                    for mc in range(DC):
                        P.wait("vector", sT, w2_done[mc])
                        dstr = a32(R_RES + mc * T + tgb, 512)
                        ev = P.do("vector", lambda e, d=dstr, p_=bank(mc),
                                  bb=a32(sm["b2col"] + mc, 1):
                                  e.scalar_tensor_tensor(d, p_, bb, d, ALU.add, ALU.add))
                        bank_read_done(mc, sV, ev)
            state["res_tok"] = (sV, sV.val)
            state["smalls_free"] = (sV, sV.val)
            for sl4 in range(4):
                state["v_free"][sl4] = (sT, sT.val)

        for ell in range(nlayers):
            emit_inner(ell)

        if dump:
            P.wait("sync", sV, sV.val)
            P.wait("sync", sS, sS.val)
            P.do("sync", lambda e: e.dma_start(out=ydbg_d, in_=a32(R_RES, DC * T)),
                 sO, 16)
            P.do("sync", lambda e: e.wait_ge(sO.h, sO.val))
        elif mode == "full":
            P.wait("sync", sV, sV.val)
            P.wait("sync", sS, sS.val)
            P.wait("tensor", sV, sV.val)
            P.wait("tensor", sS, sS.val)
            P.wait("scalar", sV, sV.val)
            P.wait("vector", sS, sS.val)
            P.wait("sync", qST, qST.val)
            for qt in range(NT):
                xd = P.do("sync", lambda e, d=a32(O_X0R[0], 1024),
                          s=x1s_d[qt * 128:(qt + 1) * 128, :]:
                          e.dma_start(out=d, in_=s), qX0[0], 16)
                for dcg in range(2):
                    b = dcg
                    P.wait_t("tensor", P.bank_free[b])
                    for k in range(4):
                        dc = dcg * 4 + k
                        src = a32(R_RES + dc * T + qt * 128, 128)
                        P.do("tensor", lambda e, o=bank(b, 128, off=k * 128), s=src:
                             e.matmul(o, s, sb_idF[:], start=True, stop=True),
                             sT if k == 3 else None, 1)
                    mmv = sT.val
                    P.wait("vector", sT, mmv)
                    P.wait("vector", qX0[0], xd)
                    ev = P.do("vector", lambda e, o=a32(O_ST + dcg * 512, 512),
                              p_=bank(b), xs=a32(O_X0R[0] + dcg * 512, 512):
                              e.tensor_tensor(o, p_, xs, ALU.add), sV, 1)
                    bank_read_done(b, sV, ev)
                P.wait("vector", sV, sV.val)
                stile = a32(O_ST, 1024)
                negmu = a32(sm["t1"], 1)
                P.do("vector", lambda e, o=negmu, i=stile: e.tensor_reduce(
                    o, i, AX.X, ALU.add, negate=True), sV, 1)
                P.wait("vector", sV, sV.val)
                P.do("vector", lambda e, o=negmu, i=negmu: e.tensor_scalar(
                    o, i, 1.0 / 1024.0, None, ALU.mult), sV, 1)
                mu_v = sV.val
                P.wait("scalar", sV, mu_v)
                ssq = a32(sm["t2"], 1)
                P.do("scalar", lambda e, o=a32(O_X1T, 1024), i=stile, b=negmu, sx=ssq:
                     e.activation(o, i, AF.Square, bias=b, accum_out=sx), sS, 1)
                P.do("scalar", lambda e, o=ssq, i=ssq: e.activation(
                    o, i, AF.Ln, scale=1.0 / 1024.0, bias=sb_eps[:]), sS, 1)
                P.do("scalar", lambda e, o=ssq, i=ssq: e.activation(
                    o, i, AF.Exp, scale=-0.5), sS, 1)
                rstd_s = sS.val
                P.wait("vector", sS, rstd_s)
                P.do("vector", lambda e, o=a32(sm["t0"], 1), i=negmu: e.tensor_tensor(
                    o, i, ssq, ALU.mult), sV, 1)
                m2_v = sV.val
                P.wait("vector", sV, m2_v)
                yv = P.do("vector", lambda e, o=a32(O_X1T, 1024), i=stile:
                          e.tensor_scalar(o, i, ssq, a32(sm["t0"], 1), ALU.mult, ALU.add),
                          sV, 1)
                P.wait("sync", sV, yv)
                P.do("sync", lambda e, s=a32(O_X1T, 1024),
                     d=y_d[qt * 128:(qt + 1) * 128, :]: e.dma_start(out=d, in_=s),
                     sO, 16)
                P.wait("tensor", sV, yv)
                P.wait("scalar", sO, sO.val)
                P.wait("sync", sV, sV.val)
            P.do("sync", lambda e: e.wait_ge(sO.h, sO.val))
        else:
            P.do("sync", lambda e: e.wait_ge(sO.h, sO.val))

        P.flush(block)
    return nc



def _in_maps(inputs, consts):
    x = np.ascontiguousarray(np.asarray(inputs["L0_attn_embeddings"], dtype=np.float32))
    base = {
        "wq": np.ascontiguousarray(np.asarray(inputs["L0_attn_W_Q"], np.float32)),
        "wk": np.ascontiguousarray(np.asarray(inputs["L0_attn_W_K"], np.float32)),
        "wv": np.ascontiguousarray(np.asarray(inputs["L0_attn_W_V"], np.float32)),
        "wqkv": np.ascontiguousarray(np.asarray(inputs["L0_ffn_Wqkv"], np.float32)),
        "qsc": np.ascontiguousarray(np.asarray(inputs["L0_ffn_q_scale"], np.float32)),
        "ksc": np.ascontiguousarray(np.asarray(inputs["L0_ffn_k_scale"], np.float32)),
        "wo": np.ascontiguousarray(np.asarray(inputs["L0_ffn_Wo"], np.float32)),
        "lng": np.ascontiguousarray(np.asarray(inputs["L0_ffn_ln_g"], np.float32)),
        "lnb": np.ascontiguousarray(np.asarray(inputs["L0_ffn_ln_b"], np.float32)),
        "w1": np.ascontiguousarray(np.asarray(inputs["L0_ffn_W1"], np.float32)),
        "b1": np.ascontiguousarray(np.asarray(inputs["L0_ffn_b1"], np.float32)),
        "w2": np.ascontiguousarray(np.asarray(inputs["L0_ffn_W2"], np.float32)),
        "b2": np.ascontiguousarray(np.asarray(inputs["L0_ffn_b2"], np.float32)),
    }
    base.update(consts)
    maps = []
    for c in range(NB):
        m = dict(base)
        m["x0"] = np.ascontiguousarray(x[c])
        maps.append(m)
    return maps


def kernel(**inputs):
    consts = _host_consts()
    nc = build_program(depth=DEPTH, mode="full")
    maps = _in_maps(inputs, consts)
    res = run_bass_kernel_spmd(nc, maps, core_ids=list(range(NB)))
    out = np.stack([np.asarray(r["y"], dtype=np.float32) for r in res.results])
    return out


# version 14
# speedup vs baseline: 1.1090x; 1.0957x over previous
import numpy as np
import ml_dtypes
from contextlib import ExitStack

import concourse.bass as bass
import concourse.mybir as mybir
from concourse.bass_utils import run_bass_kernel_spmd

F32 = mybir.dt.float32
BF16 = mybir.dt.bfloat16
AF = mybir.ActivationFunctionType
ALU = mybir.AluOpType
AX = mybir.AxisListType

NB = 8
T = 2048
D = 1024
DEPTH = 6
NHEAD = 16
DH = 64
WIN = 256
DI = 2730
NW = T // WIN
NT = T // 128
DC = D // 128
HG = 2
HGH = NHEAD // HG
NEG = -3.0e37
NKC = (DI + 127) // 128

MG0, MG1, MW00, MW01, MTRI = 0, 384, 896, 1152, 1408


def _host_consts():
    half = np.arange(0, DH, 2, dtype=np.float64)
    inv_freq = 1.0 / (10000.0 ** (half / DH))
    t = np.arange(T, dtype=np.float64)
    freqs = np.concatenate([t[:, None] * inv_freq[None, :]] * 2, -1)
    base_scale = (half + 0.4 * DH) / (1.4 * DH)
    power = (t - T // 2) / (WIN // 2)
    scale = np.concatenate([base_scale[None, :] ** power[:, None]] * 2, -1)
    cos = np.cos(freqs); sin = np.sin(freqs)
    sign = np.where(np.arange(DH) < DH // 2, -1.0, 1.0)[None, :]
    rot = lambda a: np.roll(a, -32, axis=1)
    rep = lambda a: np.tile(a, (1, HGH)).astype(ml_dtypes.bfloat16)
    tables = np.stack([rep(cos * scale), rep(rot(sign * sin * scale)),
                       rep(cos / scale), rep(rot(sign * sin / scale))])
    p = np.arange(128)[:, None]

    def band(ncols, lo, hi):
        j = np.arange(ncols)[None, :]
        return np.where((j >= lo) & (j <= hi), 0.0, NEG).astype(np.float32)
    masks = np.concatenate([
        band(384, p, p + WIN),
        band(512, p + 128, p + 384),
        band(256, 0 * p, p),
        band(256, 0 * p, p + 128),
        band(128, 0 * p, p),
    ], axis=1)
    return dict(
        tables=np.ascontiguousarray(tables),
        masks=masks,
        masksb=masks.astype(ml_dtypes.bfloat16),
        identF=np.eye(128, dtype=np.float32),
        identB=np.eye(128, dtype=np.float32).astype(ml_dtypes.bfloat16),
        c1024=np.full((128, 1), 1.0 / 1024.0, dtype=ml_dtypes.bfloat16),
        row8=np.full((1, 128), 8.0, dtype=np.float32),
        row1=np.full((1, 128), 1.0, dtype=np.float32),
        epscol=np.full((128, 1), 1.0e-5, dtype=np.float32),
        eps24col=np.full((128, 1), 1.0e-24, dtype=np.float32),
    )


class SemC:
    __slots__ = ("h", "val")
    def __init__(self, h):
        self.h = h
        self.val = 0


ENGS = ("sync", "gpsimd", "scalar", "vector", "tensor")


class Prog:
    def __init__(self):
        self.ops = {e: [] for e in ENGS}
        self.waited = {}
        self.esem = {}
        self.bank_free = [(None, 0)] * 8
        self.bank_ready = [(None, 0)] * 8

    def do(self, eng, fn, inc=None, n=1, dep=True):
        if eng in ("vector", "scalar") or (eng == "gpsimd" and n != 16):
            es = self.esem[eng]
            assert inc is None or inc is es
            if dep:
                self.wait(eng, es, es.val)
            inc, n = es, 1
        if inc is None:
            self.ops[eng].append(fn)
            return None
        inc.val += n
        h, v = inc.h, inc.val
        self.ops[eng].append(lambda e, fn=fn, h=h, n=n: fn(e).then_inc(h, n))
        return v

    def wait(self, eng, sem, val):
        if sem is None or val is None or val <= 0:
            return
        key = (eng, id(sem))
        if self.waited.get(key, 0) >= val:
            return
        self.waited[key] = val
        h = sem.h
        self.ops[eng].append(lambda e, h=h, val=val: e.wait_ge(h, val))

    def wait_t(self, eng, tok):
        if tok is not None:
            self.wait(eng, tok[0], tok[1])

    def flush(self, block):
        for eng in ENGS:
            ops = self.ops[eng]
            if not ops:
                continue
            def body(e, ops=ops):
                for fn in ops:
                    fn(e)
            getattr(block, eng)(body)


def build_program(depth=DEPTH, mode="full"):
    nc = bass.Bass("TRN2", target_bir_lowering=False, debug=False)

    def din(name, shape, dt=F32):
        return nc.dram_tensor(name, list(shape), dt, kind="ExternalInput").ap()

    x0 = din("x0", [T, D])
    wq_d = din("wq", [D, D]); wk_d = din("wk", [D, D]); wv_d = din("wv", [D, D])
    wqkv_d = din("wqkv", [DEPTH, D, 3 * D])
    qsc_d = din("qsc", [DEPTH, DH]); ksc_d = din("ksc", [DEPTH, DH])
    wo_d = din("wo", [DEPTH, D, D])
    lng_d = din("lng", [DEPTH, D]); lnb_d = din("lnb", [DEPTH, D])
    w1_d = din("w1", [DEPTH, D, 2 * DI]); b1_d = din("b1", [DEPTH, 2 * DI])
    w2_d = din("w2", [DEPTH, DI, D]); b2_d = din("b2", [DEPTH, D])
    tbl_d = din("tables", [4, T, HGH * DH], BF16)
    masks_d = din("masks", [128, 1536])
    masksb_d = din("masksb", [128, 1536], BF16)
    identF_d = din("identF", [128, 128])
    identB_d = din("identB", [128, 128], BF16)
    c1024_d = din("c1024", [128, 1], BF16)
    row8_d = din("row8", [1, 128]); row1_d = din("row1", [1, 128])
    eps_d = din("epscol", [128, 1])
    eps24_d = din("eps24col", [128, 1])

    y_d = nc.dram_tensor("y", [T, D], F32, kind="ExternalOutput").ap()
    x1s_d = nc.dram_tensor("x1s", [T, D], F32, kind="Internal").ap()
    rstash_d = nc.dram_tensor("rstash", [128, DC * T], F32, kind="Internal").ap()
    dump = mode.startswith("L") or mode.startswith("A")
    attn_only = mode.startswith("A")
    nlayers = depth if mode in ("full",) else (0 if mode == "outer" else int(mode[1:]) + 1)
    if dump:
        ydbg_d = nc.dram_tensor("ydbg", [128, DC * T], F32, kind="ExternalOutput").ap()

    with ExitStack() as ctx:
        A32 = ctx.enter_context(nc.sbuf_tensor("A32", [128, 24064], F32))
        A16 = ctx.enter_context(nc.sbuf_tensor("A16", [128, 50816], BF16))
        sb_idF = ctx.enter_context(nc.sbuf_tensor("sb_idF", [128, 128], F32))
        sb_idB = ctx.enter_context(nc.sbuf_tensor("sb_idB", [128, 128], BF16))
        sb_c1024 = ctx.enter_context(nc.sbuf_tensor("sb_c1024", [128, 1], BF16))
        sb_row8 = ctx.enter_context(nc.sbuf_tensor("sb_row8", [1, 128], F32))
        sb_row1 = ctx.enter_context(nc.sbuf_tensor("sb_row1", [1, 128], F32))
        sb_eps = ctx.enter_context(nc.sbuf_tensor("sb_eps", [128, 1], F32))
        sb_eps24 = ctx.enter_context(nc.sbuf_tensor("sb_eps24", [128, 1], F32))
        sb_mskb = ctx.enter_context(nc.sbuf_tensor("sb_mskb", [128, 1536], BF16))
        PS = ctx.enter_context(nc.psum_tensor("PS", [128, 4096], F32))

        R_RES = 0
        R_MASK = 16384
        R_A = 17920
        R_SM = 22016
        R_W = 0
        R_KT = 12288
        R_AT = 20480
        R_TAB = 28672
        R_V = 36864
        R_XS = 38912
        R_QW = 40960
        R_EP = 44032
        R_WO = 46720
        sm = {}
        _p = [R_SM]
        for nm, c in [("ssq", 32), ("ssk", 16), ("negmax", 8), ("sumexp", 8),
                      ("recip", 8), ("murow", 512), ("rstdrow", 512),
                      ("qsrow", 512), ("gcol", 8), ("bcol", 8), ("b2col", 8),
                      ("b1a", 24), ("b1g", 24), ("t0", 8), ("t1", 8), ("t2", 8),
                      ("bnd", 8)]:
            sm[nm] = _p[0]; _p[0] += c
        assert _p[0] <= R_SM + 2048

        def a32(base, cols, pr=128, po=0):
            return A32[po:po + pr, base:base + cols]

        def a16(base, cols, pr=128, po=0):
            return A16[po:po + pr, base:base + cols]

        def bank(b, cols=512, off=0, pr=128, po=0):
            return PS[po:po + pr, b * 512 + off:b * 512 + off + cols]

        sQ = SemC(ctx.enter_context(nc.semaphore("sQ")))
        qC = SemC(ctx.enter_context(nc.semaphore("qC")))
        qX0 = [SemC(ctx.enter_context(nc.semaphore("qX0a"))),
               SemC(ctx.enter_context(nc.semaphore("qX0b")))]
        qST = SemC(ctx.enter_context(nc.semaphore("qST")))
        sG = SemC(ctx.enter_context(nc.semaphore("sG")))
        sT = SemC(ctx.enter_context(nc.semaphore("sT")))
        sV = SemC(ctx.enter_context(nc.semaphore("sV")))
        sS = SemC(ctx.enter_context(nc.semaphore("sS")))
        sO = SemC(ctx.enter_context(nc.semaphore("sO")))
        sGC = SemC(ctx.enter_context(nc.semaphore("sGC")))

        block = ctx.enter_context(nc.Block())
        P = Prog()
        P.esem = {"vector": sV, "scalar": sS, "gpsimd": sGC}

        def mm_acc(b, cols, pairs, off=0, opr=128, opo=0):
            P.wait_t("tensor", P.bank_free[b])
            n = len(pairs)
            for i, (l, r) in enumerate(pairs):
                P.do("tensor",
                     lambda e, o=bank(b, cols, off, opr, opo), l=l, r=r,
                     st=(i == 0), sp=(i == n - 1):
                     e.matmul(o, l, r, start=st, stop=sp),
                     sT if i == n - 1 else None, 1)
            P.bank_ready[b] = (sT, sT.val)
            return sT.val

        def bank_read_done(b, sem, val):
            P.bank_free[b] = (sem, val)

        for dst, src in [(sb_idF[:], identF_d), (sb_idB[:], identB_d),
                         (sb_c1024[:], c1024_d), (sb_row8[:], row8_d),
                         (sb_row1[:], row1_d), (sb_eps[:], eps_d),
                         (sb_eps24[:], eps24_d),
                         (sb_mskb[:], masksb_d), (a32(R_MASK, 1536), masks_d)]:
            P.do("sync", lambda e, d=dst, s=src: e.dma_start(out=d, in_=s), qC, 16)
        consts_q = qC.val

        OX0T = 0
        OWB = 16384
        OKT = 24576
        OSTG = 40960
        OQR = 16384
        OVR = 0
        OE = 18432
        OPT = 20480
        O_X0R = [R_A, R_A + 1024]
        O_ST = R_A + 2048
        O_X1T = R_A + 3072

        qts_d = nc.dram_tensor("qts", [NT, 128, D], BF16, kind="Internal").ap()
        vs_d = nc.dram_tensor("vs", [NT, 128, D], BF16, kind="Internal").ap()
        qQS = [SemC(ctx.enter_context(nc.semaphore("qQSa"))),
               SemC(ctx.enter_context(nc.semaphore("qQSb")))]
        qQR = [SemC(ctx.enter_context(nc.semaphore("qQRa"))),
               SemC(ctx.enter_context(nc.semaphore("qQRb")))]
        qVR = [SemC(ctx.enter_context(nc.semaphore(f"qVR{i}"))) for i in range(4)]

        x0_ld = []
        P.wait("tensor", qC, consts_q)
        ring_free = [None, None]
        for tt in range(NT):
            P.wait_t("sync", ring_free[tt % 2])
            dst = a32(O_X0R[tt % 2], 1024)
            src = x0[tt * 128:(tt + 1) * 128, :]
            x0_ld.append(P.do("sync", lambda e, d=dst, s=src: e.dma_start(out=d, in_=s),
                              qX0[tt % 2], 16))
            P.wait("tensor", qX0[tt % 2], x0_ld[tt])
            P.wait_t("tensor", ring_free[tt % 2])
            for dcg in range(2):
                b = dcg
                P.wait_t("tensor", P.bank_free[b])
                for k in range(4):
                    dc = dcg * 4 + k
                    src2 = a32(O_X0R[tt % 2], 1024)[:, dc * 128:dc * 128 + 128]
                    P.do("tensor",
                         lambda e, o=bank(b, 128, off=k * 128), s=src2:
                         e.matmul(o, s, sb_idF[:], start=True, stop=True),
                         sT if k == 3 else None, 1)
                mmv = sT.val
                P.wait("scalar", sT, mmv)
                last = None
                for k in range(4):
                    dc = dcg * 4 + k
                    dst2 = a16(OX0T + dc * T + tt * 128, 128)
                    last = P.do("scalar",
                                lambda e, d=dst2, s=bank(b, 128, off=k * 128):
                                e.copy(d, s), sS, 1)
                bank_read_done(b, sS, last)
            ring_free[tt % 2] = (sS, sS.val)
        x0T_done = (sS, sS.val)

        def wload(w_ap, prev_consumed=None):
            if prev_consumed is not None:
                P.wait_t("gpsimd", prev_consumed)
            for dc in range(DC):
                src = w_ap[dc * 128:(dc + 1) * 128, :]
                dst = a16(OWB + dc * 1024, 1024)
                P.do("gpsimd", lambda e, d=dst, s=src: e.dma_start(out=d, in_=s), sG, 16)
            return sG.val

        wld = wload(wk_d)
        P.wait("tensor", sG, wld)
        P.wait_t("tensor", x0T_done)
        for mc in range(DC):
            for tg in range(4):
                b = 2 + ((mc * 4 + tg) % 2)
                pairs = [(a16(OWB + dc * 1024 + mc * 128, 128),
                          a16(OX0T + dc * T + tg * 512, 512)) for dc in range(DC)]
                mmv = mm_acc(b, 512, pairs)
                P.wait("scalar", sT, mmv)
                ev = P.do("scalar", lambda e, d=a16(OKT + mc * T + tg * 512, 512),
                          s=bank(b): e.copy(d, s), sS, 1)
                bank_read_done(b, sS, ev)
        kt_done = sS.val

        stg_free = [None, None]

        def proj_to_hbm(which, wld_mark):
            P.wait("tensor", sG, wld_mark)
            n_out = NT * 2 if which == "v" else DC * 4
            for i in range(n_out):
                b = 2 + (i % 2)
                sl = i % 2
                if which == "v":
                    tt, vg = i // 2, i % 2
                    tt, vg = divmod(i, 2)
                    pairs = [(a16(OX0T + dc * T + tt * 128, 128),
                              a16(OWB + dc * 1024 + vg * 512, 512)) for dc in range(DC)]
                else:
                    mc, tg = divmod(i, 4)
                    pairs = [(a16(OWB + dc * 1024 + mc * 128, 128),
                              a16(OX0T + dc * T + tg * 512, 512)) for dc in range(DC)]
                mmv = mm_acc(b, 512, pairs)
                P.wait("scalar", sT, mmv)
                P.wait_t("scalar", stg_free[sl])
                ev = P.do("scalar", lambda e, d=a16(OSTG + sl * 512, 512), s=bank(b):
                          e.copy(d, s), sS, 1)
                bank_read_done(b, sS, ev)
                P.wait("sync", sS, ev)
                if which == "v":
                    dd = vs_d[tt, :, vg * 512:(vg + 1) * 512]
                else:
                    mc, tg = divmod(i, 4)
                    dd = qts_d[tg * 4:(tg + 1) * 4, :, mc * 128:(mc + 1) * 128]                         .rearrange("q p c -> p q c")
                dmam = P.do("sync", lambda e, d=dd, s=a16(OSTG + sl * 512, 512)
                            .rearrange("p (q c) -> p q c", q=4 if which == "q" else 1):
                            e.dma_start(out=d, in_=s), qQS[sl], 16)
                stg_free[sl] = (qQS[sl], dmam)
            return (qQS[0].val, qQS[1].val)

        wld = wload(wv_d, prev_consumed=(sS, kt_done))
        v_done = proj_to_hbm("v", wld)
        wld = wload(wq_d, prev_consumed=(sS, sS.val))
        q_done = proj_to_hbm("q", wld)

        x1_qdma = []
        qr_free = [None, None]
        vloads = {}
        P.wait("sync", qQS[0], q_done[0])
        P.wait("sync", qQS[1], q_done[1])
        for qt in range(NT):
            nk = (qt + 1) * 128
            njg = (nk + 511) // 512
            xd = P.do("sync", lambda e, d=a32(O_X0R[0], 1024),
                      s=x0[qt * 128:(qt + 1) * 128, :]: e.dma_start(out=d, in_=s),
                      qX0[0], 16)
            qsl = qt % 2
            P.wait_t("sync", qr_free[qsl])
            qd = P.do("sync", lambda e, d=a16(OQR + qsl * 1024, 1024),
                      s=qts_d[qt]: e.dma_start(out=d, in_=s), qQR[qsl], 16)
            qs4 = qVR[qt % 4]
            P.wait("sync", qs4, qs4.val)
            vm = P.do("sync", lambda e, d=a16(OVR + qt * 1024, 1024),
                      s=vs_d[qt]: e.dma_start(out=d, in_=s), qs4, 16)
            vloads[qt] = (qs4, vm)
            P.wait("tensor", qQR[qsl], qd)
            for jg in range(njg):
                w_ = min(512, nk - jg * 512)
                pairs = [(a16(OQR + qsl * 1024 + mc * 128, 128),
                          a16(OKT + mc * T + jg * 512, w_)) for mc in range(DC)]
                mm_acc(jg, w_, pairs)
            s_mm = sT.val
            P.wait("vector", sT, s_mm)
            dsl = PS[:, qt * 128:qt * 128 + 128]
            mski = P.do("vector", lambda e, d=dsl: e.tensor_tensor(
                d, d, a32(R_MASK + MTRI, 128), ALU.add), sV, 1)
            P.wait("vector", sV, mski)
            nm = a32(sm["negmax"], 1)
            P.do("vector", lambda e, o=nm, n=nk: e.tensor_reduce(
                o, PS[:, 0:n], AX.X, ALU.max, negate=True), sV, 1)
            red_v = sV.val
            P.wait("scalar", sV, red_v)
            nm32 = a32(sm["t0"], 1)
            nm32_s = P.do("scalar", lambda e, o=nm32, i=nm: e.activation(
                o, i, AF.Copy, scale=1.0 / 32.0), sS, 1)
            P.wait("scalar", sS, nm32_s)
            sxp = a32(sm["sumexp"], 1)
            exp_s = P.do("scalar", lambda e, o=a16(OE, nk), n=nk, b=nm32, sx=sxp:
                         e.activation(o, PS[:, 0:n], AF.Exp, bias=b, scale=1.0 / 32.0,
                                      accum_out=sx), sS, 1)
            for b_ in range(njg):
                bank_read_done(b_, sS, exp_s)
            P.wait("vector", sS, exp_s)
            rc = a32(sm["recip"], 1)
            P.do("vector", lambda e, o=rc, i=sxp: e.reciprocal(o, i), sV, 1)
            rcv = sV.val
            P.wait("vector", sV, rcv)
            pn = P.do("vector", lambda e, o=a16(OE, nk), i=a16(OE, nk), r=rc:
                      e.tensor_scalar(o, i, r, None, ALU.mult), sV, 1)
            P.wait("tensor", sV, pn)
            for jc in range(qt + 1):
                b = 4 + ((jc // 2) % 2)
                off = (jc % 2) * 128
                if jc % 2 == 0:
                    P.wait_t("tensor", P.bank_free[b])
                mm = P.do("tensor", lambda e, o=bank(b, 128, off=off),
                          l=a16(OE + jc * 128, 128):
                          e.matmul(o, l, sb_idB[:], start=True, stop=True), sT, 1)
                if jc % 2 == 1 or jc == qt:
                    P.wait("scalar", sT, mm)
                    wdt = 128 * (2 if jc % 2 == 1 else 1)
                    o0 = (jc // 2) * 256
                    ev = P.do("scalar", lambda e, d=a16(OPT + o0, wdt),
                              s=bank(b, wdt): e.copy(d, s), sS, 1)
                    bank_read_done(b, sS, ev)
            pt_ev = sS.val
            P.wait("tensor", sS, pt_ev)
            for jc in range(qt + 1):
                P.wait_t("tensor", vloads[jc])
            for dg in range(2):
                pairs = [(a16(OPT + jc * 128, 128),
                          a16(OVR + jc * 1024 + dg * 512, 512))
                         for jc in range(qt + 1)]
                mm_acc(6 + dg, 512, pairs)
            pv_mm = sT.val
            qr_free[qsl] = (sT, s_mm)
            P.wait("vector", sT, pv_mm)
            P.wait("vector", qX0[0], xd)
            for dg in range(2):
                ev = P.do("vector", lambda e, o=a32(O_ST + dg * 512, 512),
                          p_=bank(6 + dg), xs=a32(O_X0R[0] + dg * 512, 512):
                          e.tensor_tensor(o, p_, xs, ALU.add), sV, 1)
                bank_read_done(6 + dg, sV, ev)
            st_v = sV.val
            P.wait("vector", sV, st_v)
            stile = a32(O_ST, 1024)
            negmu = a32(sm["t1"], 1)
            P.do("vector", lambda e, o=negmu, i=stile: e.tensor_reduce(
                o, i, AX.X, ALU.add, negate=True), sV, 1)
            P.wait("vector", sV, sV.val)
            P.do("vector", lambda e, o=negmu, i=negmu: e.tensor_scalar(
                o, i, 1.0 / 1024.0, None, ALU.mult), sV, 1)
            mu_v = sV.val
            P.wait("scalar", sV, mu_v)
            ssq = a32(sm["t2"], 1)
            P.do("scalar", lambda e, o=a32(O_X1T, 1024), i=stile, b=negmu, sx=ssq:
                 e.activation(o, i, AF.Square, bias=b, accum_out=sx), sS, 1)
            P.do("scalar", lambda e, o=ssq, i=ssq: e.activation(
                o, i, AF.Ln, scale=1.0 / 1024.0, bias=sb_eps[:]), sS, 1)
            P.do("scalar", lambda e, o=ssq, i=ssq: e.activation(
                o, i, AF.Exp, scale=-0.5), sS, 1)
            rstd_s = sS.val
            P.wait("vector", sS, rstd_s)
            P.do("vector", lambda e, o=a32(sm["t0"], 1), i=negmu, j=ssq:
                 e.tensor_tensor(o, i, j, ALU.mult), sV, 1)
            m2_v = sV.val
            P.wait("vector", sV, m2_v)
            x1v = P.do("vector", lambda e, o=a32(O_X1T, 1024), i=stile, r=ssq,
                       m2=a32(sm["t0"], 1):
                       e.tensor_scalar(o, i, r, m2, ALU.mult, ALU.add), sV, 1)
            P.wait("sync", sV, x1v)
            x1_qdma.append(P.do("sync", lambda e, s=a32(O_X1T, 1024),
                                d=x1s_d[qt * 128:(qt + 1) * 128, :]:
                                e.dma_start(out=d, in_=s), qST, 16))
            if mode == "outer":
                P.do("sync", lambda e, s=a32(O_X1T, 1024),
                     d=y_d[qt * 128:(qt + 1) * 128, :]: e.dma_start(out=d, in_=s),
                     sO, 16)
            P.wait("tensor", sV, x1v)
            for dcg in range(2):
                b = dcg
                P.wait_t("tensor", P.bank_free[b])
                for k in range(4):
                    dc = dcg * 4 + k
                    src = a32(O_X1T, 1024)[:, dc * 128:dc * 128 + 128]
                    P.do("tensor", lambda e, o=bank(b, 128, off=k * 128), s=src:
                         e.matmul(o, s, sb_idF[:], start=True, stop=True),
                         sT if k == 3 else None, 1)
                mmv = sT.val
                P.wait("vector", sT, mmv)
                ev = None
                for k in range(4):
                    dc = dcg * 4 + k
                    dst = a32(R_RES + dc * T + qt * 128, 128)
                    ev = P.do("vector", lambda e, d=dst, s=bank(b, 128, off=k * 128):
                              e.tensor_copy(d, s), sV, 1)
                bank_read_done(b, sV, ev)
            P.wait("sync", sV, st_v)
            P.wait("scalar", qST, qST.val)
            P.wait("scalar", sO, sO.val)
            P.wait("scalar", sT, sT.val)
        outer_done_v = sV.val
        outer_done_s = sS.val
        outer_done_t = sT.val

        QS8B, KS8B = R_A + 2048, R_A + 3072
        TMPA, TMPB, TMPC = R_A, R_A + 512, R_A + 1024
        XQW, XKW, XQT = R_QW, R_QW + 1024, R_QW + 2048
        EP_E = [R_EP, R_EP + 896, R_EP + 1792]
        EP_PT = R_QW
        MID = [R_W + i * 1024 for i in range(12)] + \
              [R_KT + i * 1024 for i in range(8)] + \
              [R_TAB + i * 1024 for i in range(2)]
        W1RING = [R_TAB + 2048, R_TAB + 4096]
        W2RING = [R_TAB + 6144, R_TAB + 7168, R_EP, R_EP + 1024]
        XSF = R_XS

        qWQ = SemC(ctx.enter_context(nc.semaphore("qWQ")))
        qTB = [SemC(ctx.enter_context(nc.semaphore(f"qTB{i}"))) for i in range(2)]
        qWO = SemC(ctx.enter_context(nc.semaphore("qWO")))
        qW1 = [SemC(ctx.enter_context(nc.semaphore(f"qW1{i}"))) for i in range(2)]
        qW2 = [SemC(ctx.enter_context(nc.semaphore(f"qW2{i}"))) for i in range(4)]
        qMS = SemC(ctx.enter_context(nc.semaphore("qMS")))
        qRS = SemC(ctx.enter_context(nc.semaphore("qRS")))
        qSC = SemC(ctx.enter_context(nc.semaphore("qSC")))
        qXS = SemC(ctx.enter_context(nc.semaphore("qXS")))

        state = dict(tab_free=[None, None], w1_free=[None, None],
                     w2_free=[None] * 4,
                     v_free=[None] * 4, smalls_free=None, res_tok=None, wo_free=None,
                     sq_free=[None] * 3, b_free=[None] * 3, chain_seq=[0])
        SQP = [R_A, R_A + 512, R_MASK]
        BP = [R_A + 1024, R_A + 1536, R_MASK + 512]

        def emit_inner(ell):
            if ell == 0:
                for eng in ("gpsimd", "sync", "tensor", "vector", "scalar"):
                    P.wait(eng, sT, outer_done_t)
                    P.wait(eng, sV, outer_done_v)
                    P.wait(eng, sS, outer_done_s)
                P.wait("sync", qST, qST.val)

            P.wait_t("sync", state["smalls_free"])
            qs_rep = qsc_d[ell][None, None, :].broadcast_to([128, HGH, DH])
            ks_rep = ksc_d[ell][None, None, :].broadcast_to([128, HGH, DH])
            P.wait("sync", qSC, qSC.val)
            for dst, srcp in [
                (a32(QS8B, 512).rearrange("p (a b) -> p a b", a=HGH), qs_rep),
                (a32(KS8B, 512).rearrange("p (a b) -> p a b", a=HGH), ks_rep),
            ]:
                P.do("sync", lambda e, d=dst, s=srcp: e.dma_start(out=d, in_=s), qSC, 16)
            sc_ld = qSC.val
            P.wait("vector", qSC, qSC.val)
            P.do("vector", lambda e, o=a32(QS8B, 512), i=a32(QS8B, 512):
                 e.tensor_scalar(o, i, 8.0, None, ALU.mult))
            qs8_v = sV.val
            P.wait_t("sync", state["res_tok"])
            colspecs = [
                (sm["gcol"], 8, lng_d[ell].rearrange("(c p) -> c p", p=128), None),
                (sm["bcol"], 8, lnb_d[ell].rearrange("(c p) -> c p", p=128), None),
                (sm["b2col"], 8, b2_d[ell].rearrange("(c p) -> c p", p=128), None),
                (sm["b1a"], 22, b1_d[ell, 0:2816].rearrange("(c p) -> c p", p=128), None),
                (sm["b1g"], 22, b1_d[ell, 2730:5418].rearrange("(c p) -> c p", p=128),
                 b1_d[ell, 5418:5460][None, :]),
            ]
            for dcol, ncc, srcp, tail in colspecs:
                ld = P.do("sync", lambda e, d=a32(TMPA, 128, pr=ncc if tail is None
                          else ncc - 1), s=srcp: e.dma_start(out=d, in_=s), qMS, 16)
                if tail is not None:
                    ld = P.do("sync", lambda e, d=a32(TMPA, 42, pr=1, po=ncc - 1),
                              s=tail: e.dma_start(out=d, in_=s), qMS, 16)
                P.wait("tensor", qMS, ld)
                P.wait_t("tensor", P.bank_free[5])
                mm = P.do("tensor", lambda e, o=bank(5, ncc), l=a32(TMPA, 128, pr=ncc),
                          r=sb_idF[0:ncc, 0:ncc]: e.matmul(o, l, r, start=True, stop=True),
                          sT, 1)
                P.wait("scalar", sT, mm)
                ev = P.do("scalar", lambda e, d=a32(dcol, ncc), s=bank(5, ncc):
                          e.copy(d, s))
                bank_read_done(5, sS, ev)
                P.wait("sync", sT, mm)
            smalls_ld = qMS.val
            smalls_s = sS.val

            P.wait_t("sync", state["res_tok"])
            P.wait("sync", qXS, qXS.val)
            rs = P.do("sync", lambda e, d=rstash_d, s=a32(R_RES, DC * T):
                      e.dma_start(out=d, in_=s), qRS, 16)

            qkv_mm_done = [0]
            for hg in range(2):
                P.wait_t("gpsimd", state["res_tok"])
                if hg == 1:
                    P.wait("gpsimd", sT, qkv_mm_done[0])
                for mi, cbase in enumerate((0, D, 2 * D)):
                    src = wqkv_d[ell][:, cbase + hg * 512: cbase + hg * 512 + 512] \
                        .rearrange("(c p) m -> p c m", p=128)
                    dst = a16(R_W + mi * 4096, 4096).rearrange("p (c m) -> p c m", c=DC)
                    P.do("gpsimd", lambda e, d=dst, s=src: e.dma_start(out=d, in_=s),
                         qWQ, 16)
                wqkv_ld = qWQ.val
                P.wait_t("gpsimd", state["wo_free"])
                for kc in range(4):
                    src = wo_d[ell, hg * 512 + kc * 128: hg * 512 + (kc + 1) * 128, :]
                    P.do("gpsimd", lambda e, d=a16(R_WO + kc * 1024, 1024), s=src:
                         e.dma_start(out=d, in_=s), qWO, 16)
                wo_ld = qWO.val
                if hg == 1:
                    P.wait("scalar", sT, sT.val)

                for w in range(NW):
                    tsl = w % 2
                    P.wait_t("sync", state["tab_free"][tsl])
                    for ti in range(4):
                        src = tbl_d[ti, w * 256:(w + 1) * 256, :] \
                            .rearrange("(tt p) c -> p tt c", p=128)
                        dst = a16(R_TAB + tsl * 4096 + ti * 1024, 1024) \
                            .rearrange("p (tt c) -> p tt c", tt=2)
                        P.do("sync", lambda e, d=dst, s=src: e.dma_start(out=d, in_=s),
                             qTB[tsl], 16)
                    tab_ld = qTB[tsl].val

                    P.wait("gpsimd", sT, qkv_mm_done[0])
                    P.wait("gpsimd", qRS, rs)
                    src0 = rstash_d.rearrange("p (c t) -> p c t", c=DC)[
                        :, 0:4, w * 256:(w + 1) * 256]
                    dst0 = a16(R_XS, 1024).rearrange("p (c t) -> p c t", c=4)
                    P.do("gpsimd", lambda e, d=dst0, s=src0: e.dma_start(out=d, in_=s),
                         qXS, 16)
                    if w > 0:
                        src1 = rstash_d.rearrange("p (c t) -> p c t", c=DC)[
                            :, 4:8, w * 256 - 1:(w + 1) * 256 - 1]
                        dst1 = a16(R_XS + 1024, 1024).rearrange("p (c t) -> p c t", c=4)
                        P.do("gpsimd", lambda e, d=dst1, s=src1:
                             e.dma_start(out=d, in_=s), qXS, 16)
                        xs_v = sV.val
                    else:
                        src1 = rstash_d.rearrange("p (c t) -> p c t", c=DC)[
                            :, 4:8, 0:255]
                        dst1 = a16(R_XS + 1024, 1024).rearrange(
                            "p (c t) -> p c t", c=4)[:, :, 1:256]
                        P.do("gpsimd", lambda e, d=dst1, s=src1:
                             e.dma_start(out=d, in_=s), qXS, 16)
                        P.wait("vector", sT, qkv_mm_done[0])
                        for c4 in range(4):
                            P.do("vector", lambda e, d=a16(R_XS + 1024 + c4 * 256, 1):
                                 e.memset(d, 0.0))
                        xs_v = sV.val
                    xs_ld = qXS.val

                    P.wait("tensor", qWQ, wqkv_ld)
                    P.wait("tensor", qXS, xs_ld)
                    P.wait("tensor", sV, xs_v)
                    qkv_mm_of = {}
                    for mi in (0, 1, 2):
                        for tt in range(2):
                            pairs = [(a16(R_XS + dc * 256 + tt * 128, 128),
                                      a16(R_W + mi * 4096 + dc * 512, 512))
                                     for dc in range(DC)]
                            qkv_mm_of[(tt, mi)] = mm_acc(tt * 3 + mi, 512, pairs)
                    qkv_mm = sT.val
                    qkv_mm_done[0] = qkv_mm

                    chains = [(0, 0), (0, 1), (1, 0), (1, 1)]
                    st1_s = {}; st2_v = {}; st3_s = {}; st7_v = {}
                    cpar = {}
                    wp = w % 2

                    def ssq_col(tt_, h):
                        return a32(sm["ssq"] + tt_ * 16 + wp * 8 + h, 1)

                    def sslot(ci):
                        side, tt_ = chains[ci]
                        if side == 0:
                            return a32(sm["ssq"] + tt_ * 16 + wp * 8, 8)
                        return a32(sm["ssk"] + tt_ * 8, 8)

                    for ci in range(4):
                        side, tt_ = chains[ci]
                        b = tt_ * 3 + side
                        par = state["chain_seq"][0] % 3
                        state["chain_seq"][0] += 1
                        cpar[ci] = par
                        P.wait("scalar", sT, qkv_mm_of[(tt_, side)])
                        P.wait_t("scalar", state["sq_free"][par])
                        P.do("scalar", lambda e, o=a32(SQP[par], 512), i=bank(b):
                             e.activation(o, i, AF.Square), dep=False)
                        st1_s[ci] = sS.val
                    for ci in range(4):
                        ssl = sslot(ci)
                        P.wait("vector", sS, st1_s[ci])
                        P.do("vector", lambda e, o=ssl, i=a32(SQP[cpar[ci]], 512)
                             .rearrange("p (h d) -> p h d", h=8):
                             e.tensor_reduce(o, i, AX.X, ALU.add), dep=False)
                        st2_v[ci] = sV.val
                    for ci in range(4):
                        ssl = sslot(ci)
                        P.wait("scalar", sV, st2_v[ci])
                        P.do("scalar", lambda e, o=ssl, i=ssl:
                             e.activation(o, i, AF.Ln, bias=sb_eps24[:]), dep=False)
                        P.do("scalar", lambda e, o=ssl, i=ssl:
                             e.activation(o, i, AF.Exp, scale=-0.5))
                        st3_s[ci] = sS.val

                    def chain_tail(ci):
                        side, tt_ = chains[ci]
                        b = tt_ * 3 + side
                        par = cpar[ci]
                        cosb = R_TAB + tsl * 4096 + (0 if side == 0 else 2) * 1024 + tt_ * 512
                        sinb = R_TAB + tsl * 4096 + (1 if side == 0 else 3) * 1024 + tt_ * 512
                        dstw = (XQW if side == 0 else XKW) + tt_ * 512
                        Bp = a32(BP[par], 512)
                        P.wait_t("vector", state["b_free"][par])
                        P.wait("vector", sT, qkv_mm_of[(tt_, side)])
                        P.wait("vector", qSC, sc_ld)
                        if side == 0:
                            P.wait("vector", sS, st1_s[ci])
                            P.do("vector", lambda e, o=Bp, i=bank(b), j=a32(QS8B, 512):
                                 e.tensor_tensor(o, i, j, ALU.mult), dep=False)
                        else:
                            P.wait("vector", sS, st3_s[ci])
                            ssl = sslot(ci)
                            for h in range(8):
                                P.do("vector", lambda e, o=Bp[:, h * 64:h * 64 + 64],
                                     i=bank(b, 64, off=h * 64), r=ssl[:, h:h + 1],
                                     j=a32(KS8B + h * 64, 64):
                                     e.scalar_tensor_tensor(o, i, r, j, ALU.mult,
                                                            ALU.mult), dep=False)
                        bank_read_done(b, sV, sV.val)
                        P.wait("vector", qTB[tsl], tab_ld)
                        hv = lambda base, lo: a32(base, 512).rearrange(
                            "p (h x) -> p h x", h=8)[:, :, lo:lo + 32]
                        hv16 = lambda base, lo: a16(base, 512).rearrange(
                            "p (h x) -> p h x", h=8)[:, :, lo:lo + 32]
                        P.do("vector", lambda e, o=hv(SQP[par], 0), i=hv(BP[par], 32),
                             j=hv16(sinb, 32): e.tensor_tensor(o, i, j, ALU.mult))
                        P.do("vector", lambda e, o=hv(SQP[par], 32), i=hv(BP[par], 0),
                             j=hv16(sinb, 0): e.tensor_tensor(o, i, j, ALU.mult),
                             dep=False)
                        P.do("vector", lambda e, o=Bp, i=Bp, j=a16(cosb, 512):
                             e.tensor_tensor(o, i, j, ALU.mult))
                        P.do("vector", lambda e, o=a16(dstw, 512), i=Bp,
                             j=a32(SQP[par], 512): e.tensor_tensor(o, i, j, ALU.add))
                        st7_v[ci] = sV.val
                        state["sq_free"][par] = (sV, sV.val)
                        state["b_free"][par] = (sV, sV.val)

                    def qk_transpose(side, bnk, ready_v):
                        P.wait("tensor", sV, ready_v)
                        for f0 in (0, 2):
                            P.wait_t("tensor", P.bank_free[bnk])
                            for k2, (fp, tt_) in enumerate(
                                    [(f0, 0), (f0, 1), (f0 + 1, 0), (f0 + 1, 1)]):
                                srcb = (XQW if side == 0 else XKW) + tt_ * 512
                                P.do("tensor", lambda e, o=bank(bnk, 128, off=k2 * 128),
                                     l=a16(srcb + fp * 128, 128):
                                     e.matmul(o, l, sb_idB[:], start=True, stop=True),
                                     sT if k2 == 3 else None, 1)
                            mmv = sT.val
                            P.wait("scalar", sT, mmv)
                            if side == 0:
                                dst = a16(XQT + f0 * 256, 512)
                            else:
                                dst = A16[:, R_KT:R_KT + 4 * T].rearrange(
                                    "p (f x) -> p f x", f=4)[
                                    :, f0:f0 + 2, w * 256:w * 256 + 256]
                            ev = P.do("scalar", lambda e, d=dst, s=bank(bnk, 512):
                                      e.copy(d, s))
                            bank_read_done(bnk, sS, ev)

                    chain_tail(0)
                    chain_tail(1)
                    qk_transpose(0, 6, max(st7_v[0], st7_v[1]))
                    chain_tail(2)
                    chain_tail(3)
                    qk_transpose(1, 7, max(st7_v[2], st7_v[3]))
                    for tt in range(2):
                        vb = tt * 3 + 2
                        slot = (w * 2 + tt) % 4
                        P.wait("scalar", sT, qkv_mm_of[(tt, 2)])
                        P.wait_t("scalar", state["v_free"][slot])
                        ev = P.do("scalar", lambda e, o=a16(R_V + slot * 512, 512),
                                  i=bank(vb): e.copy(o, i))
                        bank_read_done(vb, sS, ev)
                    prep_s = sS.val
                    tr_s = sS.val

                    nj0 = 128 if w == 0 else 384
                    nj1 = 256 if w == 0 else 384
                    m0 = MTRI if w == 0 else MG0
                    m1 = MW01 if w == 0 else MG0
                    kbase0 = 0 if w == 0 else (w - 1) * 256
                    kbase1 = 0 if w == 0 else (w - 1) * 256 + 128
                    jtts0 = [0] if w == 0 else [2 * w - 2, 2 * w - 1, 2 * w]
                    jtts1 = [0, 1] if w == 0 else [2 * w - 1, 2 * w, 2 * w + 1]
                    nch0, nch1 = len(jtts0), len(jtts1)
                    P.wait("tensor", sS, tr_s)

                    def sim_S(h):
                        pt = h // 2
                        po = (h % 2) * 64
                        par = (h % 3) * 2
                        for qh, nj, msk in ((0, nj0, m0), (1, nj1, m1)):
                            bnk = par + qh
                            P.wait_t("tensor", P.bank_free[bnk])
                            P.do("tensor", lambda e, o=bank(bnk, nj),
                                 m=sb_mskb[:, msk:msk + nj]:
                                 e.matmul(o, sb_idB[:], m, start=True, stop=False))
                        for qh, nj, kb in ((0, nj0, kbase0), (1, nj1, kbase1)):
                            bnk = par + qh
                            lhs = a16(XQT + pt * 256 + qh * 128, 128, pr=64, po=po)
                            rhs = a16(R_KT + pt * 2048 + kb, nj, pr=64, po=po)
                            P.do("tensor", lambda e, o=bank(bnk, nj), l=lhs, r=rhs:
                                 e.matmul(o, l, r, start=False, stop=True), sT, 1)
                            P.bank_ready[bnk] = (sT, sT.val)
                        return sT.val

                    def sim_max(h):
                        par = (h % 3) * 2
                        P.wait("vector", sT, s_mm_of[h])
                        for qh, nj in ((0, nj0), (1, nj1)):
                            P.do("vector", lambda e,
                                 o=a32(sm["negmax"] + (h % 3) * 2 + qh, 1),
                                 i=bank(par + qh, nj):
                                 e.tensor_reduce(o, i, AX.X, ALU.max, negate=True),
                                 dep=False)
                        P.wait("vector", sS, max(st3_s[0], st3_s[1]))
                        for qh in range(2):
                            P.do("vector", lambda e,
                                 o=a32(sm["negmax"] + (h % 3) * 2 + qh, 1),
                                 i=a32(sm["negmax"] + (h % 3) * 2 + qh, 1),
                                 r=ssq_col(qh, h):
                                 e.tensor_scalar(o, i, r, None, ALU.mult),
                                 dep=(qh == 0))
                        msk_v_of[h] = sV.val

                    def sim_exp(h):
                        par = (h % 3) * 2
                        P.wait("scalar", sV, msk_v_of[h])
                        if h >= 3:
                            P.wait("scalar", sT, ptmm_of.get(h - 3, 0))
                        for qh, nj in ((0, nj0), (1, nj1)):
                            bnk = par + qh
                            ex = P.do("scalar", lambda e,
                                      o=a16(EP_E[h % 3] + qh * 384, nj),
                                      i=bank(bnk, nj),
                                      b=a32(sm["negmax"] + (h % 3) * 2 + qh, 1),
                                      r=ssq_col(qh, h),
                                      sx=a32(sm["sumexp"] + (h % 3) * 2 + qh, 1):
                                      e.activation(o, i, AF.Exp, bias=b, scale=r,
                                                   accum_out=sx), dep=(qh == 0))
                            bank_read_done(bnk, sS, ex)
                        exp_s_of[h] = sS.val

                    def sim_norm(h):
                        P.wait("vector", sS, exp_s_of[h])
                        P.do("vector", lambda e,
                             o=a32(sm["recip"] + (h % 3) * 2, 2),
                             i=a32(sm["sumexp"] + (h % 3) * 2, 2):
                             e.reciprocal(o, i), dep=False)
                        for qh, nj in ((0, nj0), (1, nj1)):
                            P.do("vector", lambda e, o=a16(EP_E[h % 3] + qh * 384, nj),
                                 i=a16(EP_E[h % 3] + qh * 384, nj),
                                 r=a32(sm["recip"] + (h % 3) * 2 + qh, 1):
                                 e.tensor_scalar(o, i, r, None, ALU.mult),
                                 dep=(qh == 0))
                        norm_v_of[h] = sV.val

                    def sim_ptpv(h):
                        pt = h // 2
                        po = (h % 2) * 64
                        P.wait("tensor", sV, norm_v_of[h])
                        tiles = [(0, jc) for jc in range(nch0)] + \
                                [(1, jc) for jc in range(nch1)]
                        bA = 6 + (h % 2)
                        bB = 7 - (h % 2)
                        evs = []
                        for gi, gstart in enumerate(range(0, len(tiles), 4)):
                            grp = tiles[gstart:gstart + 4]
                            bnk = bA if gi == 0 else bB
                            P.wait_t("tensor", P.bank_free[bnk])
                            for k2, (qh, jc) in enumerate(grp):
                                P.do("tensor", lambda e, o=bank(bnk, 128, off=k2 * 128),
                                     l=a16(EP_E[h % 3] + qh * 384 + jc * 128, 128):
                                     e.matmul(o, l, sb_idB[:], start=True, stop=True),
                                     sT if k2 == len(grp) - 1 else None, 1)
                            evs.append((bnk, sT.val, gstart, len(grp)))
                        ptmm_of[h] = sT.val
                        for bnk, mmv, gstart, glen in evs:
                            P.wait("scalar", sT, mmv)
                            ev = P.do("scalar", lambda e,
                                      d=a16(EP_PT + gstart * 128, 128 * glen),
                                      s=bank(bnk, 128 * glen): e.copy(d, s),
                                      dep=(gstart == 0))
                            bank_read_done(bnk, sS, ev)
                        pt_s = sS.val
                        P.wait("tensor", sS, pt_s)
                        P.wait("tensor", sS, prep_s)
                        P.wait_t("tensor", P.bank_free[bA])
                        for qh, jtts in ((0, jtts0), (1, jtts1)):
                            nch = len(jtts)
                            for ji, jtt in enumerate(jtts):
                                idx = ji if qh == 0 else nch0 + ji
                                P.do("tensor", lambda e,
                                     o=bank(bA, 128, off=qh * 128, pr=64, po=po),
                                     l=a16(R_V + (jtt % 4) * 512 + h * 64, 64),
                                     r=a16(EP_PT + idx * 128, 128),
                                     st=(ji == 0), sp=(ji == nch - 1):
                                     e.matmul(o, l, r, start=st, stop=sp),
                                     sT if ji == nch - 1 else None, 1)
                        pv_mm = sT.val
                        P.wait("scalar", sT, pv_mm)
                        ev = P.do("scalar", lambda e,
                                  d=a16(R_AT + pt * 2048 + w * 256, 256, pr=64, po=po),
                                  s=bank(bA, 256, pr=64, po=po): e.copy(d, s))
                        bank_read_done(bA, sS, ev)

                    s_mm_of = {}
                    msk_v_of = {}
                    exp_s_of = {}
                    ptmm_of = {}
                    norm_v_of = {}
                    for i in range(11):
                        if i < 8:
                            s_mm_of[i] = sim_S(i)
                        if 1 <= i <= 8:
                            sim_max(i - 1)
                            sim_exp(i - 1)
                        if 2 <= i <= 9:
                            sim_norm(i - 2)
                        if i >= 3:
                            sim_ptpv(i - 3)
                    if w > 0:
                        for tt in range(2):
                            state["v_free"][((w - 1) * 2 + tt) % 4] = (sT, sT.val)
                    state["tab_free"][tsl] = (sV, sV.val)
                P.wait("tensor", qWO, wo_ld)
                P.wait("tensor", sS, sS.val)
                P.wait("vector", qRS, rs)
                for tg in range(4):
                    for mc in range(DC):
                        b = 5 + ((tg * DC + mc) % 2)
                        pairs = [(a16(R_WO + kc * 1024 + mc * 128, 128),
                                  a16(R_AT + kc * 2048 + tg * 512, 512))
                                 for kc in range(4)]
                        mmv = mm_acc(b, 512, pairs)
                        P.wait("vector", sT, mmv)
                        dstr = a32(R_RES + mc * T + tg * 512, 512)
                        ev = P.do("vector", lambda e, d=dstr, p_=bank(b):
                                  e.tensor_tensor(d, d, p_, ALU.add))
                        bank_read_done(b, sV, ev)
                state["wo_free"] = (sT, sT.val)
                for sl4 in range(4):
                    state["v_free"][sl4] = (sT, sT.val)
            res_after_attn = sV.val
            if attn_only and ell == nlayers - 1:
                state["res_tok"] = (sV, sV.val)
                state["smalls_free"] = (sV, sV.val)
                return

            P.wait("scalar", sV, res_after_attn)
            for c4 in range(4):
                P.do("scalar", lambda e, d=a32(sm["bnd"] + c4, 1),
                     s=a32(R_RES + (4 + c4) * T + 1023, 1): e.copy(d, s))
            bnd_s = sS.val
            for half in range(2):
                hbase = half * 1024
                for tg2 in range(2):
                    tgb = hbase + tg2 * 512
                    P.wait("scalar", sV, res_after_attn if half == 0 else sV.val)
                    for dc in range(DC):
                        dst = a16(XSF + dc * 512, 512)
                        if dc < 4:
                            P.do("scalar", lambda e, d=dst,
                                 s=a32(R_RES + dc * T + tgb, 512): e.copy(d, s))
                        elif tgb == 0:
                            P.do("vector", lambda e, d=a16(XSF + dc * 512, 1):
                                 e.memset(d, 0.0))
                            P.do("scalar", lambda e, d=a16(XSF + dc * 512 + 1, 511),
                                 s=a32(R_RES + dc * T, 511): e.copy(d, s))
                        elif tgb == 1024:
                            P.wait("scalar", sS, bnd_s)
                            P.do("scalar", lambda e, d=a16(XSF + dc * 512, 1),
                                 s=a32(sm["bnd"] + dc - 4, 1): e.copy(d, s))
                            P.do("scalar", lambda e, d=a16(XSF + dc * 512 + 1, 511),
                                 s=a32(R_RES + dc * T + tgb, 511): e.copy(d, s))
                        else:
                            P.do("scalar", lambda e, d=dst,
                                 s=a32(R_RES + dc * T + tgb - 1, 512): e.copy(d, s))
                    xs_s = sS.val
                    xs_v = sV.val
                    P.wait("tensor", sS, xs_s)
                    P.wait("tensor", sV, xs_v)
                    pairs = [(sb_c1024[:], a16(XSF + dc * 512, 512)) for dc in range(DC)]
                    mu_mm = mm_acc(6, 512, pairs, opr=1)
                    P.wait("vector", sT, mu_mm)
                    mu_c = P.do("vector", lambda e, o=a32(sm["murow"], 512, pr=1),
                                i=bank(6, 512, pr=1): e.tensor_copy(o, i))
                    bank_read_done(6, sV, mu_c)
                    P.wait_t("tensor", P.bank_free[7])
                    P.wait("tensor", sV, mu_c)
                    P.do("tensor", lambda e, o=bank(7, 512),
                         l=sb_row1[:], r=a32(sm["murow"], 512, pr=1):
                         e.matmul(o, l, r, start=True, stop=True), sT, 1)
                    mub_mm = sT.val
                    P.wait("vector", sT, mub_mm)
                    for dc in range(DC):
                        P.do("vector", lambda e, o=a32(R_A + dc * 512, 512),
                             i=a16(XSF + dc * 512, 512), j=bank(7, 512):
                             e.tensor_tensor(o, i, j, ALU.subtract), dep=(dc == 0))
                    p1_v = sV.val
                    bank_read_done(7, sV, p1_v)
                    P.wait("scalar", sV, p1_v)
                    P.wait_t("tensor", P.bank_free[6])
                    P.wait("scalar", sT, sT.val)
                    for dc in range(DC):
                        P.do("scalar", lambda e, o=a16(R_W + dc * 512, 512),
                             i=a32(R_A + dc * 512, 512): e.activation(o, i, AF.Square),
                             dep=(dc == 0))
                    sq_s = sS.val
                    P.wait("tensor", sS, sq_s)
                    for dc in range(DC):
                        P.do("tensor", lambda e, o=bank(6, 512, pr=1),
                             l=sb_c1024[:], r=a16(R_W + dc * 512, 512),
                             st=(dc == 0), sp=(dc == DC - 1):
                             e.matmul(o, l, r, start=st, stop=sp),
                             sT if dc == DC - 1 else None, 1)
                    var_mm = sT.val
                    P.wait("vector", sT, var_mm)
                    vr_c = P.do("vector", lambda e, o=a32(sm["rstdrow"], 512, pr=1),
                                i=bank(6, 512, pr=1): e.tensor_copy(o, i))
                    bank_read_done(6, sV, vr_c)
                    P.wait("scalar", sV, vr_c)
                    P.do("scalar", lambda e, o=a32(sm["rstdrow"], 512, pr=1),
                         i=a32(sm["rstdrow"], 512, pr=1):
                         e.activation(o, i, AF.Ln, bias=sb_eps[0:1, :]))
                    rst_s = P.do("scalar", lambda e, o=a32(sm["rstdrow"], 512, pr=1),
                                 i=a32(sm["rstdrow"], 512, pr=1):
                                 e.activation(o, i, AF.Exp, scale=-0.5))
                    P.wait_t("tensor", P.bank_free[7])
                    P.wait("tensor", sS, rst_s)
                    P.do("tensor", lambda e, o=bank(7, 512),
                         l=sb_row1[:], r=a32(sm["rstdrow"], 512, pr=1):
                         e.matmul(o, l, r, start=True, stop=True), sT, 1)
                    ab_mm = sT.val
                    P.wait("vector", sT, ab_mm)
                    P.wait("vector", sS, smalls_s)
                    for dc in range(DC):
                        P.do("vector", lambda e, o=a32(R_A + dc * 512, 512),
                             i=a32(R_A + dc * 512, 512), j=bank(7, 512):
                             e.tensor_tensor(o, i, j, ALU.mult), dep=(dc == 0))
                        P.do("vector", lambda e, o=a16(R_AT + dc * 1024 + tg2 * 512, 512),
                             i=a32(R_A + dc * 512, 512), g=a32(sm["gcol"] + dc, 1),
                             bb=a32(sm["bcol"] + dc, 1):
                             e.tensor_scalar(o, i, g, bb, ALU.mult, ALU.add))
                    xn_v = sV.val
                    bank_read_done(7, sV, xn_v)
                xn_done = sV.val

                P.wait("gpsimd", sV, xn_done)
                for ci in range(NKC):
                    wci = 128 if ci < NKC - 1 else DI - 128 * (NKC - 1)
                    slot = ci % 2
                    P.wait_t("gpsimd", state["w1_free"][slot])
                    for gside in range(2):
                        c0 = gside * DI + ci * 128
                        src = w1_d[ell][:, c0:c0 + wci] \
                            .rearrange("(c p) m -> p c m", p=128)
                        dst = a16(W1RING[slot] + gside * 1024, DC * wci) \
                            .rearrange("p (c m) -> p c m", c=DC)
                        P.do("gpsimd", lambda e, d=dst, s=src: e.dma_start(out=d, in_=s),
                             qW1[slot], 16)
                    w1_ld = qW1[slot].val
                    P.wait("tensor", qW1[slot], w1_ld)
                    P.wait("tensor", sV, xn_done)
                    for tg2 in range(2):
                        for gside in range(2):
                            pairs = [(a16(W1RING[slot] + gside * 1024 + dc * wci, wci),
                                      a16(R_AT + dc * 1024 + tg2 * 512, 512))
                                     for dc in range(DC)]
                            mm_acc((ci % 2) * 4 + tg2 * 2 + gside, 512, pairs, opr=wci)
                    w1_mm = sT.val
                    state["w1_free"][slot] = (sT, w1_mm)
                    for tg2 in range(2):
                        ba, bg = (ci % 2) * 4 + tg2 * 2, (ci % 2) * 4 + tg2 * 2 + 1
                        P.wait("scalar", sT, w1_mm)
                        P.wait("scalar", sV, sV.val)
                        gl = P.do("scalar", lambda e, o=a16(R_V + 512, 512, pr=wci),
                                  i=bank(bg, 512, pr=wci),
                                  bb=a32(sm["b1g"] + ci, 1, pr=wci):
                                  e.activation(o, i, AF.Gelu, bias=bb))
                        bank_read_done(bg, sS, gl)
                        P.wait("vector", sS, gl)
                        mv = P.do("vector", lambda e,
                                  o=a16(MID[ci] + tg2 * 512, 512, pr=wci),
                                  i=bank(ba, 512, pr=wci),
                                  bb=a32(sm["b1a"] + ci, 1, pr=wci),
                                  g=a16(R_V + 512, 512, pr=wci):
                                  e.scalar_tensor_tensor(o, i, bb, g, ALU.add, ALU.mult))
                        bank_read_done(ba, sV, mv)
                mid_v = sV.val
                mid_s = sS.val

                for tg2 in range(2):
                    tgb = hbase + tg2 * 512
                    w2_done = [None] * DC
                    for kc in range(NKC):
                        wkc = 128 if kc < NKC - 1 else DI - 128 * (NKC - 1)
                        slot = kc % 4
                        P.wait_t("gpsimd", state["w2_free"][slot])
                        src = w2_d[ell, kc * 128:kc * 128 + wkc, :]
                        wd = P.do("gpsimd", lambda e, d=a16(W2RING[slot], 1024, pr=wkc),
                                  s=src: e.dma_start(out=d, in_=s), qW2[slot], 16)
                        P.wait("tensor", qW2[slot], wd)
                        if kc == 0:
                            P.wait("tensor", sV, mid_v)
                            P.wait("tensor", sS, mid_s)
                            for b2_ in range(8):
                                P.wait_t("tensor", P.bank_free[b2_])
                        for mc in range(DC):
                            tok = P.do("tensor", lambda e, o=bank(mc, 512),
                                       l=a16(W2RING[slot] + mc * 128, 128, pr=wkc),
                                       r=a16(MID[kc] + tg2 * 512, 512, pr=wkc),
                                       st=(kc == 0), sp=(kc == NKC - 1):
                                       e.matmul(o, l, r, start=st, stop=sp),
                                       sT if (mc == DC - 1 or kc == NKC - 1) else None, 1)
                            if kc == NKC - 1:
                                w2_done[mc] = tok
                        state["w2_free"][slot] = (sT, sT.val)
                    for mc in range(DC):
                        P.wait("vector", sT, w2_done[mc])
                        dstr = a32(R_RES + mc * T + tgb, 512)
                        ev = P.do("vector", lambda e, d=dstr, p_=bank(mc),
                                  bb=a32(sm["b2col"] + mc, 1):
                                  e.scalar_tensor_tensor(d, p_, bb, d, ALU.add, ALU.add),
                                  dep=(mc == 0))
                        bank_read_done(mc, sV, ev)
            state["res_tok"] = (sV, sV.val)
            state["smalls_free"] = (sV, sV.val)
            for sl4 in range(4):
                state["v_free"][sl4] = (sT, sT.val)

        for ell in range(nlayers):
            emit_inner(ell)

        if dump:
            P.wait("sync", sV, sV.val)
            P.wait("sync", sS, sS.val)
            P.do("sync", lambda e: e.dma_start(out=ydbg_d, in_=a32(R_RES, DC * T)),
                 sO, 16)
            P.do("sync", lambda e: e.wait_ge(sO.h, sO.val))
        elif mode == "full":
            P.wait("sync", sV, sV.val)
            P.wait("sync", sS, sS.val)
            P.wait("tensor", sV, sV.val)
            P.wait("tensor", sS, sS.val)
            P.wait("scalar", sV, sV.val)
            P.wait("vector", sS, sS.val)
            P.wait("sync", qST, qST.val)
            for qt in range(NT):
                xd = P.do("sync", lambda e, d=a32(O_X0R[0], 1024),
                          s=x1s_d[qt * 128:(qt + 1) * 128, :]:
                          e.dma_start(out=d, in_=s), qX0[0], 16)
                for dcg in range(2):
                    b = dcg
                    P.wait_t("tensor", P.bank_free[b])
                    for k in range(4):
                        dc = dcg * 4 + k
                        src = a32(R_RES + dc * T + qt * 128, 128)
                        P.do("tensor", lambda e, o=bank(b, 128, off=k * 128), s=src:
                             e.matmul(o, s, sb_idF[:], start=True, stop=True),
                             sT if k == 3 else None, 1)
                    mmv = sT.val
                    P.wait("vector", sT, mmv)
                    P.wait("vector", qX0[0], xd)
                    ev = P.do("vector", lambda e, o=a32(O_ST + dcg * 512, 512),
                              p_=bank(b), xs=a32(O_X0R[0] + dcg * 512, 512):
                              e.tensor_tensor(o, p_, xs, ALU.add), sV, 1)
                    bank_read_done(b, sV, ev)
                P.wait("vector", sV, sV.val)
                stile = a32(O_ST, 1024)
                negmu = a32(sm["t1"], 1)
                P.do("vector", lambda e, o=negmu, i=stile: e.tensor_reduce(
                    o, i, AX.X, ALU.add, negate=True), sV, 1)
                P.wait("vector", sV, sV.val)
                P.do("vector", lambda e, o=negmu, i=negmu: e.tensor_scalar(
                    o, i, 1.0 / 1024.0, None, ALU.mult), sV, 1)
                mu_v = sV.val
                P.wait("scalar", sV, mu_v)
                ssq = a32(sm["t2"], 1)
                P.do("scalar", lambda e, o=a32(O_X1T, 1024), i=stile, b=negmu, sx=ssq:
                     e.activation(o, i, AF.Square, bias=b, accum_out=sx), sS, 1)
                P.do("scalar", lambda e, o=ssq, i=ssq: e.activation(
                    o, i, AF.Ln, scale=1.0 / 1024.0, bias=sb_eps[:]), sS, 1)
                P.do("scalar", lambda e, o=ssq, i=ssq: e.activation(
                    o, i, AF.Exp, scale=-0.5), sS, 1)
                rstd_s = sS.val
                P.wait("vector", sS, rstd_s)
                P.do("vector", lambda e, o=a32(sm["t0"], 1), i=negmu: e.tensor_tensor(
                    o, i, ssq, ALU.mult), sV, 1)
                m2_v = sV.val
                P.wait("vector", sV, m2_v)
                yv = P.do("vector", lambda e, o=a32(O_X1T, 1024), i=stile:
                          e.tensor_scalar(o, i, ssq, a32(sm["t0"], 1), ALU.mult, ALU.add),
                          sV, 1)
                P.wait("sync", sV, yv)
                P.do("sync", lambda e, s=a32(O_X1T, 1024),
                     d=y_d[qt * 128:(qt + 1) * 128, :]: e.dma_start(out=d, in_=s),
                     sO, 16)
                P.wait("tensor", sV, yv)
                P.wait("scalar", sO, sO.val)
                P.wait("sync", sV, sV.val)
            P.do("sync", lambda e: e.wait_ge(sO.h, sO.val))
        else:
            P.do("sync", lambda e: e.wait_ge(sO.h, sO.val))

        P.flush(block)
    return nc



def _in_maps(inputs, consts):
    x = np.ascontiguousarray(np.asarray(inputs["L0_attn_embeddings"], dtype=np.float32))
    base = {
        "wq": np.ascontiguousarray(np.asarray(inputs["L0_attn_W_Q"], np.float32)),
        "wk": np.ascontiguousarray(np.asarray(inputs["L0_attn_W_K"], np.float32)),
        "wv": np.ascontiguousarray(np.asarray(inputs["L0_attn_W_V"], np.float32)),
        "wqkv": np.ascontiguousarray(np.asarray(inputs["L0_ffn_Wqkv"], np.float32)),
        "qsc": np.ascontiguousarray(np.asarray(inputs["L0_ffn_q_scale"], np.float32)),
        "ksc": np.ascontiguousarray(np.asarray(inputs["L0_ffn_k_scale"], np.float32)),
        "wo": np.ascontiguousarray(np.asarray(inputs["L0_ffn_Wo"], np.float32)),
        "lng": np.ascontiguousarray(np.asarray(inputs["L0_ffn_ln_g"], np.float32)),
        "lnb": np.ascontiguousarray(np.asarray(inputs["L0_ffn_ln_b"], np.float32)),
        "w1": np.ascontiguousarray(np.asarray(inputs["L0_ffn_W1"], np.float32)),
        "b1": np.ascontiguousarray(np.asarray(inputs["L0_ffn_b1"], np.float32)),
        "w2": np.ascontiguousarray(np.asarray(inputs["L0_ffn_W2"], np.float32)),
        "b2": np.ascontiguousarray(np.asarray(inputs["L0_ffn_b2"], np.float32)),
    }
    base.update(consts)
    maps = []
    for c in range(NB):
        m = dict(base)
        m["x0"] = np.ascontiguousarray(x[c])
        maps.append(m)
    return maps


def kernel(**inputs):
    consts = _host_consts()
    nc = build_program(depth=DEPTH, mode="full")
    maps = _in_maps(inputs, consts)
    res = run_bass_kernel_spmd(nc, maps, core_ids=list(range(NB)))
    out = np.stack([np.asarray(r["y"], dtype=np.float32) for r in res.results])
    return out


# version 15
# speedup vs baseline: 1.1261x; 1.0154x over previous
import numpy as np
import ml_dtypes
from contextlib import ExitStack

import concourse.bass as bass
import concourse.mybir as mybir
from concourse.bass_utils import run_bass_kernel_spmd

F32 = mybir.dt.float32
BF16 = mybir.dt.bfloat16
AF = mybir.ActivationFunctionType
ALU = mybir.AluOpType
AX = mybir.AxisListType

NB = 8
T = 2048
D = 1024
DEPTH = 6
NHEAD = 16
DH = 64
WIN = 256
DI = 2730
NW = T // WIN
NT = T // 128
DC = D // 128
HG = 2
HGH = NHEAD // HG
NEG = -3.0e37
NKC = (DI + 127) // 128

MG0, MG1, MW00, MW01, MTRI = 0, 384, 896, 1152, 1408


def _host_consts():
    half = np.arange(0, DH, 2, dtype=np.float64)
    inv_freq = 1.0 / (10000.0 ** (half / DH))
    t = np.arange(T, dtype=np.float64)
    freqs = np.concatenate([t[:, None] * inv_freq[None, :]] * 2, -1)
    base_scale = (half + 0.4 * DH) / (1.4 * DH)
    power = (t - T // 2) / (WIN // 2)
    scale = np.concatenate([base_scale[None, :] ** power[:, None]] * 2, -1)
    cos = np.cos(freqs); sin = np.sin(freqs)
    sign = np.where(np.arange(DH) < DH // 2, -1.0, 1.0)[None, :]
    rot = lambda a: np.roll(a, -32, axis=1)
    rep = lambda a: np.tile(a, (1, HGH)).astype(ml_dtypes.bfloat16)
    tables = np.stack([rep(cos * scale), rep(rot(sign * sin * scale)),
                       rep(cos / scale), rep(rot(sign * sin / scale))])
    p = np.arange(128)[:, None]

    def band(ncols, lo, hi):
        j = np.arange(ncols)[None, :]
        return np.where((j >= lo) & (j <= hi), 0.0, NEG).astype(np.float32)
    masks = np.concatenate([
        band(384, p, p + WIN),
        band(512, p + 128, p + 384),
        band(256, 0 * p, p),
        band(256, 0 * p, p + 128),
        band(128, 0 * p, p),
    ], axis=1)
    return dict(
        tables=np.ascontiguousarray(tables),
        masks=masks,
        masksb=masks.astype(ml_dtypes.bfloat16),
        identF=np.eye(128, dtype=np.float32),
        identB=np.eye(128, dtype=np.float32).astype(ml_dtypes.bfloat16),
        c1024=np.full((128, 1), 1.0 / 1024.0, dtype=ml_dtypes.bfloat16),
        row8=np.full((1, 128), 8.0, dtype=np.float32),
        row1=np.full((1, 128), 1.0, dtype=np.float32),
        epscol=np.full((128, 1), 1.0e-5, dtype=np.float32),
        eps24col=np.full((128, 1), 1.0e-24, dtype=np.float32),
    )


class SemC:
    __slots__ = ("h", "val")
    def __init__(self, h):
        self.h = h
        self.val = 0


ENGS = ("sync", "gpsimd", "scalar", "vector", "tensor")


class Prog:
    def __init__(self):
        self.ops = {e: [] for e in ENGS}
        self.waited = {}
        self.esem = {}
        self.bank_free = [(None, 0)] * 8
        self.bank_ready = [(None, 0)] * 8

    def do(self, eng, fn, inc=None, n=1, dep=True):
        if eng in ("vector", "scalar") or (eng == "gpsimd" and n != 16):
            es = self.esem[eng]
            assert inc is None or inc is es
            if dep:
                self.wait(eng, es, es.val)
            inc, n = es, 1
        if inc is None:
            self.ops[eng].append(fn)
            return None
        inc.val += n
        h, v = inc.h, inc.val
        self.ops[eng].append(lambda e, fn=fn, h=h, n=n: fn(e).then_inc(h, n))
        return v

    def wait(self, eng, sem, val):
        if sem is None or val is None or val <= 0:
            return
        key = (eng, id(sem))
        if self.waited.get(key, 0) >= val:
            return
        self.waited[key] = val
        h = sem.h
        self.ops[eng].append(lambda e, h=h, val=val: e.wait_ge(h, val))

    def wait_t(self, eng, tok):
        if tok is not None:
            self.wait(eng, tok[0], tok[1])

    def flush(self, block):
        for eng in ENGS:
            ops = self.ops[eng]
            if not ops:
                continue
            def body(e, ops=ops):
                for fn in ops:
                    fn(e)
            getattr(block, eng)(body)


def build_program(depth=DEPTH, mode="full"):
    nc = bass.Bass("TRN2", target_bir_lowering=False, debug=False)

    def din(name, shape, dt=F32):
        return nc.dram_tensor(name, list(shape), dt, kind="ExternalInput").ap()

    x0 = din("x0", [T, D])
    wq_d = din("wq", [D, D]); wk_d = din("wk", [D, D]); wv_d = din("wv", [D, D])
    wqkv_d = din("wqkv", [DEPTH, D, 3 * D])
    qsc_d = din("qsc", [DEPTH, DH]); ksc_d = din("ksc", [DEPTH, DH])
    wo_d = din("wo", [DEPTH, D, D])
    lng_d = din("lng", [DEPTH, D]); lnb_d = din("lnb", [DEPTH, D])
    w1_d = din("w1", [DEPTH, D, 2 * DI]); b1_d = din("b1", [DEPTH, 2 * DI])
    w2_d = din("w2", [DEPTH, DI, D]); b2_d = din("b2", [DEPTH, D])
    tbl_d = din("tables", [4, T, HGH * DH], BF16)
    masks_d = din("masks", [128, 1536])
    masksb_d = din("masksb", [128, 1536], BF16)
    identF_d = din("identF", [128, 128])
    identB_d = din("identB", [128, 128], BF16)
    c1024_d = din("c1024", [128, 1], BF16)
    row8_d = din("row8", [1, 128]); row1_d = din("row1", [1, 128])
    eps_d = din("epscol", [128, 1])
    eps24_d = din("eps24col", [128, 1])

    y_d = nc.dram_tensor("y", [T, D], F32, kind="ExternalOutput").ap()
    x1s_d = nc.dram_tensor("x1s", [T, D], F32, kind="Internal").ap()
    rstash_d = nc.dram_tensor("rstash", [128, DC * T], F32, kind="Internal").ap()
    dump = mode.startswith("L") or mode.startswith("A")
    attn_only = mode.startswith("A")
    nlayers = depth if mode in ("full",) else (0 if mode == "outer" else int(mode[1:]) + 1)
    if dump:
        ydbg_d = nc.dram_tensor("ydbg", [128, DC * T], F32, kind="ExternalOutput").ap()

    with ExitStack() as ctx:
        A32 = ctx.enter_context(nc.sbuf_tensor("A32", [128, 24064], F32))
        A16 = ctx.enter_context(nc.sbuf_tensor("A16", [128, 50816], BF16))
        sb_idF = ctx.enter_context(nc.sbuf_tensor("sb_idF", [128, 128], F32))
        sb_idB = ctx.enter_context(nc.sbuf_tensor("sb_idB", [128, 128], BF16))
        sb_c1024 = ctx.enter_context(nc.sbuf_tensor("sb_c1024", [128, 1], BF16))
        sb_row8 = ctx.enter_context(nc.sbuf_tensor("sb_row8", [1, 128], F32))
        sb_row1 = ctx.enter_context(nc.sbuf_tensor("sb_row1", [1, 128], F32))
        sb_eps = ctx.enter_context(nc.sbuf_tensor("sb_eps", [128, 1], F32))
        sb_eps24 = ctx.enter_context(nc.sbuf_tensor("sb_eps24", [128, 1], F32))
        sb_mskb = ctx.enter_context(nc.sbuf_tensor("sb_mskb", [128, 1536], BF16))
        PS = ctx.enter_context(nc.psum_tensor("PS", [128, 4096], F32))

        R_RES = 0
        R_MASK = 16384
        R_A = 17920
        R_SM = 22016
        R_W = 0
        R_KT = 12288
        R_AT = 20480
        R_TAB = 28672
        R_V = 36864
        R_XS = 38912
        R_QW = 40960
        R_EP = 44032
        R_WO = 46720
        sm = {}
        _p = [R_SM]
        for nm, c in [("ssq", 32), ("ssk", 16), ("negmax", 8), ("sumexp", 8),
                      ("recip", 8), ("murow", 512), ("rstdrow", 512),
                      ("qsrow", 512), ("gcol", 8), ("bcol", 8), ("b2col", 8),
                      ("b1a", 24), ("b1g", 24), ("t0", 8), ("t1", 8), ("t2", 8),
                      ("bnd", 8)]:
            sm[nm] = _p[0]; _p[0] += c
        assert _p[0] <= R_SM + 2048

        def a32(base, cols, pr=128, po=0):
            return A32[po:po + pr, base:base + cols]

        def a16(base, cols, pr=128, po=0):
            return A16[po:po + pr, base:base + cols]

        def bank(b, cols=512, off=0, pr=128, po=0):
            return PS[po:po + pr, b * 512 + off:b * 512 + off + cols]

        sQ = SemC(ctx.enter_context(nc.semaphore("sQ")))
        qC = SemC(ctx.enter_context(nc.semaphore("qC")))
        qX0 = [SemC(ctx.enter_context(nc.semaphore("qX0a"))),
               SemC(ctx.enter_context(nc.semaphore("qX0b")))]
        qST = SemC(ctx.enter_context(nc.semaphore("qST")))
        sG = SemC(ctx.enter_context(nc.semaphore("sG")))
        sT = SemC(ctx.enter_context(nc.semaphore("sT")))
        sV = SemC(ctx.enter_context(nc.semaphore("sV")))
        sS = SemC(ctx.enter_context(nc.semaphore("sS")))
        sO = SemC(ctx.enter_context(nc.semaphore("sO")))
        sGC = SemC(ctx.enter_context(nc.semaphore("sGC")))

        block = ctx.enter_context(nc.Block())
        P = Prog()
        P.esem = {"vector": sV, "scalar": sS, "gpsimd": sGC}

        def mm_acc(b, cols, pairs, off=0, opr=128, opo=0):
            P.wait_t("tensor", P.bank_free[b])
            n = len(pairs)
            for i, (l, r) in enumerate(pairs):
                P.do("tensor",
                     lambda e, o=bank(b, cols, off, opr, opo), l=l, r=r,
                     st=(i == 0), sp=(i == n - 1):
                     e.matmul(o, l, r, start=st, stop=sp),
                     sT if i == n - 1 else None, 1)
            P.bank_ready[b] = (sT, sT.val)
            return sT.val

        def bank_read_done(b, sem, val):
            P.bank_free[b] = (sem, val)

        for dst, src in [(sb_idF[:], identF_d), (sb_idB[:], identB_d),
                         (sb_c1024[:], c1024_d), (sb_row8[:], row8_d),
                         (sb_row1[:], row1_d), (sb_eps[:], eps_d),
                         (sb_eps24[:], eps24_d),
                         (sb_mskb[:], masksb_d), (a32(R_MASK, 1536), masks_d)]:
            P.do("sync", lambda e, d=dst, s=src: e.dma_start(out=d, in_=s), qC, 16)
        consts_q = qC.val

        OX0T = 0
        OWB = 16384
        OKT = 24576
        OSTG = 40960
        OQR = 16384
        OVR = 0
        OE = 18432
        OPT = 20480
        O_X0R = [R_A, R_A + 1024]
        O_ST = R_A + 2048
        O_X1T = R_A + 3072

        qts_d = nc.dram_tensor("qts", [NT, 128, D], BF16, kind="Internal").ap()
        vs_d = nc.dram_tensor("vs", [NT, 128, D], BF16, kind="Internal").ap()
        qQS = [SemC(ctx.enter_context(nc.semaphore("qQSa"))),
               SemC(ctx.enter_context(nc.semaphore("qQSb")))]
        qQR = [SemC(ctx.enter_context(nc.semaphore("qQRa"))),
               SemC(ctx.enter_context(nc.semaphore("qQRb")))]
        qVR = [SemC(ctx.enter_context(nc.semaphore(f"qVR{i}"))) for i in range(4)]

        x0_ld = []
        P.wait("tensor", qC, consts_q)
        ring_free = [None, None]
        for tt in range(NT):
            P.wait_t("sync", ring_free[tt % 2])
            dst = a32(O_X0R[tt % 2], 1024)
            src = x0[tt * 128:(tt + 1) * 128, :]
            x0_ld.append(P.do("sync", lambda e, d=dst, s=src: e.dma_start(out=d, in_=s),
                              qX0[tt % 2], 16))
            P.wait("tensor", qX0[tt % 2], x0_ld[tt])
            P.wait_t("tensor", ring_free[tt % 2])
            for dcg in range(2):
                b = dcg
                P.wait_t("tensor", P.bank_free[b])
                for k in range(4):
                    dc = dcg * 4 + k
                    src2 = a32(O_X0R[tt % 2], 1024)[:, dc * 128:dc * 128 + 128]
                    P.do("tensor",
                         lambda e, o=bank(b, 128, off=k * 128), s=src2:
                         e.matmul(o, s, sb_idF[:], start=True, stop=True),
                         sT if k == 3 else None, 1)
                mmv = sT.val
                P.wait("scalar", sT, mmv)
                last = None
                for k in range(4):
                    dc = dcg * 4 + k
                    dst2 = a16(OX0T + dc * T + tt * 128, 128)
                    last = P.do("scalar",
                                lambda e, d=dst2, s=bank(b, 128, off=k * 128):
                                e.copy(d, s), sS, 1, dep=(k == 0))
                bank_read_done(b, sS, last)
            ring_free[tt % 2] = (sS, sS.val)
        x0T_done = (sS, sS.val)

        def wload(w_ap, prev_consumed=None):
            if prev_consumed is not None:
                P.wait_t("gpsimd", prev_consumed)
            for dc in range(DC):
                src = w_ap[dc * 128:(dc + 1) * 128, :]
                dst = a16(OWB + dc * 1024, 1024)
                P.do("gpsimd", lambda e, d=dst, s=src: e.dma_start(out=d, in_=s), sG, 16)
            return sG.val

        wld = wload(wk_d)
        P.wait("tensor", sG, wld)
        P.wait_t("tensor", x0T_done)
        for mc in range(DC):
            for tg in range(4):
                b = 2 + ((mc * 4 + tg) % 2)
                pairs = [(a16(OWB + dc * 1024 + mc * 128, 128),
                          a16(OX0T + dc * T + tg * 512, 512)) for dc in range(DC)]
                mmv = mm_acc(b, 512, pairs)
                P.wait("scalar", sT, mmv)
                ev = P.do("scalar", lambda e, d=a16(OKT + mc * T + tg * 512, 512),
                          s=bank(b): e.copy(d, s), sS, 1)
                bank_read_done(b, sS, ev)
        kt_done = sS.val

        stg_free = [None, None]

        def proj_to_hbm(which, wld_mark):
            P.wait("tensor", sG, wld_mark)
            n_out = NT * 2 if which == "v" else DC * 4
            for i in range(n_out):
                b = 2 + (i % 2)
                sl = i % 2
                if which == "v":
                    tt, vg = i // 2, i % 2
                    tt, vg = divmod(i, 2)
                    pairs = [(a16(OX0T + dc * T + tt * 128, 128),
                              a16(OWB + dc * 1024 + vg * 512, 512)) for dc in range(DC)]
                else:
                    mc, tg = divmod(i, 4)
                    pairs = [(a16(OWB + dc * 1024 + mc * 128, 128),
                              a16(OX0T + dc * T + tg * 512, 512)) for dc in range(DC)]
                mmv = mm_acc(b, 512, pairs)
                P.wait("scalar", sT, mmv)
                P.wait_t("scalar", stg_free[sl])
                ev = P.do("scalar", lambda e, d=a16(OSTG + sl * 512, 512), s=bank(b):
                          e.copy(d, s), sS, 1)
                bank_read_done(b, sS, ev)
                P.wait("sync", sS, ev)
                if which == "v":
                    dd = vs_d[tt, :, vg * 512:(vg + 1) * 512]
                else:
                    mc, tg = divmod(i, 4)
                    dd = qts_d[tg * 4:(tg + 1) * 4, :, mc * 128:(mc + 1) * 128]                         .rearrange("q p c -> p q c")
                dmam = P.do("sync", lambda e, d=dd, s=a16(OSTG + sl * 512, 512)
                            .rearrange("p (q c) -> p q c", q=4 if which == "q" else 1):
                            e.dma_start(out=d, in_=s), qQS[sl], 16)
                stg_free[sl] = (qQS[sl], dmam)
            return (qQS[0].val, qQS[1].val)

        wld = wload(wv_d, prev_consumed=(sS, kt_done))
        v_done = proj_to_hbm("v", wld)
        wld = wload(wq_d, prev_consumed=(sS, sS.val))
        q_done = proj_to_hbm("q", wld)

        x1_qdma = []
        qr_free = [None, None]
        vloads = {}
        P.wait("sync", qQS[0], q_done[0])
        P.wait("sync", qQS[1], q_done[1])
        for qt in range(NT):
            nk = (qt + 1) * 128
            njg = (nk + 511) // 512
            xd = P.do("sync", lambda e, d=a32(O_X0R[0], 1024),
                      s=x0[qt * 128:(qt + 1) * 128, :]: e.dma_start(out=d, in_=s),
                      qX0[0], 16)
            qsl = qt % 2
            P.wait_t("sync", qr_free[qsl])
            qd = P.do("sync", lambda e, d=a16(OQR + qsl * 1024, 1024),
                      s=qts_d[qt]: e.dma_start(out=d, in_=s), qQR[qsl], 16)
            qs4 = qVR[qt % 4]
            P.wait("sync", qs4, qs4.val)
            vm = P.do("sync", lambda e, d=a16(OVR + qt * 1024, 1024),
                      s=vs_d[qt]: e.dma_start(out=d, in_=s), qs4, 16)
            vloads[qt] = (qs4, vm)
            P.wait("tensor", qQR[qsl], qd)
            for jg in range(njg):
                w_ = min(512, nk - jg * 512)
                pairs = [(a16(OQR + qsl * 1024 + mc * 128, 128),
                          a16(OKT + mc * T + jg * 512, w_)) for mc in range(DC)]
                mm_acc(jg, w_, pairs)
            s_mm = sT.val
            P.wait("vector", sT, s_mm)
            dsl = PS[:, qt * 128:qt * 128 + 128]
            mski = P.do("vector", lambda e, d=dsl: e.tensor_tensor(
                d, d, a32(R_MASK + MTRI, 128), ALU.add), sV, 1)
            P.wait("vector", sV, mski)
            nm = a32(sm["negmax"], 1)
            P.do("vector", lambda e, o=nm, n=nk: e.tensor_reduce(
                o, PS[:, 0:n], AX.X, ALU.max, negate=True), sV, 1)
            red_v = sV.val
            P.wait("scalar", sV, red_v)
            nm32 = a32(sm["t0"], 1)
            nm32_s = P.do("scalar", lambda e, o=nm32, i=nm: e.activation(
                o, i, AF.Copy, scale=1.0 / 32.0), sS, 1)
            P.wait("scalar", sS, nm32_s)
            sxp = a32(sm["sumexp"], 1)
            exp_s = P.do("scalar", lambda e, o=a16(OE, nk), n=nk, b=nm32, sx=sxp:
                         e.activation(o, PS[:, 0:n], AF.Exp, bias=b, scale=1.0 / 32.0,
                                      accum_out=sx), sS, 1)
            for b_ in range(njg):
                bank_read_done(b_, sS, exp_s)
            P.wait("vector", sS, exp_s)
            rc = a32(sm["recip"], 1)
            P.do("vector", lambda e, o=rc, i=sxp: e.reciprocal(o, i), sV, 1)
            rcv = sV.val
            P.wait("vector", sV, rcv)
            pn = P.do("vector", lambda e, o=a16(OE, nk), i=a16(OE, nk), r=rc:
                      e.tensor_scalar(o, i, r, None, ALU.mult), sV, 1)
            P.wait("tensor", sV, pn)
            for jc in range(qt + 1):
                b = 4 + ((jc // 2) % 2)
                off = (jc % 2) * 128
                if jc % 2 == 0:
                    P.wait_t("tensor", P.bank_free[b])
                mm = P.do("tensor", lambda e, o=bank(b, 128, off=off),
                          l=a16(OE + jc * 128, 128):
                          e.matmul(o, l, sb_idB[:], start=True, stop=True), sT, 1)
                if jc % 2 == 1 or jc == qt:
                    P.wait("scalar", sT, mm)
                    wdt = 128 * (2 if jc % 2 == 1 else 1)
                    o0 = (jc // 2) * 256
                    ev = P.do("scalar", lambda e, d=a16(OPT + o0, wdt),
                              s=bank(b, wdt): e.copy(d, s), sS, 1)
                    bank_read_done(b, sS, ev)
            pt_ev = sS.val
            P.wait("tensor", sS, pt_ev)
            for jc in range(qt + 1):
                P.wait_t("tensor", vloads[jc])
            for dg in range(2):
                pairs = [(a16(OPT + jc * 128, 128),
                          a16(OVR + jc * 1024 + dg * 512, 512))
                         for jc in range(qt + 1)]
                mm_acc(6 + dg, 512, pairs)
            pv_mm = sT.val
            qr_free[qsl] = (sT, s_mm)
            P.wait("vector", sT, pv_mm)
            P.wait("vector", qX0[0], xd)
            for dg in range(2):
                ev = P.do("vector", lambda e, o=a32(O_ST + dg * 512, 512),
                          p_=bank(6 + dg), xs=a32(O_X0R[0] + dg * 512, 512):
                          e.tensor_tensor(o, p_, xs, ALU.add), sV, 1)
                bank_read_done(6 + dg, sV, ev)
            st_v = sV.val
            P.wait("vector", sV, st_v)
            stile = a32(O_ST, 1024)
            negmu = a32(sm["t1"], 1)
            P.do("vector", lambda e, o=negmu, i=stile: e.tensor_reduce(
                o, i, AX.X, ALU.add, negate=True), sV, 1)
            P.wait("vector", sV, sV.val)
            P.do("vector", lambda e, o=negmu, i=negmu: e.tensor_scalar(
                o, i, 1.0 / 1024.0, None, ALU.mult), sV, 1)
            mu_v = sV.val
            P.wait("scalar", sV, mu_v)
            ssq = a32(sm["t2"], 1)
            P.do("scalar", lambda e, o=a32(O_X1T, 1024), i=stile, b=negmu, sx=ssq:
                 e.activation(o, i, AF.Square, bias=b, accum_out=sx), sS, 1)
            P.do("scalar", lambda e, o=ssq, i=ssq: e.activation(
                o, i, AF.Ln, scale=1.0 / 1024.0, bias=sb_eps[:]), sS, 1)
            P.do("scalar", lambda e, o=ssq, i=ssq: e.activation(
                o, i, AF.Exp, scale=-0.5), sS, 1)
            rstd_s = sS.val
            P.wait("vector", sS, rstd_s)
            P.do("vector", lambda e, o=a32(sm["t0"], 1), i=negmu, j=ssq:
                 e.tensor_tensor(o, i, j, ALU.mult), sV, 1)
            m2_v = sV.val
            P.wait("vector", sV, m2_v)
            x1v = P.do("vector", lambda e, o=a32(O_X1T, 1024), i=stile, r=ssq,
                       m2=a32(sm["t0"], 1):
                       e.tensor_scalar(o, i, r, m2, ALU.mult, ALU.add), sV, 1)
            P.wait("sync", sV, x1v)
            x1_qdma.append(P.do("sync", lambda e, s=a32(O_X1T, 1024),
                                d=x1s_d[qt * 128:(qt + 1) * 128, :]:
                                e.dma_start(out=d, in_=s), qST, 16))
            if mode == "outer":
                P.do("sync", lambda e, s=a32(O_X1T, 1024),
                     d=y_d[qt * 128:(qt + 1) * 128, :]: e.dma_start(out=d, in_=s),
                     sO, 16)
            P.wait("tensor", sV, x1v)
            for dcg in range(2):
                b = dcg
                P.wait_t("tensor", P.bank_free[b])
                for k in range(4):
                    dc = dcg * 4 + k
                    src = a32(O_X1T, 1024)[:, dc * 128:dc * 128 + 128]
                    P.do("tensor", lambda e, o=bank(b, 128, off=k * 128), s=src:
                         e.matmul(o, s, sb_idF[:], start=True, stop=True),
                         sT if k == 3 else None, 1)
                mmv = sT.val
                P.wait("vector", sT, mmv)
                ev = None
                for k in range(4):
                    dc = dcg * 4 + k
                    dst = a32(R_RES + dc * T + qt * 128, 128)
                    ev = P.do("vector", lambda e, d=dst, s=bank(b, 128, off=k * 128):
                              e.tensor_copy(d, s), sV, 1, dep=(k == 0))
                bank_read_done(b, sV, ev)
            P.wait("sync", sV, st_v)
            P.wait("scalar", qST, qST.val)
            P.wait("scalar", sO, sO.val)
            P.wait("scalar", sT, sT.val)
        outer_done_v = sV.val
        outer_done_s = sS.val
        outer_done_t = sT.val

        QS8B, KS8B = R_A + 2048, R_A + 3072
        TMPA, TMPB, TMPC = R_A, R_A + 512, R_A + 1024
        XQW, XKW, XQT = R_QW, R_QW + 1024, R_QW + 2048
        EP_E = [R_EP, R_EP + 896, R_EP + 1792]
        EP_PT = R_QW
        MID = [R_W + i * 1024 for i in range(12)] + \
              [R_KT + i * 1024 for i in range(8)] + \
              [R_TAB + i * 1024 for i in range(2)]
        W1RING = [R_TAB + 2048, R_TAB + 4096]
        W2RING = [R_TAB + 6144, R_TAB + 7168, R_EP, R_EP + 1024]
        XSF = R_XS

        qWQ = SemC(ctx.enter_context(nc.semaphore("qWQ")))
        qTB = [SemC(ctx.enter_context(nc.semaphore(f"qTB{i}"))) for i in range(2)]
        qWO = SemC(ctx.enter_context(nc.semaphore("qWO")))
        qW1 = [SemC(ctx.enter_context(nc.semaphore(f"qW1{i}"))) for i in range(2)]
        qW2 = [SemC(ctx.enter_context(nc.semaphore(f"qW2{i}"))) for i in range(4)]
        qMS = SemC(ctx.enter_context(nc.semaphore("qMS")))
        qRS = SemC(ctx.enter_context(nc.semaphore("qRS")))
        qSC = SemC(ctx.enter_context(nc.semaphore("qSC")))
        qXS = SemC(ctx.enter_context(nc.semaphore("qXS")))

        state = dict(tab_free=[None, None], w1_free=[None, None],
                     w2_free=[None] * 4,
                     v_free=[None] * 4, smalls_free=None, res_tok=None, wo_free=None,
                     sq_free=[None] * 3, b_free=[None] * 3, chain_seq=[0])
        SQP = [R_A, R_A + 512, R_MASK]
        BP = [R_A + 1024, R_A + 1536, R_MASK + 512]

        def emit_inner(ell):
            if ell == 0:
                for eng in ("gpsimd", "sync", "tensor", "vector", "scalar"):
                    P.wait(eng, sT, outer_done_t)
                    P.wait(eng, sV, outer_done_v)
                    P.wait(eng, sS, outer_done_s)
                P.wait("sync", qST, qST.val)

            P.wait_t("sync", state["smalls_free"])
            qs_rep = qsc_d[ell][None, None, :].broadcast_to([128, HGH, DH])
            ks_rep = ksc_d[ell][None, None, :].broadcast_to([128, HGH, DH])
            P.wait("sync", qSC, qSC.val)
            for dst, srcp in [
                (a32(QS8B, 512).rearrange("p (a b) -> p a b", a=HGH), qs_rep),
                (a32(KS8B, 512).rearrange("p (a b) -> p a b", a=HGH), ks_rep),
            ]:
                P.do("sync", lambda e, d=dst, s=srcp: e.dma_start(out=d, in_=s), qSC, 16)
            sc_ld = qSC.val
            P.wait("vector", qSC, qSC.val)
            P.do("vector", lambda e, o=a32(QS8B, 512), i=a32(QS8B, 512):
                 e.tensor_scalar(o, i, 8.0, None, ALU.mult))
            qs8_v = sV.val
            P.wait_t("sync", state["res_tok"])
            colspecs = [
                (sm["gcol"], 8, lng_d[ell].rearrange("(c p) -> c p", p=128), None),
                (sm["bcol"], 8, lnb_d[ell].rearrange("(c p) -> c p", p=128), None),
                (sm["b2col"], 8, b2_d[ell].rearrange("(c p) -> c p", p=128), None),
                (sm["b1a"], 22, b1_d[ell, 0:2816].rearrange("(c p) -> c p", p=128), None),
                (sm["b1g"], 22, b1_d[ell, 2730:5418].rearrange("(c p) -> c p", p=128),
                 b1_d[ell, 5418:5460][None, :]),
            ]
            for dcol, ncc, srcp, tail in colspecs:
                ld = P.do("sync", lambda e, d=a32(TMPA, 128, pr=ncc if tail is None
                          else ncc - 1), s=srcp: e.dma_start(out=d, in_=s), qMS, 16)
                if tail is not None:
                    ld = P.do("sync", lambda e, d=a32(TMPA, 42, pr=1, po=ncc - 1),
                              s=tail: e.dma_start(out=d, in_=s), qMS, 16)
                P.wait("tensor", qMS, ld)
                P.wait_t("tensor", P.bank_free[5])
                mm = P.do("tensor", lambda e, o=bank(5, ncc), l=a32(TMPA, 128, pr=ncc),
                          r=sb_idF[0:ncc, 0:ncc]: e.matmul(o, l, r, start=True, stop=True),
                          sT, 1)
                P.wait("scalar", sT, mm)
                ev = P.do("scalar", lambda e, d=a32(dcol, ncc), s=bank(5, ncc):
                          e.copy(d, s))
                bank_read_done(5, sS, ev)
                P.wait("sync", sT, mm)
            smalls_ld = qMS.val
            smalls_s = sS.val

            P.wait_t("sync", state["res_tok"])
            P.wait("sync", qXS, qXS.val)
            rs = P.do("sync", lambda e, d=rstash_d, s=a32(R_RES, DC * T):
                      e.dma_start(out=d, in_=s), qRS, 16)

            qkv_mm_done = [0]
            for hg in range(2):
                P.wait_t("gpsimd", state["res_tok"])
                if hg == 1:
                    P.wait("gpsimd", sT, qkv_mm_done[0])
                for mi, cbase in enumerate((0, D, 2 * D)):
                    src = wqkv_d[ell][:, cbase + hg * 512: cbase + hg * 512 + 512] \
                        .rearrange("(c p) m -> p c m", p=128)
                    dst = a16(R_W + mi * 4096, 4096).rearrange("p (c m) -> p c m", c=DC)
                    P.do("gpsimd", lambda e, d=dst, s=src: e.dma_start(out=d, in_=s),
                         qWQ, 16)
                wqkv_ld = qWQ.val
                P.wait_t("gpsimd", state["wo_free"])
                for kc in range(4):
                    src = wo_d[ell, hg * 512 + kc * 128: hg * 512 + (kc + 1) * 128, :]
                    P.do("gpsimd", lambda e, d=a16(R_WO + kc * 1024, 1024), s=src:
                         e.dma_start(out=d, in_=s), qWO, 16)
                wo_ld = qWO.val
                if hg == 1:
                    P.wait("scalar", sT, sT.val)

                for w in range(NW):
                    tsl = w % 2
                    P.wait_t("sync", state["tab_free"][tsl])
                    for ti in range(4):
                        src = tbl_d[ti, w * 256:(w + 1) * 256, :] \
                            .rearrange("(tt p) c -> p tt c", p=128)
                        dst = a16(R_TAB + tsl * 4096 + ti * 1024, 1024) \
                            .rearrange("p (tt c) -> p tt c", tt=2)
                        P.do("sync", lambda e, d=dst, s=src: e.dma_start(out=d, in_=s),
                             qTB[tsl], 16)
                    tab_ld = qTB[tsl].val

                    P.wait("gpsimd", sT, qkv_mm_done[0])
                    P.wait("gpsimd", qRS, rs)
                    src0 = rstash_d.rearrange("p (c t) -> p c t", c=DC)[
                        :, 0:4, w * 256:(w + 1) * 256]
                    dst0 = a16(R_XS, 1024).rearrange("p (c t) -> p c t", c=4)
                    P.do("gpsimd", lambda e, d=dst0, s=src0: e.dma_start(out=d, in_=s),
                         qXS, 16)
                    if w > 0:
                        src1 = rstash_d.rearrange("p (c t) -> p c t", c=DC)[
                            :, 4:8, w * 256 - 1:(w + 1) * 256 - 1]
                        dst1 = a16(R_XS + 1024, 1024).rearrange("p (c t) -> p c t", c=4)
                        P.do("gpsimd", lambda e, d=dst1, s=src1:
                             e.dma_start(out=d, in_=s), qXS, 16)
                        xs_v = sV.val
                    else:
                        src1 = rstash_d.rearrange("p (c t) -> p c t", c=DC)[
                            :, 4:8, 0:255]
                        dst1 = a16(R_XS + 1024, 1024).rearrange(
                            "p (c t) -> p c t", c=4)[:, :, 1:256]
                        P.do("gpsimd", lambda e, d=dst1, s=src1:
                             e.dma_start(out=d, in_=s), qXS, 16)
                        P.wait("vector", sT, qkv_mm_done[0])
                        for c4 in range(4):
                            P.do("vector", lambda e, d=a16(R_XS + 1024 + c4 * 256, 1):
                                 e.memset(d, 0.0))
                        xs_v = sV.val
                    xs_ld = qXS.val

                    P.wait("tensor", qWQ, wqkv_ld)
                    P.wait("tensor", qXS, xs_ld)
                    P.wait("tensor", sV, xs_v)
                    qkv_mm_of = {}
                    for mi in (0, 1, 2):
                        for tt in range(2):
                            pairs = [(a16(R_XS + dc * 256 + tt * 128, 128),
                                      a16(R_W + mi * 4096 + dc * 512, 512))
                                     for dc in range(DC)]
                            qkv_mm_of[(tt, mi)] = mm_acc(tt * 3 + mi, 512, pairs)
                    qkv_mm = sT.val
                    qkv_mm_done[0] = qkv_mm

                    chains = [(0, 0), (0, 1), (1, 0), (1, 1)]
                    st1_s = {}; st2_v = {}; st3_s = {}; st7_v = {}
                    cpar = {}
                    wp = w % 2

                    def ssq_col(tt_, h):
                        return a32(sm["ssq"] + tt_ * 16 + wp * 8 + h, 1)

                    def sslot(ci):
                        side, tt_ = chains[ci]
                        if side == 0:
                            return a32(sm["ssq"] + tt_ * 16 + wp * 8, 8)
                        return a32(sm["ssk"] + tt_ * 8, 8)

                    for ci in range(4):
                        side, tt_ = chains[ci]
                        b = tt_ * 3 + side
                        par = state["chain_seq"][0] % 3
                        state["chain_seq"][0] += 1
                        cpar[ci] = par
                        P.wait("scalar", sT, qkv_mm_of[(tt_, side)])
                        P.wait_t("scalar", state["sq_free"][par])
                        P.do("scalar", lambda e, o=a32(SQP[par], 512), i=bank(b):
                             e.activation(o, i, AF.Square), dep=False)
                        st1_s[ci] = sS.val
                    for ci in range(4):
                        ssl = sslot(ci)
                        P.wait("vector", sS, st1_s[ci])
                        P.do("vector", lambda e, o=ssl, i=a32(SQP[cpar[ci]], 512)
                             .rearrange("p (h d) -> p h d", h=8):
                             e.tensor_reduce(o, i, AX.X, ALU.add), dep=False)
                        st2_v[ci] = sV.val
                    for ci in range(4):
                        ssl = sslot(ci)
                        P.wait("scalar", sV, st2_v[ci])
                        P.do("scalar", lambda e, o=ssl, i=ssl:
                             e.activation(o, i, AF.Ln, bias=sb_eps24[:]), dep=False)
                        P.do("scalar", lambda e, o=ssl, i=ssl:
                             e.activation(o, i, AF.Exp, scale=-0.5))
                        st3_s[ci] = sS.val

                    def chain_tail(ci):
                        side, tt_ = chains[ci]
                        b = tt_ * 3 + side
                        par = cpar[ci]
                        cosb = R_TAB + tsl * 4096 + (0 if side == 0 else 2) * 1024 + tt_ * 512
                        sinb = R_TAB + tsl * 4096 + (1 if side == 0 else 3) * 1024 + tt_ * 512
                        dstw = (XQW if side == 0 else XKW) + tt_ * 512
                        Bp = a32(BP[par], 512)
                        P.wait_t("vector", state["b_free"][par])
                        P.wait("vector", sT, qkv_mm_of[(tt_, side)])
                        P.wait("vector", qSC, sc_ld)
                        if side == 0:
                            P.wait("vector", sS, st1_s[ci])
                            P.do("vector", lambda e, o=Bp, i=bank(b), j=a32(QS8B, 512):
                                 e.tensor_tensor(o, i, j, ALU.mult), dep=False)
                        else:
                            P.wait("vector", sS, st3_s[ci])
                            ssl = sslot(ci)
                            for h in range(8):
                                P.do("vector", lambda e, o=Bp[:, h * 64:h * 64 + 64],
                                     i=bank(b, 64, off=h * 64), r=ssl[:, h:h + 1],
                                     j=a32(KS8B + h * 64, 64):
                                     e.scalar_tensor_tensor(o, i, r, j, ALU.mult,
                                                            ALU.mult), dep=False)
                        bank_read_done(b, sV, sV.val)
                        P.wait("vector", qTB[tsl], tab_ld)
                        hv = lambda base, lo: a32(base, 512).rearrange(
                            "p (h x) -> p h x", h=8)[:, :, lo:lo + 32]
                        hv16 = lambda base, lo: a16(base, 512).rearrange(
                            "p (h x) -> p h x", h=8)[:, :, lo:lo + 32]
                        P.do("vector", lambda e, o=hv(SQP[par], 0), i=hv(BP[par], 32),
                             j=hv16(sinb, 32): e.tensor_tensor(o, i, j, ALU.mult))
                        P.do("vector", lambda e, o=hv(SQP[par], 32), i=hv(BP[par], 0),
                             j=hv16(sinb, 0): e.tensor_tensor(o, i, j, ALU.mult),
                             dep=False)
                        P.do("vector", lambda e, o=Bp, i=Bp, j=a16(cosb, 512):
                             e.tensor_tensor(o, i, j, ALU.mult))
                        P.do("vector", lambda e, o=a16(dstw, 512), i=Bp,
                             j=a32(SQP[par], 512): e.tensor_tensor(o, i, j, ALU.add))
                        st7_v[ci] = sV.val
                        state["sq_free"][par] = (sV, sV.val)
                        state["b_free"][par] = (sV, sV.val)

                    def qk_transpose(side, bnk, ready_v):
                        P.wait("tensor", sV, ready_v)
                        for f0 in (0, 2):
                            P.wait_t("tensor", P.bank_free[bnk])
                            for k2, (fp, tt_) in enumerate(
                                    [(f0, 0), (f0, 1), (f0 + 1, 0), (f0 + 1, 1)]):
                                srcb = (XQW if side == 0 else XKW) + tt_ * 512
                                P.do("tensor", lambda e, o=bank(bnk, 128, off=k2 * 128),
                                     l=a16(srcb + fp * 128, 128):
                                     e.matmul(o, l, sb_idB[:], start=True, stop=True),
                                     sT if k2 == 3 else None, 1)
                            mmv = sT.val
                            P.wait("scalar", sT, mmv)
                            if side == 0:
                                dst = a16(XQT + f0 * 256, 512)
                            else:
                                dst = A16[:, R_KT:R_KT + 4 * T].rearrange(
                                    "p (f x) -> p f x", f=4)[
                                    :, f0:f0 + 2, w * 256:w * 256 + 256]
                            ev = P.do("scalar", lambda e, d=dst, s=bank(bnk, 512):
                                      e.copy(d, s), dep=(f0 == 0))
                            bank_read_done(bnk, sS, ev)

                    chain_tail(0)
                    chain_tail(1)
                    qk_transpose(0, 6, max(st7_v[0], st7_v[1]))
                    chain_tail(2)
                    chain_tail(3)
                    qk_transpose(1, 7, max(st7_v[2], st7_v[3]))
                    for tt in range(2):
                        vb = tt * 3 + 2
                        slot = (w * 2 + tt) % 4
                        P.wait("scalar", sT, qkv_mm_of[(tt, 2)])
                        P.wait_t("scalar", state["v_free"][slot])
                        ev = P.do("scalar", lambda e, o=a16(R_V + slot * 512, 512),
                                  i=bank(vb): e.copy(o, i), dep=(tt == 0))
                        bank_read_done(vb, sS, ev)
                    prep_s = sS.val
                    tr_s = sS.val

                    nj0 = 128 if w == 0 else 384
                    nj1 = 256 if w == 0 else 384
                    m0 = MTRI if w == 0 else MG0
                    m1 = MW01 if w == 0 else MG0
                    kbase0 = 0 if w == 0 else (w - 1) * 256
                    kbase1 = 0 if w == 0 else (w - 1) * 256 + 128
                    jtts0 = [0] if w == 0 else [2 * w - 2, 2 * w - 1, 2 * w]
                    jtts1 = [0, 1] if w == 0 else [2 * w - 1, 2 * w, 2 * w + 1]
                    nch0, nch1 = len(jtts0), len(jtts1)
                    P.wait("tensor", sS, tr_s)

                    def sim_S(h):
                        pt = h // 2
                        po = (h % 2) * 64
                        par = (h % 3) * 2
                        for qh, nj, msk in ((0, nj0, m0), (1, nj1, m1)):
                            bnk = par + qh
                            P.wait_t("tensor", P.bank_free[bnk])
                            P.do("tensor", lambda e, o=bank(bnk, nj),
                                 m=sb_mskb[:, msk:msk + nj]:
                                 e.matmul(o, sb_idB[:], m, start=True, stop=False))
                        for qh, nj, kb in ((0, nj0, kbase0), (1, nj1, kbase1)):
                            bnk = par + qh
                            lhs = a16(XQT + pt * 256 + qh * 128, 128, pr=64, po=po)
                            rhs = a16(R_KT + pt * 2048 + kb, nj, pr=64, po=po)
                            P.do("tensor", lambda e, o=bank(bnk, nj), l=lhs, r=rhs:
                                 e.matmul(o, l, r, start=False, stop=True), sT, 1)
                            P.bank_ready[bnk] = (sT, sT.val)
                        return sT.val

                    def sim_max(h):
                        par = (h % 3) * 2
                        P.wait("vector", sT, s_mm_of[h])
                        for qh, nj in ((0, nj0), (1, nj1)):
                            P.do("vector", lambda e,
                                 o=a32(sm["negmax"] + (h % 3) * 2 + qh, 1),
                                 i=bank(par + qh, nj):
                                 e.tensor_reduce(o, i, AX.X, ALU.max, negate=True),
                                 dep=False)
                        P.wait("vector", sS, max(st3_s[0], st3_s[1]))
                        for qh in range(2):
                            P.do("vector", lambda e,
                                 o=a32(sm["negmax"] + (h % 3) * 2 + qh, 1),
                                 i=a32(sm["negmax"] + (h % 3) * 2 + qh, 1),
                                 r=ssq_col(qh, h):
                                 e.tensor_scalar(o, i, r, None, ALU.mult),
                                 dep=(qh == 0))
                        msk_v_of[h] = sV.val

                    def sim_exp(h):
                        par = (h % 3) * 2
                        P.wait("scalar", sV, msk_v_of[h])
                        if h >= 3:
                            P.wait("scalar", sT, ptmm_of.get(h - 3, 0))
                        for qh, nj in ((0, nj0), (1, nj1)):
                            bnk = par + qh
                            ex = P.do("scalar", lambda e,
                                      o=a16(EP_E[h % 3] + qh * 384, nj),
                                      i=bank(bnk, nj),
                                      b=a32(sm["negmax"] + (h % 3) * 2 + qh, 1),
                                      r=ssq_col(qh, h),
                                      sx=a32(sm["sumexp"] + (h % 3) * 2 + qh, 1):
                                      e.activation(o, i, AF.Exp, bias=b, scale=r,
                                                   accum_out=sx), dep=(qh == 0))
                            bank_read_done(bnk, sS, ex)
                        exp_s_of[h] = sS.val

                    def sim_norm(h):
                        P.wait("vector", sS, exp_s_of[h])
                        P.do("vector", lambda e,
                             o=a32(sm["recip"] + (h % 3) * 2, 2),
                             i=a32(sm["sumexp"] + (h % 3) * 2, 2):
                             e.reciprocal(o, i), dep=False)
                        for qh, nj in ((0, nj0), (1, nj1)):
                            P.do("vector", lambda e, o=a16(EP_E[h % 3] + qh * 384, nj),
                                 i=a16(EP_E[h % 3] + qh * 384, nj),
                                 r=a32(sm["recip"] + (h % 3) * 2 + qh, 1):
                                 e.tensor_scalar(o, i, r, None, ALU.mult),
                                 dep=(qh == 0))
                        norm_v_of[h] = sV.val

                    def sim_ptpv(h):
                        pt = h // 2
                        po = (h % 2) * 64
                        P.wait("tensor", sV, norm_v_of[h])
                        tiles = [(0, jc) for jc in range(nch0)] + \
                                [(1, jc) for jc in range(nch1)]
                        bA = 6 + (h % 2)
                        bB = 7 - (h % 2)
                        evs = []
                        for gi, gstart in enumerate(range(0, len(tiles), 4)):
                            grp = tiles[gstart:gstart + 4]
                            bnk = bA if gi == 0 else bB
                            P.wait_t("tensor", P.bank_free[bnk])
                            for k2, (qh, jc) in enumerate(grp):
                                P.do("tensor", lambda e, o=bank(bnk, 128, off=k2 * 128),
                                     l=a16(EP_E[h % 3] + qh * 384 + jc * 128, 128):
                                     e.matmul(o, l, sb_idB[:], start=True, stop=True),
                                     sT if k2 == len(grp) - 1 else None, 1)
                            evs.append((bnk, sT.val, gstart, len(grp)))
                        ptmm_of[h] = sT.val
                        for bnk, mmv, gstart, glen in evs:
                            P.wait("scalar", sT, mmv)
                            ev = P.do("scalar", lambda e,
                                      d=a16(EP_PT + gstart * 128, 128 * glen),
                                      s=bank(bnk, 128 * glen): e.copy(d, s),
                                      dep=(gstart == 0))
                            bank_read_done(bnk, sS, ev)
                        pt_s = sS.val
                        P.wait("tensor", sS, pt_s)
                        P.wait("tensor", sS, prep_s)
                        P.wait_t("tensor", P.bank_free[bA])
                        for qh, jtts in ((0, jtts0), (1, jtts1)):
                            nch = len(jtts)
                            for ji, jtt in enumerate(jtts):
                                idx = ji if qh == 0 else nch0 + ji
                                P.do("tensor", lambda e,
                                     o=bank(bA, 128, off=qh * 128, pr=64, po=po),
                                     l=a16(R_V + (jtt % 4) * 512 + h * 64, 64),
                                     r=a16(EP_PT + idx * 128, 128),
                                     st=(ji == 0), sp=(ji == nch - 1):
                                     e.matmul(o, l, r, start=st, stop=sp),
                                     sT if ji == nch - 1 else None, 1)
                        pv_mm = sT.val
                        P.wait("scalar", sT, pv_mm)
                        ev = P.do("scalar", lambda e,
                                  d=a16(R_AT + pt * 2048 + w * 256, 256, pr=64, po=po),
                                  s=bank(bA, 256, pr=64, po=po): e.copy(d, s),
                                  dep=False)
                        bank_read_done(bA, sS, ev)

                    s_mm_of = {}
                    msk_v_of = {}
                    exp_s_of = {}
                    ptmm_of = {}
                    norm_v_of = {}
                    for i in range(11):
                        if i < 8:
                            s_mm_of[i] = sim_S(i)
                        if 1 <= i <= 8:
                            sim_max(i - 1)
                            sim_exp(i - 1)
                        if 2 <= i <= 9:
                            sim_norm(i - 2)
                        if i >= 3:
                            sim_ptpv(i - 3)
                    if w > 0:
                        for tt in range(2):
                            state["v_free"][((w - 1) * 2 + tt) % 4] = (sT, sT.val)
                    state["tab_free"][tsl] = (sV, sV.val)
                P.wait("tensor", qWO, wo_ld)
                P.wait("tensor", sS, sS.val)
                P.wait("vector", qRS, rs)
                for tg in range(4):
                    for mc in range(DC):
                        b = 5 + ((tg * DC + mc) % 2)
                        pairs = [(a16(R_WO + kc * 1024 + mc * 128, 128),
                                  a16(R_AT + kc * 2048 + tg * 512, 512))
                                 for kc in range(4)]
                        mmv = mm_acc(b, 512, pairs)
                        P.wait("vector", sT, mmv)
                        dstr = a32(R_RES + mc * T + tg * 512, 512)
                        ev = P.do("vector", lambda e, d=dstr, p_=bank(b):
                                  e.tensor_tensor(d, d, p_, ALU.add),
                                  dep=(tg == 0 and mc == 0))
                        bank_read_done(b, sV, ev)
                state["wo_free"] = (sT, sT.val)
                for sl4 in range(4):
                    state["v_free"][sl4] = (sT, sT.val)
            res_after_attn = sV.val
            if attn_only and ell == nlayers - 1:
                state["res_tok"] = (sV, sV.val)
                state["smalls_free"] = (sV, sV.val)
                return

            P.wait("scalar", sV, res_after_attn)
            for c4 in range(4):
                P.do("scalar", lambda e, d=a32(sm["bnd"] + c4, 1),
                     s=a32(R_RES + (4 + c4) * T + 1023, 1): e.copy(d, s))
            bnd_s = sS.val
            for half in range(2):
                hbase = half * 1024
                for tg2 in range(2):
                    tgb = hbase + tg2 * 512
                    P.wait("scalar", sV, res_after_attn if half == 0 else sV.val)
                    for dc in range(DC):
                        dst = a16(XSF + dc * 512, 512)
                        if dc < 4:
                            P.do("scalar", lambda e, d=dst,
                                 s=a32(R_RES + dc * T + tgb, 512): e.copy(d, s),
                                 dep=(dc == 0))
                        elif tgb == 0:
                            P.do("vector", lambda e, d=a16(XSF + dc * 512, 1):
                                 e.memset(d, 0.0))
                            P.do("scalar", lambda e, d=a16(XSF + dc * 512 + 1, 511),
                                 s=a32(R_RES + dc * T, 511): e.copy(d, s))
                        elif tgb == 1024:
                            P.wait("scalar", sS, bnd_s)
                            P.do("scalar", lambda e, d=a16(XSF + dc * 512, 1),
                                 s=a32(sm["bnd"] + dc - 4, 1): e.copy(d, s))
                            P.do("scalar", lambda e, d=a16(XSF + dc * 512 + 1, 511),
                                 s=a32(R_RES + dc * T + tgb, 511): e.copy(d, s))
                        else:
                            P.do("scalar", lambda e, d=dst,
                                 s=a32(R_RES + dc * T + tgb - 1, 512): e.copy(d, s),
                                 dep=False)
                    xs_s = sS.val
                    xs_v = sV.val
                    P.wait("tensor", sS, xs_s)
                    P.wait("tensor", sV, xs_v)
                    pairs = [(sb_c1024[:], a16(XSF + dc * 512, 512)) for dc in range(DC)]
                    mu_mm = mm_acc(6, 512, pairs, opr=1)
                    P.wait("vector", sT, mu_mm)
                    mu_c = P.do("vector", lambda e, o=a32(sm["murow"], 512, pr=1),
                                i=bank(6, 512, pr=1): e.tensor_copy(o, i))
                    bank_read_done(6, sV, mu_c)
                    P.wait_t("tensor", P.bank_free[7])
                    P.wait("tensor", sV, mu_c)
                    P.do("tensor", lambda e, o=bank(7, 512),
                         l=sb_row1[:], r=a32(sm["murow"], 512, pr=1):
                         e.matmul(o, l, r, start=True, stop=True), sT, 1)
                    mub_mm = sT.val
                    P.wait("vector", sT, mub_mm)
                    for dc in range(DC):
                        P.do("vector", lambda e, o=a32(R_A + dc * 512, 512),
                             i=a16(XSF + dc * 512, 512), j=bank(7, 512):
                             e.tensor_tensor(o, i, j, ALU.subtract), dep=(dc == 0))
                    p1_v = sV.val
                    bank_read_done(7, sV, p1_v)
                    P.wait("scalar", sV, p1_v)
                    P.wait_t("tensor", P.bank_free[6])
                    P.wait("scalar", sT, sT.val)
                    for dc in range(DC):
                        P.do("scalar", lambda e, o=a16(R_W + dc * 512, 512),
                             i=a32(R_A + dc * 512, 512): e.activation(o, i, AF.Square),
                             dep=(dc == 0))
                    sq_s = sS.val
                    P.wait("tensor", sS, sq_s)
                    for dc in range(DC):
                        P.do("tensor", lambda e, o=bank(6, 512, pr=1),
                             l=sb_c1024[:], r=a16(R_W + dc * 512, 512),
                             st=(dc == 0), sp=(dc == DC - 1):
                             e.matmul(o, l, r, start=st, stop=sp),
                             sT if dc == DC - 1 else None, 1)
                    var_mm = sT.val
                    P.wait("vector", sT, var_mm)
                    vr_c = P.do("vector", lambda e, o=a32(sm["rstdrow"], 512, pr=1),
                                i=bank(6, 512, pr=1): e.tensor_copy(o, i))
                    bank_read_done(6, sV, vr_c)
                    P.wait("scalar", sV, vr_c)
                    P.do("scalar", lambda e, o=a32(sm["rstdrow"], 512, pr=1),
                         i=a32(sm["rstdrow"], 512, pr=1):
                         e.activation(o, i, AF.Ln, bias=sb_eps[0:1, :]))
                    rst_s = P.do("scalar", lambda e, o=a32(sm["rstdrow"], 512, pr=1),
                                 i=a32(sm["rstdrow"], 512, pr=1):
                                 e.activation(o, i, AF.Exp, scale=-0.5))
                    P.wait_t("tensor", P.bank_free[7])
                    P.wait("tensor", sS, rst_s)
                    P.do("tensor", lambda e, o=bank(7, 512),
                         l=sb_row1[:], r=a32(sm["rstdrow"], 512, pr=1):
                         e.matmul(o, l, r, start=True, stop=True), sT, 1)
                    ab_mm = sT.val
                    P.wait("vector", sT, ab_mm)
                    P.wait("vector", sS, smalls_s)
                    for dc in range(DC):
                        P.do("vector", lambda e, o=a32(R_A + dc * 512, 512),
                             i=a32(R_A + dc * 512, 512), j=bank(7, 512):
                             e.tensor_tensor(o, i, j, ALU.mult), dep=(dc == 0))
                        P.do("vector", lambda e, o=a16(R_AT + dc * 1024 + tg2 * 512, 512),
                             i=a32(R_A + dc * 512, 512), g=a32(sm["gcol"] + dc, 1),
                             bb=a32(sm["bcol"] + dc, 1):
                             e.tensor_scalar(o, i, g, bb, ALU.mult, ALU.add))
                    xn_v = sV.val
                    bank_read_done(7, sV, xn_v)
                xn_done = sV.val

                P.wait("gpsimd", sV, xn_done)
                for ci in range(NKC):
                    wci = 128 if ci < NKC - 1 else DI - 128 * (NKC - 1)
                    slot = ci % 2
                    P.wait_t("gpsimd", state["w1_free"][slot])
                    for gside in range(2):
                        c0 = gside * DI + ci * 128
                        src = w1_d[ell][:, c0:c0 + wci] \
                            .rearrange("(c p) m -> p c m", p=128)
                        dst = a16(W1RING[slot] + gside * 1024, DC * wci) \
                            .rearrange("p (c m) -> p c m", c=DC)
                        P.do("gpsimd", lambda e, d=dst, s=src: e.dma_start(out=d, in_=s),
                             qW1[slot], 16)
                    w1_ld = qW1[slot].val
                    P.wait("tensor", qW1[slot], w1_ld)
                    P.wait("tensor", sV, xn_done)
                    for tg2 in range(2):
                        for gside in range(2):
                            pairs = [(a16(W1RING[slot] + gside * 1024 + dc * wci, wci),
                                      a16(R_AT + dc * 1024 + tg2 * 512, 512))
                                     for dc in range(DC)]
                            mm_acc((ci % 2) * 4 + tg2 * 2 + gside, 512, pairs, opr=wci)
                    w1_mm = sT.val
                    state["w1_free"][slot] = (sT, w1_mm)
                    for tg2 in range(2):
                        ba, bg = (ci % 2) * 4 + tg2 * 2, (ci % 2) * 4 + tg2 * 2 + 1
                        P.wait("scalar", sT, w1_mm)
                        P.wait("scalar", sV, sV.val)
                        gl = P.do("scalar", lambda e, o=a16(R_V + 512, 512, pr=wci),
                                  i=bank(bg, 512, pr=wci),
                                  bb=a32(sm["b1g"] + ci, 1, pr=wci):
                                  e.activation(o, i, AF.Gelu, bias=bb), dep=False)
                        bank_read_done(bg, sS, gl)
                        P.wait("vector", sS, gl)
                        mv = P.do("vector", lambda e,
                                  o=a16(MID[ci] + tg2 * 512, 512, pr=wci),
                                  i=bank(ba, 512, pr=wci),
                                  bb=a32(sm["b1a"] + ci, 1, pr=wci),
                                  g=a16(R_V + 512, 512, pr=wci):
                                  e.scalar_tensor_tensor(o, i, bb, g, ALU.add,
                                                         ALU.mult), dep=False)
                        bank_read_done(ba, sV, mv)
                mid_v = sV.val
                mid_s = sS.val

                for tg2 in range(2):
                    tgb = hbase + tg2 * 512
                    w2_done = [None] * DC
                    for kc in range(NKC):
                        wkc = 128 if kc < NKC - 1 else DI - 128 * (NKC - 1)
                        slot = kc % 4
                        P.wait_t("gpsimd", state["w2_free"][slot])
                        src = w2_d[ell, kc * 128:kc * 128 + wkc, :]
                        wd = P.do("gpsimd", lambda e, d=a16(W2RING[slot], 1024, pr=wkc),
                                  s=src: e.dma_start(out=d, in_=s), qW2[slot], 16)
                        P.wait("tensor", qW2[slot], wd)
                        if kc == 0:
                            P.wait("tensor", sV, mid_v)
                            P.wait("tensor", sS, mid_s)
                            for b2_ in range(8):
                                P.wait_t("tensor", P.bank_free[b2_])
                        for mc in range(DC):
                            tok = P.do("tensor", lambda e, o=bank(mc, 512),
                                       l=a16(W2RING[slot] + mc * 128, 128, pr=wkc),
                                       r=a16(MID[kc] + tg2 * 512, 512, pr=wkc),
                                       st=(kc == 0), sp=(kc == NKC - 1):
                                       e.matmul(o, l, r, start=st, stop=sp),
                                       sT if (mc == DC - 1 or kc == NKC - 1) else None, 1)
                            if kc == NKC - 1:
                                w2_done[mc] = tok
                        state["w2_free"][slot] = (sT, sT.val)
                    for mc in range(DC):
                        P.wait("vector", sT, w2_done[mc])
                        dstr = a32(R_RES + mc * T + tgb, 512)
                        ev = P.do("vector", lambda e, d=dstr, p_=bank(mc),
                                  bb=a32(sm["b2col"] + mc, 1):
                                  e.scalar_tensor_tensor(d, p_, bb, d, ALU.add, ALU.add),
                                  dep=(mc == 0))
                        bank_read_done(mc, sV, ev)
            state["res_tok"] = (sV, sV.val)
            state["smalls_free"] = (sV, sV.val)
            for sl4 in range(4):
                state["v_free"][sl4] = (sT, sT.val)

        for ell in range(nlayers):
            emit_inner(ell)

        if dump:
            P.wait("sync", sV, sV.val)
            P.wait("sync", sS, sS.val)
            P.do("sync", lambda e: e.dma_start(out=ydbg_d, in_=a32(R_RES, DC * T)),
                 sO, 16)
            P.do("sync", lambda e: e.wait_ge(sO.h, sO.val))
        elif mode == "full":
            P.wait("sync", sV, sV.val)
            P.wait("sync", sS, sS.val)
            P.wait("tensor", sV, sV.val)
            P.wait("tensor", sS, sS.val)
            P.wait("scalar", sV, sV.val)
            P.wait("vector", sS, sS.val)
            P.wait("sync", qST, qST.val)
            for qt in range(NT):
                xd = P.do("sync", lambda e, d=a32(O_X0R[0], 1024),
                          s=x1s_d[qt * 128:(qt + 1) * 128, :]:
                          e.dma_start(out=d, in_=s), qX0[0], 16)
                for dcg in range(2):
                    b = dcg
                    P.wait_t("tensor", P.bank_free[b])
                    for k in range(4):
                        dc = dcg * 4 + k
                        src = a32(R_RES + dc * T + qt * 128, 128)
                        P.do("tensor", lambda e, o=bank(b, 128, off=k * 128), s=src:
                             e.matmul(o, s, sb_idF[:], start=True, stop=True),
                             sT if k == 3 else None, 1)
                    mmv = sT.val
                    P.wait("vector", sT, mmv)
                    P.wait("vector", qX0[0], xd)
                    ev = P.do("vector", lambda e, o=a32(O_ST + dcg * 512, 512),
                              p_=bank(b), xs=a32(O_X0R[0] + dcg * 512, 512):
                              e.tensor_tensor(o, p_, xs, ALU.add), sV, 1)
                    bank_read_done(b, sV, ev)
                P.wait("vector", sV, sV.val)
                stile = a32(O_ST, 1024)
                negmu = a32(sm["t1"], 1)
                P.do("vector", lambda e, o=negmu, i=stile: e.tensor_reduce(
                    o, i, AX.X, ALU.add, negate=True), sV, 1)
                P.wait("vector", sV, sV.val)
                P.do("vector", lambda e, o=negmu, i=negmu: e.tensor_scalar(
                    o, i, 1.0 / 1024.0, None, ALU.mult), sV, 1)
                mu_v = sV.val
                P.wait("scalar", sV, mu_v)
                ssq = a32(sm["t2"], 1)
                P.do("scalar", lambda e, o=a32(O_X1T, 1024), i=stile, b=negmu, sx=ssq:
                     e.activation(o, i, AF.Square, bias=b, accum_out=sx), sS, 1)
                P.do("scalar", lambda e, o=ssq, i=ssq: e.activation(
                    o, i, AF.Ln, scale=1.0 / 1024.0, bias=sb_eps[:]), sS, 1)
                P.do("scalar", lambda e, o=ssq, i=ssq: e.activation(
                    o, i, AF.Exp, scale=-0.5), sS, 1)
                rstd_s = sS.val
                P.wait("vector", sS, rstd_s)
                P.do("vector", lambda e, o=a32(sm["t0"], 1), i=negmu: e.tensor_tensor(
                    o, i, ssq, ALU.mult), sV, 1)
                m2_v = sV.val
                P.wait("vector", sV, m2_v)
                yv = P.do("vector", lambda e, o=a32(O_X1T, 1024), i=stile:
                          e.tensor_scalar(o, i, ssq, a32(sm["t0"], 1), ALU.mult, ALU.add),
                          sV, 1)
                P.wait("sync", sV, yv)
                P.do("sync", lambda e, s=a32(O_X1T, 1024),
                     d=y_d[qt * 128:(qt + 1) * 128, :]: e.dma_start(out=d, in_=s),
                     sO, 16)
                P.wait("tensor", sV, yv)
                P.wait("scalar", sO, sO.val)
                P.wait("sync", sV, sV.val)
            P.do("sync", lambda e: e.wait_ge(sO.h, sO.val))
        else:
            P.do("sync", lambda e: e.wait_ge(sO.h, sO.val))

        P.flush(block)
    return nc



def _in_maps(inputs, consts):
    x = np.ascontiguousarray(np.asarray(inputs["L0_attn_embeddings"], dtype=np.float32))
    base = {
        "wq": np.ascontiguousarray(np.asarray(inputs["L0_attn_W_Q"], np.float32)),
        "wk": np.ascontiguousarray(np.asarray(inputs["L0_attn_W_K"], np.float32)),
        "wv": np.ascontiguousarray(np.asarray(inputs["L0_attn_W_V"], np.float32)),
        "wqkv": np.ascontiguousarray(np.asarray(inputs["L0_ffn_Wqkv"], np.float32)),
        "qsc": np.ascontiguousarray(np.asarray(inputs["L0_ffn_q_scale"], np.float32)),
        "ksc": np.ascontiguousarray(np.asarray(inputs["L0_ffn_k_scale"], np.float32)),
        "wo": np.ascontiguousarray(np.asarray(inputs["L0_ffn_Wo"], np.float32)),
        "lng": np.ascontiguousarray(np.asarray(inputs["L0_ffn_ln_g"], np.float32)),
        "lnb": np.ascontiguousarray(np.asarray(inputs["L0_ffn_ln_b"], np.float32)),
        "w1": np.ascontiguousarray(np.asarray(inputs["L0_ffn_W1"], np.float32)),
        "b1": np.ascontiguousarray(np.asarray(inputs["L0_ffn_b1"], np.float32)),
        "w2": np.ascontiguousarray(np.asarray(inputs["L0_ffn_W2"], np.float32)),
        "b2": np.ascontiguousarray(np.asarray(inputs["L0_ffn_b2"], np.float32)),
    }
    base.update(consts)
    maps = []
    for c in range(NB):
        m = dict(base)
        m["x0"] = np.ascontiguousarray(x[c])
        maps.append(m)
    return maps


def kernel(**inputs):
    consts = _host_consts()
    nc = build_program(depth=DEPTH, mode="full")
    maps = _in_maps(inputs, consts)
    res = run_bass_kernel_spmd(nc, maps, core_ids=list(range(NB)))
    out = np.stack([np.asarray(r["y"], dtype=np.float32) for r in res.results])
    return out


# version 16
# speedup vs baseline: 1.1472x; 1.0188x over previous
import numpy as np
import ml_dtypes
from contextlib import ExitStack

import concourse.bass as bass
import concourse.mybir as mybir
from concourse.bass_utils import run_bass_kernel_spmd

F32 = mybir.dt.float32
BF16 = mybir.dt.bfloat16
AF = mybir.ActivationFunctionType
ALU = mybir.AluOpType
AX = mybir.AxisListType

NB = 8
T = 2048
D = 1024
DEPTH = 6
NHEAD = 16
DH = 64
WIN = 256
DI = 2730
NW = T // WIN
NT = T // 128
DC = D // 128
HG = 2
HGH = NHEAD // HG
NEG = -3.0e37
NKC = (DI + 127) // 128

MG0, MG1, MW00, MW01, MTRI = 0, 384, 896, 1152, 1408


def _host_consts():
    half = np.arange(0, DH, 2, dtype=np.float64)
    inv_freq = 1.0 / (10000.0 ** (half / DH))
    t = np.arange(T, dtype=np.float64)
    freqs = np.concatenate([t[:, None] * inv_freq[None, :]] * 2, -1)
    base_scale = (half + 0.4 * DH) / (1.4 * DH)
    power = (t - T // 2) / (WIN // 2)
    scale = np.concatenate([base_scale[None, :] ** power[:, None]] * 2, -1)
    cos = np.cos(freqs); sin = np.sin(freqs)
    sign = np.where(np.arange(DH) < DH // 2, -1.0, 1.0)[None, :]
    rot = lambda a: np.roll(a, -32, axis=1)
    rep = lambda a: np.tile(a, (1, HGH)).astype(ml_dtypes.bfloat16)
    tables = np.stack([rep(cos * scale), rep(rot(sign * sin * scale)),
                       rep(cos / scale), rep(rot(sign * sin / scale))])
    p = np.arange(128)[:, None]

    def band(ncols, lo, hi):
        j = np.arange(ncols)[None, :]
        return np.where((j >= lo) & (j <= hi), 0.0, NEG).astype(np.float32)
    masks = np.concatenate([
        band(384, p, p + WIN),
        band(512, p + 128, p + 384),
        band(256, 0 * p, p),
        band(256, 0 * p, p + 128),
        band(128, 0 * p, p),
    ], axis=1)
    return dict(
        tables=np.ascontiguousarray(tables),
        masks=masks,
        masksb=masks.astype(ml_dtypes.bfloat16),
        identF=np.eye(128, dtype=np.float32),
        identB=np.eye(128, dtype=np.float32).astype(ml_dtypes.bfloat16),
        c1024=np.full((128, 1), 1.0 / 1024.0, dtype=ml_dtypes.bfloat16),
        row8=np.full((1, 128), 8.0, dtype=np.float32),
        row1=np.full((1, 128), 1.0, dtype=np.float32),
        epscol=np.full((128, 1), 1.0e-5, dtype=np.float32),
        eps24col=np.full((128, 1), 1.0e-24, dtype=np.float32),
    )


class SemC:
    __slots__ = ("h", "val")
    def __init__(self, h):
        self.h = h
        self.val = 0


ENGS = ("sync", "gpsimd", "scalar", "vector", "tensor")


class Prog:
    def __init__(self):
        self.ops = {e: [] for e in ENGS}
        self.waited = {}
        self.esem = {}
        self.bank_free = [(None, 0)] * 8
        self.bank_ready = [(None, 0)] * 8

    def do(self, eng, fn, inc=None, n=1, dep=True):
        if eng in ("vector", "scalar") or (eng == "gpsimd" and n != 16):
            es = self.esem[eng]
            assert inc is None or inc is es
            if dep:
                self.wait(eng, es, es.val)
            inc, n = es, 1
        if inc is None:
            self.ops[eng].append(fn)
            return None
        inc.val += n
        h, v = inc.h, inc.val
        self.ops[eng].append(lambda e, fn=fn, h=h, n=n: fn(e).then_inc(h, n))
        return v

    def wait(self, eng, sem, val):
        if sem is None or val is None or val <= 0:
            return
        key = (eng, id(sem))
        if self.waited.get(key, 0) >= val:
            return
        self.waited[key] = val
        h = sem.h
        self.ops[eng].append(lambda e, h=h, val=val: e.wait_ge(h, val))

    def wait_t(self, eng, tok):
        if tok is not None:
            self.wait(eng, tok[0], tok[1])

    def flush(self, block):
        for eng in ENGS:
            ops = self.ops[eng]
            if not ops:
                continue
            def body(e, ops=ops):
                for fn in ops:
                    fn(e)
            getattr(block, eng)(body)


def build_program(depth=DEPTH, mode="full"):
    nc = bass.Bass("TRN2", target_bir_lowering=False, debug=False)

    def din(name, shape, dt=F32):
        return nc.dram_tensor(name, list(shape), dt, kind="ExternalInput").ap()

    x0 = din("x0", [T, D])
    wq_d = din("wq", [D, D]); wk_d = din("wk", [D, D]); wv_d = din("wv", [D, D])
    wqkv_d = din("wqkv", [DEPTH, D, 3 * D])
    qsc_d = din("qsc", [DEPTH, DH]); ksc_d = din("ksc", [DEPTH, DH])
    wo_d = din("wo", [DEPTH, D, D])
    lng_d = din("lng", [DEPTH, D]); lnb_d = din("lnb", [DEPTH, D])
    w1_d = din("w1", [DEPTH, D, 2 * DI]); b1_d = din("b1", [DEPTH, 2 * DI])
    w2_d = din("w2", [DEPTH, DI, D]); b2_d = din("b2", [DEPTH, D])
    tbl_d = din("tables", [4, T, HGH * DH], BF16)
    masks_d = din("masks", [128, 1536])
    masksb_d = din("masksb", [128, 1536], BF16)
    identF_d = din("identF", [128, 128])
    identB_d = din("identB", [128, 128], BF16)
    c1024_d = din("c1024", [128, 1], BF16)
    row8_d = din("row8", [1, 128]); row1_d = din("row1", [1, 128])
    eps_d = din("epscol", [128, 1])
    eps24_d = din("eps24col", [128, 1])

    y_d = nc.dram_tensor("y", [T, D], F32, kind="ExternalOutput").ap()
    x1s_d = nc.dram_tensor("x1s", [T, D], F32, kind="Internal").ap()
    rstash_d = nc.dram_tensor("rstash", [128, DC * T], F32, kind="Internal").ap()
    dump = mode.startswith("L") or mode.startswith("A")
    attn_only = mode.startswith("A")
    nlayers = depth if mode in ("full",) else (0 if mode == "outer" else int(mode[1:]) + 1)
    if dump:
        ydbg_d = nc.dram_tensor("ydbg", [128, DC * T], F32, kind="ExternalOutput").ap()

    with ExitStack() as ctx:
        A32 = ctx.enter_context(nc.sbuf_tensor("A32", [128, 24064], F32))
        A16 = ctx.enter_context(nc.sbuf_tensor("A16", [128, 50816], BF16))
        sb_idF = ctx.enter_context(nc.sbuf_tensor("sb_idF", [128, 128], F32))
        sb_idB = ctx.enter_context(nc.sbuf_tensor("sb_idB", [128, 128], BF16))
        sb_c1024 = ctx.enter_context(nc.sbuf_tensor("sb_c1024", [128, 1], BF16))
        sb_row8 = ctx.enter_context(nc.sbuf_tensor("sb_row8", [1, 128], F32))
        sb_row1 = ctx.enter_context(nc.sbuf_tensor("sb_row1", [1, 128], F32))
        sb_eps = ctx.enter_context(nc.sbuf_tensor("sb_eps", [128, 1], F32))
        sb_eps24 = ctx.enter_context(nc.sbuf_tensor("sb_eps24", [128, 1], F32))
        sb_mskb = ctx.enter_context(nc.sbuf_tensor("sb_mskb", [128, 1536], BF16))
        PS = ctx.enter_context(nc.psum_tensor("PS", [128, 4096], F32))

        R_RES = 0
        R_MASK = 16384
        R_A = 17920
        R_SM = 22016
        R_W = 0
        R_KT = 12288
        R_AT = 20480
        R_TAB = 28672
        R_V = 36864
        R_XS = 38912
        R_QW = 40960
        R_EP = 44032
        R_WO = 46720
        sm = {}
        _p = [R_SM]
        for nm, c in [("ssq", 32), ("ssk", 16), ("negmax", 8), ("sumexp", 8),
                      ("recip", 8), ("murow", 512), ("rstdrow", 512),
                      ("qsrow", 512), ("gcol", 8), ("bcol", 8), ("b2col", 8),
                      ("b1a", 24), ("b1g", 24), ("t0", 8), ("t1", 8), ("t2", 8),
                      ("bnd", 8)]:
            sm[nm] = _p[0]; _p[0] += c
        assert _p[0] <= R_SM + 2048

        def a32(base, cols, pr=128, po=0):
            return A32[po:po + pr, base:base + cols]

        def a16(base, cols, pr=128, po=0):
            return A16[po:po + pr, base:base + cols]

        def bank(b, cols=512, off=0, pr=128, po=0):
            return PS[po:po + pr, b * 512 + off:b * 512 + off + cols]

        sQ = SemC(ctx.enter_context(nc.semaphore("sQ")))
        qC = SemC(ctx.enter_context(nc.semaphore("qC")))
        qX0 = [SemC(ctx.enter_context(nc.semaphore("qX0a"))),
               SemC(ctx.enter_context(nc.semaphore("qX0b")))]
        qST = SemC(ctx.enter_context(nc.semaphore("qST")))
        sG = SemC(ctx.enter_context(nc.semaphore("sG")))
        sT = SemC(ctx.enter_context(nc.semaphore("sT")))
        sV = SemC(ctx.enter_context(nc.semaphore("sV")))
        sS = SemC(ctx.enter_context(nc.semaphore("sS")))
        sO = SemC(ctx.enter_context(nc.semaphore("sO")))
        sGC = SemC(ctx.enter_context(nc.semaphore("sGC")))

        block = ctx.enter_context(nc.Block())
        P = Prog()
        P.esem = {"vector": sV, "scalar": sS, "gpsimd": sGC}

        def mm_acc(b, cols, pairs, off=0, opr=128, opo=0):
            P.wait_t("tensor", P.bank_free[b])
            n = len(pairs)
            for i, (l, r) in enumerate(pairs):
                P.do("tensor",
                     lambda e, o=bank(b, cols, off, opr, opo), l=l, r=r,
                     st=(i == 0), sp=(i == n - 1):
                     e.matmul(o, l, r, start=st, stop=sp),
                     sT if i == n - 1 else None, 1)
            P.bank_ready[b] = (sT, sT.val)
            return sT.val

        def bank_read_done(b, sem, val):
            P.bank_free[b] = (sem, val)

        for dst, src in [(sb_idF[:], identF_d), (sb_idB[:], identB_d),
                         (sb_c1024[:], c1024_d), (sb_row8[:], row8_d),
                         (sb_row1[:], row1_d), (sb_eps[:], eps_d),
                         (sb_eps24[:], eps24_d),
                         (sb_mskb[:], masksb_d), (a32(R_MASK, 1536), masks_d)]:
            P.do("sync", lambda e, d=dst, s=src: e.dma_start(out=d, in_=s), qC, 16)
        consts_q = qC.val

        OX0T = 0
        OWB = 16384
        OKT = 24576
        OSTG = 40960
        OQR = 16384
        OVR = 0
        OE = 18432
        OPT = 20480
        O_X0R = [R_A, R_A + 1024]
        O_ST = R_A + 2048
        O_X1T = R_A + 3072

        qts_d = nc.dram_tensor("qts", [NT, 128, D], BF16, kind="Internal").ap()
        vs_d = nc.dram_tensor("vs", [NT, 128, D], BF16, kind="Internal").ap()
        qQS = [SemC(ctx.enter_context(nc.semaphore("qQSa"))),
               SemC(ctx.enter_context(nc.semaphore("qQSb")))]
        qQR = [SemC(ctx.enter_context(nc.semaphore("qQRa"))),
               SemC(ctx.enter_context(nc.semaphore("qQRb")))]
        qVR = [SemC(ctx.enter_context(nc.semaphore(f"qVR{i}"))) for i in range(4)]

        x0_ld = []
        P.wait("tensor", qC, consts_q)
        ring_free = [None, None]
        for tt in range(NT):
            P.wait_t("sync", ring_free[tt % 2])
            dst = a32(O_X0R[tt % 2], 1024)
            src = x0[tt * 128:(tt + 1) * 128, :]
            x0_ld.append(P.do("sync", lambda e, d=dst, s=src: e.dma_start(out=d, in_=s),
                              qX0[tt % 2], 16))
            P.wait("tensor", qX0[tt % 2], x0_ld[tt])
            P.wait_t("tensor", ring_free[tt % 2])
            for dcg in range(2):
                b = dcg
                P.wait_t("tensor", P.bank_free[b])
                for k in range(4):
                    dc = dcg * 4 + k
                    src2 = a32(O_X0R[tt % 2], 1024)[:, dc * 128:dc * 128 + 128]
                    P.do("tensor",
                         lambda e, o=bank(b, 128, off=k * 128), s=src2:
                         e.matmul(o, s, sb_idF[:], start=True, stop=True),
                         sT if k == 3 else None, 1)
                mmv = sT.val
                P.wait("scalar", sT, mmv)
                last = None
                for k in range(4):
                    dc = dcg * 4 + k
                    dst2 = a16(OX0T + dc * T + tt * 128, 128)
                    last = P.do("scalar",
                                lambda e, d=dst2, s=bank(b, 128, off=k * 128):
                                e.copy(d, s), sS, 1, dep=(k == 0))
                bank_read_done(b, sS, last)
            ring_free[tt % 2] = (sS, sS.val)
        x0T_done = (sS, sS.val)

        def wload(w_ap, prev_consumed=None):
            if prev_consumed is not None:
                P.wait_t("gpsimd", prev_consumed)
            for dc in range(DC):
                src = w_ap[dc * 128:(dc + 1) * 128, :]
                dst = a16(OWB + dc * 1024, 1024)
                P.do("gpsimd", lambda e, d=dst, s=src: e.dma_start(out=d, in_=s), sG, 16)
            return sG.val

        wld = wload(wk_d)
        P.wait("tensor", sG, wld)
        P.wait_t("tensor", x0T_done)
        for mc in range(DC):
            for tg in range(4):
                b = 2 + ((mc * 4 + tg) % 2)
                pairs = [(a16(OWB + dc * 1024 + mc * 128, 128),
                          a16(OX0T + dc * T + tg * 512, 512)) for dc in range(DC)]
                mmv = mm_acc(b, 512, pairs)
                P.wait("scalar", sT, mmv)
                ev = P.do("scalar", lambda e, d=a16(OKT + mc * T + tg * 512, 512),
                          s=bank(b): e.copy(d, s), sS, 1)
                bank_read_done(b, sS, ev)
        kt_done = sS.val

        stg_free = [None, None]

        def proj_to_hbm(which, wld_mark):
            P.wait("tensor", sG, wld_mark)
            n_out = NT * 2 if which == "v" else DC * 4
            for i in range(n_out):
                b = 2 + (i % 2)
                sl = i % 2
                if which == "v":
                    tt, vg = i // 2, i % 2
                    tt, vg = divmod(i, 2)
                    pairs = [(a16(OX0T + dc * T + tt * 128, 128),
                              a16(OWB + dc * 1024 + vg * 512, 512)) for dc in range(DC)]
                else:
                    mc, tg = divmod(i, 4)
                    pairs = [(a16(OWB + dc * 1024 + mc * 128, 128),
                              a16(OX0T + dc * T + tg * 512, 512)) for dc in range(DC)]
                mmv = mm_acc(b, 512, pairs)
                P.wait("scalar", sT, mmv)
                P.wait_t("scalar", stg_free[sl])
                ev = P.do("scalar", lambda e, d=a16(OSTG + sl * 512, 512), s=bank(b):
                          e.copy(d, s), sS, 1)
                bank_read_done(b, sS, ev)
                P.wait("sync", sS, ev)
                if which == "v":
                    dd = vs_d[tt, :, vg * 512:(vg + 1) * 512]
                else:
                    mc, tg = divmod(i, 4)
                    dd = qts_d[tg * 4:(tg + 1) * 4, :, mc * 128:(mc + 1) * 128]                         .rearrange("q p c -> p q c")
                dmam = P.do("sync", lambda e, d=dd, s=a16(OSTG + sl * 512, 512)
                            .rearrange("p (q c) -> p q c", q=4 if which == "q" else 1):
                            e.dma_start(out=d, in_=s), qQS[sl], 16)
                stg_free[sl] = (qQS[sl], dmam)
            return (qQS[0].val, qQS[1].val)

        wld = wload(wv_d, prev_consumed=(sS, kt_done))
        v_done = proj_to_hbm("v", wld)
        wld = wload(wq_d, prev_consumed=(sS, sS.val))
        q_done = proj_to_hbm("q", wld)

        x1_qdma = []
        qr_free = [None, None]
        vloads = {}
        P.wait("sync", qQS[0], q_done[0])
        P.wait("sync", qQS[1], q_done[1])
        for qt in range(NT):
            nk = (qt + 1) * 128
            njg = (nk + 511) // 512
            xd = P.do("sync", lambda e, d=a32(O_X0R[0], 1024),
                      s=x0[qt * 128:(qt + 1) * 128, :]: e.dma_start(out=d, in_=s),
                      qX0[0], 16)
            qsl = qt % 2
            P.wait_t("sync", qr_free[qsl])
            qd = P.do("sync", lambda e, d=a16(OQR + qsl * 1024, 1024),
                      s=qts_d[qt]: e.dma_start(out=d, in_=s), qQR[qsl], 16)
            qs4 = qVR[qt % 4]
            P.wait("sync", qs4, qs4.val)
            vm = P.do("sync", lambda e, d=a16(OVR + qt * 1024, 1024),
                      s=vs_d[qt]: e.dma_start(out=d, in_=s), qs4, 16)
            vloads[qt] = (qs4, vm)
            P.wait("tensor", qQR[qsl], qd)
            for jg in range(njg):
                w_ = min(512, nk - jg * 512)
                pairs = [(a16(OQR + qsl * 1024 + mc * 128, 128),
                          a16(OKT + mc * T + jg * 512, w_)) for mc in range(DC)]
                mm_acc(jg, w_, pairs)
            s_mm = sT.val
            P.wait("vector", sT, s_mm)
            dsl = PS[:, qt * 128:qt * 128 + 128]
            mski = P.do("vector", lambda e, d=dsl: e.tensor_tensor(
                d, d, a32(R_MASK + MTRI, 128), ALU.add), sV, 1)
            P.wait("vector", sV, mski)
            nm = a32(sm["negmax"], 1)
            P.do("vector", lambda e, o=nm, n=nk: e.tensor_reduce(
                o, PS[:, 0:n], AX.X, ALU.max, negate=True), sV, 1)
            red_v = sV.val
            P.wait("scalar", sV, red_v)
            nm32 = a32(sm["t0"], 1)
            nm32_s = P.do("scalar", lambda e, o=nm32, i=nm: e.activation(
                o, i, AF.Copy, scale=1.0 / 32.0), sS, 1)
            P.wait("scalar", sS, nm32_s)
            sxp = a32(sm["sumexp"], 1)
            exp_s = P.do("scalar", lambda e, o=a16(OE, nk), n=nk, b=nm32, sx=sxp:
                         e.activation(o, PS[:, 0:n], AF.Exp, bias=b, scale=1.0 / 32.0,
                                      accum_out=sx), sS, 1)
            for b_ in range(njg):
                bank_read_done(b_, sS, exp_s)
            P.wait("vector", sS, exp_s)
            rc = a32(sm["recip"], 1)
            P.do("vector", lambda e, o=rc, i=sxp: e.reciprocal(o, i), sV, 1)
            rcv = sV.val
            P.wait("vector", sV, rcv)
            pn = P.do("vector", lambda e, o=a16(OE, nk), i=a16(OE, nk), r=rc:
                      e.tensor_scalar(o, i, r, None, ALU.mult), sV, 1)
            P.wait("tensor", sV, pn)
            for jc in range(qt + 1):
                b = 4 + ((jc // 2) % 2)
                off = (jc % 2) * 128
                if jc % 2 == 0:
                    P.wait_t("tensor", P.bank_free[b])
                mm = P.do("tensor", lambda e, o=bank(b, 128, off=off),
                          l=a16(OE + jc * 128, 128):
                          e.matmul(o, l, sb_idB[:], start=True, stop=True), sT, 1)
                if jc % 2 == 1 or jc == qt:
                    P.wait("scalar", sT, mm)
                    wdt = 128 * (2 if jc % 2 == 1 else 1)
                    o0 = (jc // 2) * 256
                    ev = P.do("scalar", lambda e, d=a16(OPT + o0, wdt),
                              s=bank(b, wdt): e.copy(d, s), sS, 1)
                    bank_read_done(b, sS, ev)
            pt_ev = sS.val
            P.wait("tensor", sS, pt_ev)
            for jc in range(qt + 1):
                P.wait_t("tensor", vloads[jc])
            for dg in range(2):
                pairs = [(a16(OPT + jc * 128, 128),
                          a16(OVR + jc * 1024 + dg * 512, 512))
                         for jc in range(qt + 1)]
                mm_acc(6 + dg, 512, pairs)
            pv_mm = sT.val
            qr_free[qsl] = (sT, s_mm)
            P.wait("vector", sT, pv_mm)
            P.wait("vector", qX0[0], xd)
            for dg in range(2):
                ev = P.do("vector", lambda e, o=a32(O_ST + dg * 512, 512),
                          p_=bank(6 + dg), xs=a32(O_X0R[0] + dg * 512, 512):
                          e.tensor_tensor(o, p_, xs, ALU.add), sV, 1)
                bank_read_done(6 + dg, sV, ev)
            st_v = sV.val
            P.wait("vector", sV, st_v)
            stile = a32(O_ST, 1024)
            negmu = a32(sm["t1"], 1)
            P.do("vector", lambda e, o=negmu, i=stile: e.tensor_reduce(
                o, i, AX.X, ALU.add, negate=True), sV, 1)
            P.wait("vector", sV, sV.val)
            P.do("vector", lambda e, o=negmu, i=negmu: e.tensor_scalar(
                o, i, 1.0 / 1024.0, None, ALU.mult), sV, 1)
            mu_v = sV.val
            P.wait("scalar", sV, mu_v)
            ssq = a32(sm["t2"], 1)
            P.do("scalar", lambda e, o=a32(O_X1T, 1024), i=stile, b=negmu, sx=ssq:
                 e.activation(o, i, AF.Square, bias=b, accum_out=sx), sS, 1)
            P.do("scalar", lambda e, o=ssq, i=ssq: e.activation(
                o, i, AF.Ln, scale=1.0 / 1024.0, bias=sb_eps[:]), sS, 1)
            P.do("scalar", lambda e, o=ssq, i=ssq: e.activation(
                o, i, AF.Exp, scale=-0.5), sS, 1)
            rstd_s = sS.val
            P.wait("vector", sS, rstd_s)
            P.do("vector", lambda e, o=a32(sm["t0"], 1), i=negmu, j=ssq:
                 e.tensor_tensor(o, i, j, ALU.mult), sV, 1)
            m2_v = sV.val
            P.wait("vector", sV, m2_v)
            x1v = P.do("vector", lambda e, o=a32(O_X1T, 1024), i=stile, r=ssq,
                       m2=a32(sm["t0"], 1):
                       e.tensor_scalar(o, i, r, m2, ALU.mult, ALU.add), sV, 1)
            P.wait("sync", sV, x1v)
            x1_qdma.append(P.do("sync", lambda e, s=a32(O_X1T, 1024),
                                d=x1s_d[qt * 128:(qt + 1) * 128, :]:
                                e.dma_start(out=d, in_=s), qST, 16))
            if mode == "outer":
                P.do("sync", lambda e, s=a32(O_X1T, 1024),
                     d=y_d[qt * 128:(qt + 1) * 128, :]: e.dma_start(out=d, in_=s),
                     sO, 16)
            P.wait("tensor", sV, x1v)
            for dcg in range(2):
                b = dcg
                P.wait_t("tensor", P.bank_free[b])
                for k in range(4):
                    dc = dcg * 4 + k
                    src = a32(O_X1T, 1024)[:, dc * 128:dc * 128 + 128]
                    P.do("tensor", lambda e, o=bank(b, 128, off=k * 128), s=src:
                         e.matmul(o, s, sb_idF[:], start=True, stop=True),
                         sT if k == 3 else None, 1)
                mmv = sT.val
                P.wait("vector", sT, mmv)
                ev = None
                for k in range(4):
                    dc = dcg * 4 + k
                    dst = a32(R_RES + dc * T + qt * 128, 128)
                    ev = P.do("vector", lambda e, d=dst, s=bank(b, 128, off=k * 128):
                              e.tensor_copy(d, s), sV, 1, dep=(k == 0))
                bank_read_done(b, sV, ev)
            P.wait("sync", sV, st_v)
            P.wait("scalar", qST, qST.val)
            P.wait("scalar", sO, sO.val)
            P.wait("scalar", sT, sT.val)
        outer_done_v = sV.val
        outer_done_s = sS.val
        outer_done_t = sT.val

        QS8B, KS8B = R_A + 2048, R_A + 3072
        TMPA, TMPB, TMPC = R_A, R_A + 512, R_A + 1024
        XQW, XKW, XQT = R_QW, R_QW + 1024, R_QW + 2048
        EP_E = [R_EP, R_EP + 896, R_EP + 1792]
        EP_PT = R_QW
        MID = [R_W + i * 1024 for i in range(12)] + \
              [R_KT + i * 1024 for i in range(8)] + \
              [R_TAB + i * 1024 for i in range(2)]
        W1RING = [R_TAB + 2048, R_TAB + 4096]
        W2RING = [R_TAB + 6144, R_TAB + 7168, R_EP, R_EP + 1024]
        XSF = R_XS

        qWQ = SemC(ctx.enter_context(nc.semaphore("qWQ")))
        qTB = [SemC(ctx.enter_context(nc.semaphore(f"qTB{i}"))) for i in range(2)]
        qWO = SemC(ctx.enter_context(nc.semaphore("qWO")))
        qW1 = [SemC(ctx.enter_context(nc.semaphore(f"qW1{i}"))) for i in range(2)]
        qW2 = [SemC(ctx.enter_context(nc.semaphore(f"qW2{i}"))) for i in range(4)]
        qMS = SemC(ctx.enter_context(nc.semaphore("qMS")))
        qRS = SemC(ctx.enter_context(nc.semaphore("qRS")))
        qSC = SemC(ctx.enter_context(nc.semaphore("qSC")))
        qXS = SemC(ctx.enter_context(nc.semaphore("qXS")))

        state = dict(tab_free=[None, None], w1_free=[None, None],
                     w2_free=[None] * 4,
                     v_free=[None] * 4, smalls_free=None, res_tok=None, wo_free=None,
                     sq_free=[None] * 3, b_free=[None] * 3, chain_seq=[0])
        SQP = [R_A, R_A + 512, R_MASK]
        BP = [R_A + 1024, R_A + 1536, R_MASK + 512]

        def emit_inner(ell):
            if ell == 0:
                for eng in ("gpsimd", "sync", "tensor", "vector", "scalar"):
                    P.wait(eng, sT, outer_done_t)
                    P.wait(eng, sV, outer_done_v)
                    P.wait(eng, sS, outer_done_s)
                P.wait("sync", qST, qST.val)

            P.wait_t("sync", state["smalls_free"])
            qs_rep = qsc_d[ell][None, None, :].broadcast_to([128, HGH, DH])
            ks_rep = ksc_d[ell][None, None, :].broadcast_to([128, HGH, DH])
            P.wait("sync", qSC, qSC.val)
            for dst, srcp in [
                (a32(QS8B, 512).rearrange("p (a b) -> p a b", a=HGH), qs_rep),
                (a32(KS8B, 512).rearrange("p (a b) -> p a b", a=HGH), ks_rep),
            ]:
                P.do("sync", lambda e, d=dst, s=srcp: e.dma_start(out=d, in_=s), qSC, 16)
            sc_ld = qSC.val
            P.wait("vector", qSC, qSC.val)
            P.do("vector", lambda e, o=a32(QS8B, 512), i=a32(QS8B, 512):
                 e.tensor_scalar(o, i, 8.0, None, ALU.mult))
            qs8_v = sV.val
            P.wait_t("sync", state["res_tok"])
            colspecs = [
                (sm["gcol"], 8, lng_d[ell].rearrange("(c p) -> c p", p=128), None),
                (sm["bcol"], 8, lnb_d[ell].rearrange("(c p) -> c p", p=128), None),
                (sm["b2col"], 8, b2_d[ell].rearrange("(c p) -> c p", p=128), None),
                (sm["b1a"], 22, b1_d[ell, 0:2816].rearrange("(c p) -> c p", p=128), None),
                (sm["b1g"], 22, b1_d[ell, 2730:5418].rearrange("(c p) -> c p", p=128),
                 b1_d[ell, 5418:5460][None, :]),
            ]
            for dcol, ncc, srcp, tail in colspecs:
                ld = P.do("sync", lambda e, d=a32(TMPA, 128, pr=ncc if tail is None
                          else ncc - 1), s=srcp: e.dma_start(out=d, in_=s), qMS, 16)
                if tail is not None:
                    ld = P.do("sync", lambda e, d=a32(TMPA, 42, pr=1, po=ncc - 1),
                              s=tail: e.dma_start(out=d, in_=s), qMS, 16)
                P.wait("tensor", qMS, ld)
                P.wait_t("tensor", P.bank_free[5])
                mm = P.do("tensor", lambda e, o=bank(5, ncc), l=a32(TMPA, 128, pr=ncc),
                          r=sb_idF[0:ncc, 0:ncc]: e.matmul(o, l, r, start=True, stop=True),
                          sT, 1)
                P.wait("scalar", sT, mm)
                ev = P.do("scalar", lambda e, d=a32(dcol, ncc), s=bank(5, ncc):
                          e.copy(d, s))
                bank_read_done(5, sS, ev)
                P.wait("sync", sT, mm)
            smalls_ld = qMS.val
            smalls_s = sS.val

            P.wait_t("sync", state["res_tok"])
            P.wait("sync", qXS, qXS.val)
            rs = P.do("sync", lambda e, d=rstash_d, s=a32(R_RES, DC * T):
                      e.dma_start(out=d, in_=s), qRS, 16)

            qkv_mm_done = [0]
            for hg in range(2):
                P.wait_t("gpsimd", state["res_tok"])
                if hg == 1:
                    P.wait("gpsimd", sT, qkv_mm_done[0])
                for mi, cbase in enumerate((0, D, 2 * D)):
                    src = wqkv_d[ell][:, cbase + hg * 512: cbase + hg * 512 + 512] \
                        .rearrange("(c p) m -> p c m", p=128)
                    dst = a16(R_W + mi * 4096, 4096).rearrange("p (c m) -> p c m", c=DC)
                    P.do("gpsimd", lambda e, d=dst, s=src: e.dma_start(out=d, in_=s),
                         qWQ, 16)
                wqkv_ld = qWQ.val
                P.wait_t("gpsimd", state["wo_free"])
                for kc in range(4):
                    src = wo_d[ell, hg * 512 + kc * 128: hg * 512 + (kc + 1) * 128, :]
                    P.do("gpsimd", lambda e, d=a16(R_WO + kc * 1024, 1024), s=src:
                         e.dma_start(out=d, in_=s), qWO, 16)
                wo_ld = qWO.val
                if hg == 1:
                    P.wait("scalar", sT, sT.val)

                for w in range(NW):
                    tsl = w % 2
                    P.wait_t("sync", state["tab_free"][tsl])
                    for ti in range(4):
                        src = tbl_d[ti, w * 256:(w + 1) * 256, :] \
                            .rearrange("(tt p) c -> p tt c", p=128)
                        dst = a16(R_TAB + tsl * 4096 + ti * 1024, 1024) \
                            .rearrange("p (tt c) -> p tt c", tt=2)
                        P.do("sync", lambda e, d=dst, s=src: e.dma_start(out=d, in_=s),
                             qTB[tsl], 16)
                    tab_ld = qTB[tsl].val

                    P.wait("gpsimd", sT, qkv_mm_done[0])
                    P.wait("gpsimd", qRS, rs)
                    src0 = rstash_d.rearrange("p (c t) -> p c t", c=DC)[
                        :, 0:4, w * 256:(w + 1) * 256]
                    dst0 = a16(R_XS, 1024).rearrange("p (c t) -> p c t", c=4)
                    P.do("gpsimd", lambda e, d=dst0, s=src0: e.dma_start(out=d, in_=s),
                         qXS, 16)
                    if w > 0:
                        src1 = rstash_d.rearrange("p (c t) -> p c t", c=DC)[
                            :, 4:8, w * 256 - 1:(w + 1) * 256 - 1]
                        dst1 = a16(R_XS + 1024, 1024).rearrange("p (c t) -> p c t", c=4)
                        P.do("gpsimd", lambda e, d=dst1, s=src1:
                             e.dma_start(out=d, in_=s), qXS, 16)
                        xs_v = sV.val
                    else:
                        src1 = rstash_d.rearrange("p (c t) -> p c t", c=DC)[
                            :, 4:8, 0:255]
                        dst1 = a16(R_XS + 1024, 1024).rearrange(
                            "p (c t) -> p c t", c=4)[:, :, 1:256]
                        P.do("gpsimd", lambda e, d=dst1, s=src1:
                             e.dma_start(out=d, in_=s), qXS, 16)
                        P.wait("vector", sT, qkv_mm_done[0])
                        for c4 in range(4):
                            P.do("vector", lambda e, d=a16(R_XS + 1024 + c4 * 256, 1):
                                 e.memset(d, 0.0))
                        xs_v = sV.val
                    xs_ld = qXS.val

                    P.wait("tensor", qWQ, wqkv_ld)
                    P.wait("tensor", qXS, xs_ld)
                    P.wait("tensor", sV, xs_v)
                    qkv_mm_of = {}
                    for mi in (0, 1, 2):
                        for tt in range(2):
                            pairs = [(a16(R_XS + dc * 256 + tt * 128, 128),
                                      a16(R_W + mi * 4096 + dc * 512, 512))
                                     for dc in range(DC)]
                            qkv_mm_of[(tt, mi)] = mm_acc(tt * 3 + mi, 512, pairs)
                    qkv_mm = sT.val
                    qkv_mm_done[0] = qkv_mm

                    chains = [(0, 0), (0, 1), (1, 0), (1, 1)]
                    st1_s = {}; st2_v = {}; st3_s = {}; st7_v = {}
                    cpar = {}
                    wp = w % 2

                    def ssq_col(tt_, h):
                        return a32(sm["ssq"] + tt_ * 16 + wp * 8 + h, 1)

                    def sslot(ci):
                        side, tt_ = chains[ci]
                        if side == 0:
                            return a32(sm["ssq"] + tt_ * 16 + wp * 8, 8)
                        return a32(sm["ssk"] + tt_ * 8, 8)

                    for ci in range(4):
                        side, tt_ = chains[ci]
                        b = tt_ * 3 + side
                        par = state["chain_seq"][0] % 3
                        state["chain_seq"][0] += 1
                        cpar[ci] = par
                        P.wait("scalar", sT, qkv_mm_of[(tt_, side)])
                        P.wait_t("scalar", state["sq_free"][par])
                        P.do("scalar", lambda e, o=a32(SQP[par], 512), i=bank(b):
                             e.activation(o, i, AF.Square), dep=False)
                        st1_s[ci] = sS.val
                    for ci in range(4):
                        ssl = sslot(ci)
                        P.wait("vector", sS, st1_s[ci])
                        P.do("vector", lambda e, o=ssl, i=a32(SQP[cpar[ci]], 512)
                             .rearrange("p (h d) -> p h d", h=8):
                             e.tensor_reduce(o, i, AX.X, ALU.add), dep=False)
                        st2_v[ci] = sV.val
                    for ci in range(4):
                        ssl = sslot(ci)
                        P.wait("scalar", sV, st2_v[ci])
                        P.do("scalar", lambda e, o=ssl, i=ssl:
                             e.activation(o, i, AF.Ln, bias=sb_eps24[:]), dep=False)
                        P.do("scalar", lambda e, o=ssl, i=ssl:
                             e.activation(o, i, AF.Exp, scale=-0.5))
                        st3_s[ci] = sS.val

                    def chain_tail(ci):
                        side, tt_ = chains[ci]
                        b = tt_ * 3 + side
                        par = cpar[ci]
                        cosb = R_TAB + tsl * 4096 + (0 if side == 0 else 2) * 1024 + tt_ * 512
                        sinb = R_TAB + tsl * 4096 + (1 if side == 0 else 3) * 1024 + tt_ * 512
                        dstw = (XQW if side == 0 else XKW) + tt_ * 512
                        Bp = a32(BP[par], 512)
                        P.wait_t("vector", state["b_free"][par])
                        P.wait("vector", sT, qkv_mm_of[(tt_, side)])
                        P.wait("vector", qSC, sc_ld)
                        if side == 0:
                            P.wait("vector", sS, st1_s[ci])
                            P.do("vector", lambda e, o=Bp, i=bank(b), j=a32(QS8B, 512):
                                 e.tensor_tensor(o, i, j, ALU.mult), dep=False)
                        else:
                            P.wait("vector", sS, st3_s[ci])
                            ssl = sslot(ci)
                            for h in range(8):
                                P.do("vector", lambda e, o=Bp[:, h * 64:h * 64 + 64],
                                     i=bank(b, 64, off=h * 64), r=ssl[:, h:h + 1],
                                     j=a32(KS8B + h * 64, 64):
                                     e.scalar_tensor_tensor(o, i, r, j, ALU.mult,
                                                            ALU.mult), dep=False)
                        bank_read_done(b, sV, sV.val)
                        P.wait("vector", qTB[tsl], tab_ld)
                        hv = lambda base, lo: a32(base, 512).rearrange(
                            "p (h x) -> p h x", h=8)[:, :, lo:lo + 32]
                        hv16 = lambda base, lo: a16(base, 512).rearrange(
                            "p (h x) -> p h x", h=8)[:, :, lo:lo + 32]
                        P.do("vector", lambda e, o=hv(SQP[par], 0), i=hv(BP[par], 32),
                             j=hv16(sinb, 32): e.tensor_tensor(o, i, j, ALU.mult))
                        P.do("vector", lambda e, o=hv(SQP[par], 32), i=hv(BP[par], 0),
                             j=hv16(sinb, 0): e.tensor_tensor(o, i, j, ALU.mult),
                             dep=False)
                        P.do("vector", lambda e, o=Bp, i=Bp, j=a16(cosb, 512):
                             e.tensor_tensor(o, i, j, ALU.mult))
                        P.do("vector", lambda e, o=a16(dstw, 512), i=Bp,
                             j=a32(SQP[par], 512): e.tensor_tensor(o, i, j, ALU.add))
                        st7_v[ci] = sV.val
                        state["sq_free"][par] = (sV, sV.val)
                        state["b_free"][par] = (sV, sV.val)

                    def qk_transpose(side, bnk, ready_v):
                        P.wait("tensor", sV, ready_v)
                        for f0 in (0, 2):
                            P.wait_t("tensor", P.bank_free[bnk])
                            for k2, (fp, tt_) in enumerate(
                                    [(f0, 0), (f0, 1), (f0 + 1, 0), (f0 + 1, 1)]):
                                srcb = (XQW if side == 0 else XKW) + tt_ * 512
                                P.do("tensor", lambda e, o=bank(bnk, 128, off=k2 * 128),
                                     l=a16(srcb + fp * 128, 128):
                                     e.matmul(o, l, sb_idB[:], start=True, stop=True),
                                     sT if k2 == 3 else None, 1)
                            mmv = sT.val
                            P.wait("scalar", sT, mmv)
                            if side == 0:
                                dst = a16(XQT + f0 * 256, 512)
                            else:
                                dst = A16[:, R_KT:R_KT + 4 * T].rearrange(
                                    "p (f x) -> p f x", f=4)[
                                    :, f0:f0 + 2, w * 256:w * 256 + 256]
                            ev = P.do("scalar", lambda e, d=dst, s=bank(bnk, 512):
                                      e.copy(d, s), dep=(f0 == 0))
                            bank_read_done(bnk, sS, ev)

                    chain_tail(0)
                    chain_tail(1)
                    qk_transpose(0, 6, max(st7_v[0], st7_v[1]))
                    chain_tail(2)
                    chain_tail(3)
                    qk_transpose(1, 7, max(st7_v[2], st7_v[3]))
                    for tt in range(2):
                        vb = tt * 3 + 2
                        slot = (w * 2 + tt) % 4
                        P.wait("scalar", sT, qkv_mm_of[(tt, 2)])
                        P.wait_t("scalar", state["v_free"][slot])
                        ev = P.do("scalar", lambda e, o=a16(R_V + slot * 512, 512),
                                  i=bank(vb): e.copy(o, i), dep=(tt == 0))
                        bank_read_done(vb, sS, ev)
                    prep_s = sS.val
                    tr_s = sS.val

                    nj0 = 128 if w == 0 else 384
                    nj1 = 256 if w == 0 else 384
                    m0 = MTRI if w == 0 else MG0
                    m1 = MW01 if w == 0 else MG0
                    kbase0 = 0 if w == 0 else (w - 1) * 256
                    kbase1 = 0 if w == 0 else (w - 1) * 256 + 128
                    jtts0 = [0] if w == 0 else [2 * w - 2, 2 * w - 1, 2 * w]
                    jtts1 = [0, 1] if w == 0 else [2 * w - 1, 2 * w, 2 * w + 1]
                    nch0, nch1 = len(jtts0), len(jtts1)
                    P.wait("tensor", sS, tr_s)

                    def sim_S(h):
                        pt = h // 2
                        po = (h % 2) * 64
                        par = (h % 3) * 2
                        for qh, nj, msk in ((0, nj0, m0), (1, nj1, m1)):
                            bnk = par + qh
                            P.wait_t("tensor", P.bank_free[bnk])
                            P.do("tensor", lambda e, o=bank(bnk, nj),
                                 m=sb_mskb[:, msk:msk + nj]:
                                 e.matmul(o, sb_idB[:], m, start=True, stop=False))
                        for qh, nj, kb in ((0, nj0, kbase0), (1, nj1, kbase1)):
                            bnk = par + qh
                            lhs = a16(XQT + pt * 256 + qh * 128, 128, pr=64, po=po)
                            rhs = a16(R_KT + pt * 2048 + kb, nj, pr=64, po=po)
                            P.do("tensor", lambda e, o=bank(bnk, nj), l=lhs, r=rhs:
                                 e.matmul(o, l, r, start=False, stop=True), sT, 1)
                            P.bank_ready[bnk] = (sT, sT.val)
                        return sT.val

                    def sim_max(h):
                        par = (h % 3) * 2
                        P.wait("vector", sT, s_mm_of[h])
                        for qh, nj in ((0, nj0), (1, nj1)):
                            P.do("vector", lambda e,
                                 o=a32(sm["negmax"] + (h % 3) * 2 + qh, 1),
                                 i=bank(par + qh, nj):
                                 e.tensor_reduce(o, i, AX.X, ALU.max, negate=True),
                                 dep=False)
                        P.wait("vector", sS, max(st3_s[0], st3_s[1]))
                        for qh in range(2):
                            P.do("vector", lambda e,
                                 o=a32(sm["negmax"] + (h % 3) * 2 + qh, 1),
                                 i=a32(sm["negmax"] + (h % 3) * 2 + qh, 1),
                                 r=ssq_col(qh, h):
                                 e.tensor_scalar(o, i, r, None, ALU.mult),
                                 dep=(qh == 0))
                        msk_v_of[h] = sV.val

                    def sim_exp(h):
                        par = (h % 3) * 2
                        P.wait("scalar", sV, msk_v_of[h])
                        if h >= 3:
                            P.wait("scalar", sT, ptmm_of.get(h - 3, 0))
                        for qh, nj in ((0, nj0), (1, nj1)):
                            bnk = par + qh
                            ex = P.do("scalar", lambda e,
                                      o=a16(EP_E[h % 3] + qh * 384, nj),
                                      i=bank(bnk, nj),
                                      b=a32(sm["negmax"] + (h % 3) * 2 + qh, 1),
                                      r=ssq_col(qh, h),
                                      sx=a32(sm["sumexp"] + (h % 3) * 2 + qh, 1):
                                      e.activation(o, i, AF.Exp, bias=b, scale=r,
                                                   accum_out=sx), dep=(qh == 0))
                            bank_read_done(bnk, sS, ex)
                        exp_s_of[h] = sS.val

                    def sim_norm(h):
                        P.wait("vector", sS, exp_s_of[h])
                        P.do("vector", lambda e,
                             o=a32(sm["recip"] + (h % 3) * 2, 2),
                             i=a32(sm["sumexp"] + (h % 3) * 2, 2):
                             e.reciprocal(o, i), dep=False)
                        for qh, nj in ((0, nj0), (1, nj1)):
                            P.do("vector", lambda e, o=a16(EP_E[h % 3] + qh * 384, nj),
                                 i=a16(EP_E[h % 3] + qh * 384, nj),
                                 r=a32(sm["recip"] + (h % 3) * 2 + qh, 1):
                                 e.tensor_scalar(o, i, r, None, ALU.mult),
                                 dep=(qh == 0))
                        norm_v_of[h] = sV.val

                    def sim_ptpv(h):
                        pt = h // 2
                        po = (h % 2) * 64
                        P.wait("tensor", sV, norm_v_of[h])
                        tiles = [(0, jc) for jc in range(nch0)] + \
                                [(1, jc) for jc in range(nch1)]
                        bA = 6 + (h % 2)
                        bB = 7 - (h % 2)
                        evs = []
                        for gi, gstart in enumerate(range(0, len(tiles), 4)):
                            grp = tiles[gstart:gstart + 4]
                            bnk = bA if gi == 0 else bB
                            P.wait_t("tensor", P.bank_free[bnk])
                            for k2, (qh, jc) in enumerate(grp):
                                P.do("tensor", lambda e, o=bank(bnk, 128, off=k2 * 128),
                                     l=a16(EP_E[h % 3] + qh * 384 + jc * 128, 128):
                                     e.matmul(o, l, sb_idB[:], start=True, stop=True),
                                     sT if k2 == len(grp) - 1 else None, 1)
                            evs.append((bnk, sT.val, gstart, len(grp)))
                        ptmm_of[h] = sT.val
                        for bnk, mmv, gstart, glen in evs:
                            P.wait("scalar", sT, mmv)
                            ev = P.do("scalar", lambda e,
                                      d=a16(EP_PT + gstart * 128, 128 * glen),
                                      s=bank(bnk, 128 * glen): e.copy(d, s),
                                      dep=(gstart == 0))
                            bank_read_done(bnk, sS, ev)
                        pt_s = sS.val
                        P.wait("tensor", sS, pt_s)
                        P.wait("tensor", sS, prep_s)
                        P.wait_t("tensor", P.bank_free[bA])
                        for qh, jtts in ((0, jtts0), (1, jtts1)):
                            nch = len(jtts)
                            for ji, jtt in enumerate(jtts):
                                idx = ji if qh == 0 else nch0 + ji
                                P.do("tensor", lambda e,
                                     o=bank(bA, 128, off=qh * 128, pr=64, po=po),
                                     l=a16(R_V + (jtt % 4) * 512 + h * 64, 64),
                                     r=a16(EP_PT + idx * 128, 128),
                                     st=(ji == 0), sp=(ji == nch - 1):
                                     e.matmul(o, l, r, start=st, stop=sp),
                                     sT if ji == nch - 1 else None, 1)
                        pv_mm = sT.val
                        P.wait("scalar", sT, pv_mm)
                        ev = P.do("scalar", lambda e,
                                  d=a16(R_AT + pt * 2048 + w * 256, 256, pr=64, po=po),
                                  s=bank(bA, 256, pr=64, po=po): e.copy(d, s),
                                  dep=False)
                        bank_read_done(bA, sS, ev)

                    s_mm_of = {}
                    msk_v_of = {}
                    exp_s_of = {}
                    ptmm_of = {}
                    norm_v_of = {}
                    for i in range(11):
                        if i < 8:
                            s_mm_of[i] = sim_S(i)
                        if 1 <= i <= 8:
                            sim_max(i - 1)
                            sim_exp(i - 1)
                        if 2 <= i <= 9:
                            sim_norm(i - 2)
                        if i >= 3:
                            sim_ptpv(i - 3)
                    if w > 0:
                        for tt in range(2):
                            state["v_free"][((w - 1) * 2 + tt) % 4] = (sT, sT.val)
                    state["tab_free"][tsl] = (sV, sV.val)
                P.wait("tensor", qWO, wo_ld)
                P.wait("tensor", sS, sS.val)
                P.wait("vector", qRS, rs)
                for tg in range(4):
                    for mc in range(DC):
                        b = 5 + ((tg * DC + mc) % 2)
                        pairs = [(a16(R_WO + kc * 1024 + mc * 128, 128),
                                  a16(R_AT + kc * 2048 + tg * 512, 512))
                                 for kc in range(4)]
                        mmv = mm_acc(b, 512, pairs)
                        P.wait("vector", sT, mmv)
                        dstr = a32(R_RES + mc * T + tg * 512, 512)
                        ev = P.do("vector", lambda e, d=dstr, p_=bank(b):
                                  e.tensor_tensor(d, d, p_, ALU.add),
                                  dep=(tg == 0 and mc == 0))
                        bank_read_done(b, sV, ev)
                state["wo_free"] = (sT, sT.val)
                for sl4 in range(4):
                    state["v_free"][sl4] = (sT, sT.val)
            res_after_attn = sV.val
            if attn_only and ell == nlayers - 1:
                state["res_tok"] = (sV, sV.val)
                state["smalls_free"] = (sV, sV.val)
                return

            P.wait("scalar", sV, res_after_attn)
            for c4 in range(4):
                P.do("scalar", lambda e, d=a32(sm["bnd"] + c4, 1),
                     s=a32(R_RES + (4 + c4) * T + 1023, 1): e.copy(d, s))
            bnd_s = sS.val
            for half in range(2):
                hbase = half * 1024
                for tg2 in range(2):
                    tgb = hbase + tg2 * 512
                    P.wait("scalar", sV, res_after_attn if half == 0 else sV.val)
                    if half == 0:
                        P.wait("vector", sV, res_after_attn)
                    for dc in range(4):
                        P.do("scalar", lambda e, d=a16(XSF + dc * 512, 512),
                             s=a32(R_RES + dc * T + tgb, 512): e.copy(d, s),
                             dep=(dc == 0))
                    for dc in range(4, DC):
                        dst = a16(XSF + dc * 512, 512)
                        if tgb == 0:
                            P.do("vector", lambda e, d=a16(XSF + dc * 512, 1):
                                 e.memset(d, 0.0), dep=(dc == 4))
                            P.do("vector", lambda e, d=a16(XSF + dc * 512 + 1, 511),
                                 s=a32(R_RES + dc * T, 511): e.tensor_copy(d, s),
                                 dep=False)
                        elif tgb == 1024:
                            P.wait("vector", sS, bnd_s)
                            P.do("vector", lambda e, d=a16(XSF + dc * 512, 1),
                                 s=a32(sm["bnd"] + dc - 4, 1): e.tensor_copy(d, s),
                                 dep=(dc == 4))
                            P.do("vector", lambda e, d=a16(XSF + dc * 512 + 1, 511),
                                 s=a32(R_RES + dc * T + tgb, 511): e.tensor_copy(d, s),
                                 dep=False)
                        else:
                            P.do("vector", lambda e, d=dst,
                                 s=a32(R_RES + dc * T + tgb - 1, 512): e.tensor_copy(d, s),
                                 dep=(dc == 4))
                    xs_s = sS.val
                    xs_v = sV.val
                    P.wait("scalar", sT, sT.val)
                    sqv_fix = 0
                    for dc in range(DC):
                        o_full = a16(R_W + dc * 512, 512)
                        if dc < 4:
                            P.do("scalar", lambda e, o=o_full,
                                 i=a32(R_RES + dc * T + tgb, 512):
                                 e.activation(o, i, AF.Square), dep=(dc == 0))
                        elif tgb == 0:
                            P.wait("vector", sT, sT.val)
                            P.do("vector", lambda e, d=a16(R_W + dc * 512, 1):
                                 e.memset(d, 0.0), dep=False)
                            sqv_fix = sV.val
                            P.do("scalar", lambda e, o=a16(R_W + dc * 512 + 1, 511),
                                 i=a32(R_RES + dc * T, 511):
                                 e.activation(o, i, AF.Square), dep=False)
                        elif tgb == 1024:
                            P.do("scalar", lambda e, o=a16(R_W + dc * 512, 1),
                                 i=a32(sm["bnd"] + dc - 4, 1):
                                 e.activation(o, i, AF.Square), dep=False)
                            P.do("scalar", lambda e, o=a16(R_W + dc * 512 + 1, 511),
                                 i=a32(R_RES + dc * T + tgb, 511):
                                 e.activation(o, i, AF.Square), dep=False)
                        else:
                            P.do("scalar", lambda e, o=o_full,
                                 i=a32(R_RES + dc * T + tgb - 1, 512):
                                 e.activation(o, i, AF.Square), dep=False)
                    sq_s = sS.val
                    P.wait("tensor", sS, xs_s)
                    P.wait("tensor", sV, xs_v)
                    pairs = [(sb_c1024[:], a16(XSF + dc * 512, 512)) for dc in range(DC)]
                    mu_mm = mm_acc(6, 512, pairs, opr=1)
                    P.wait("tensor", sS, sq_s)
                    P.wait("tensor", sV, sqv_fix)
                    pairs = [(sb_c1024[:], a16(R_W + dc * 512, 512)) for dc in range(DC)]
                    sq_mm = mm_acc(5, 512, pairs, opr=1)
                    P.wait("vector", sT, mu_mm)
                    mu_c = P.do("vector", lambda e, o=a32(sm["murow"], 512, pr=1),
                                i=bank(6, 512, pr=1): e.tensor_copy(o, i))
                    bank_read_done(6, sV, mu_c)
                    P.wait("vector", sT, sq_mm)
                    vr_c = P.do("vector", lambda e, o=a32(sm["rstdrow"], 512, pr=1),
                                i=bank(5, 512, pr=1): e.tensor_copy(o, i), dep=False)
                    bank_read_done(5, sV, vr_c)
                    P.wait_t("tensor", P.bank_free[7])
                    P.wait("tensor", sV, mu_c)
                    P.do("tensor", lambda e, o=bank(7, 512),
                         l=sb_row1[:], r=a32(sm["murow"], 512, pr=1):
                         e.matmul(o, l, r, start=True, stop=True), sT, 1)
                    mub_mm = sT.val
                    P.do("vector", lambda e, o=a32(sm["qsrow"], 512, pr=1),
                         i=a32(sm["murow"], 512, pr=1), j=a32(sm["murow"], 512, pr=1):
                         e.tensor_tensor(o, i, j, ALU.mult))
                    P.do("vector", lambda e, o=a32(sm["rstdrow"], 512, pr=1),
                         i=a32(sm["rstdrow"], 512, pr=1), j=a32(sm["qsrow"], 512, pr=1):
                         e.tensor_tensor(o, i, j, ALU.subtract))
                    var_v = sV.val
                    P.wait("scalar", sV, var_v)
                    P.do("scalar", lambda e, o=a32(sm["rstdrow"], 512, pr=1),
                         i=a32(sm["rstdrow"], 512, pr=1):
                         e.activation(o, i, AF.Ln, bias=sb_eps[0:1, :]))
                    rst_s = P.do("scalar", lambda e, o=a32(sm["rstdrow"], 512, pr=1),
                                 i=a32(sm["rstdrow"], 512, pr=1):
                                 e.activation(o, i, AF.Exp, scale=-0.5))
                    P.wait_t("tensor", P.bank_free[6])
                    P.wait("tensor", sS, rst_s)
                    P.do("tensor", lambda e, o=bank(6, 512),
                         l=sb_row1[:], r=a32(sm["rstdrow"], 512, pr=1):
                         e.matmul(o, l, r, start=True, stop=True), sT, 1)
                    ab_mm = sT.val
                    P.wait("vector", sT, mub_mm)
                    for dc in range(DC):
                        P.do("vector", lambda e, o=a32(R_A + dc * 512, 512),
                             i=a16(XSF + dc * 512, 512), j=bank(7, 512):
                             e.tensor_tensor(o, i, j, ALU.subtract), dep=(dc == 0))
                    p1_v = sV.val
                    bank_read_done(7, sV, p1_v)
                    P.wait("vector", sT, ab_mm)
                    P.wait("vector", sS, smalls_s)
                    for dc in range(DC):
                        P.do("vector", lambda e, o=a32(R_A + dc * 512, 512),
                             i=a32(R_A + dc * 512, 512), j=bank(6, 512):
                             e.tensor_tensor(o, i, j, ALU.mult), dep=(dc == 0))
                        P.do("vector", lambda e, o=a16(R_AT + dc * 1024 + tg2 * 512, 512),
                             i=a32(R_A + dc * 512, 512), g=a32(sm["gcol"] + dc, 1),
                             bb=a32(sm["bcol"] + dc, 1):
                             e.tensor_scalar(o, i, g, bb, ALU.mult, ALU.add))
                    xn_v = sV.val
                    bank_read_done(6, sV, xn_v)
                xn_done = sV.val

                P.wait("gpsimd", sV, xn_done)
                for ci in range(NKC):
                    wci = 128 if ci < NKC - 1 else DI - 128 * (NKC - 1)
                    slot = ci % 2
                    P.wait_t("gpsimd", state["w1_free"][slot])
                    for gside in range(2):
                        c0 = gside * DI + ci * 128
                        src = w1_d[ell][:, c0:c0 + wci] \
                            .rearrange("(c p) m -> p c m", p=128)
                        dst = a16(W1RING[slot] + gside * 1024, DC * wci) \
                            .rearrange("p (c m) -> p c m", c=DC)
                        P.do("gpsimd", lambda e, d=dst, s=src: e.dma_start(out=d, in_=s),
                             qW1[slot], 16)
                    w1_ld = qW1[slot].val
                    P.wait("tensor", qW1[slot], w1_ld)
                    P.wait("tensor", sV, xn_done)
                    for tg2 in range(2):
                        for gside in range(2):
                            pairs = [(a16(W1RING[slot] + gside * 1024 + dc * wci, wci),
                                      a16(R_AT + dc * 1024 + tg2 * 512, 512))
                                     for dc in range(DC)]
                            mm_acc((ci % 2) * 4 + tg2 * 2 + gside, 512, pairs, opr=wci)
                    w1_mm = sT.val
                    state["w1_free"][slot] = (sT, w1_mm)
                    for tg2 in range(2):
                        ba, bg = (ci % 2) * 4 + tg2 * 2, (ci % 2) * 4 + tg2 * 2 + 1
                        P.wait("scalar", sT, w1_mm)
                        P.wait("scalar", sV, sV.val)
                        gl = P.do("scalar", lambda e, o=a16(R_V + 512, 512, pr=wci),
                                  i=bank(bg, 512, pr=wci),
                                  bb=a32(sm["b1g"] + ci, 1, pr=wci):
                                  e.activation(o, i, AF.Gelu, bias=bb), dep=False)
                        bank_read_done(bg, sS, gl)
                        P.wait("vector", sS, gl)
                        mv = P.do("vector", lambda e,
                                  o=a16(MID[ci] + tg2 * 512, 512, pr=wci),
                                  i=bank(ba, 512, pr=wci),
                                  bb=a32(sm["b1a"] + ci, 1, pr=wci),
                                  g=a16(R_V + 512, 512, pr=wci):
                                  e.scalar_tensor_tensor(o, i, bb, g, ALU.add,
                                                         ALU.mult), dep=False)
                        bank_read_done(ba, sV, mv)
                mid_v = sV.val
                mid_s = sS.val

                for tg2 in range(2):
                    tgb = hbase + tg2 * 512
                    w2_done = [None] * DC
                    for kc in range(NKC):
                        wkc = 128 if kc < NKC - 1 else DI - 128 * (NKC - 1)
                        slot = kc % 4
                        P.wait_t("gpsimd", state["w2_free"][slot])
                        src = w2_d[ell, kc * 128:kc * 128 + wkc, :]
                        wd = P.do("gpsimd", lambda e, d=a16(W2RING[slot], 1024, pr=wkc),
                                  s=src: e.dma_start(out=d, in_=s), qW2[slot], 16)
                        P.wait("tensor", qW2[slot], wd)
                        if kc == 0:
                            P.wait("tensor", sV, mid_v)
                            P.wait("tensor", sS, mid_s)
                            for b2_ in range(8):
                                P.wait_t("tensor", P.bank_free[b2_])
                        for mc in range(DC):
                            tok = P.do("tensor", lambda e, o=bank(mc, 512),
                                       l=a16(W2RING[slot] + mc * 128, 128, pr=wkc),
                                       r=a16(MID[kc] + tg2 * 512, 512, pr=wkc),
                                       st=(kc == 0), sp=(kc == NKC - 1):
                                       e.matmul(o, l, r, start=st, stop=sp),
                                       sT if (mc == DC - 1 or kc == NKC - 1) else None, 1)
                            if kc == NKC - 1:
                                w2_done[mc] = tok
                        state["w2_free"][slot] = (sT, sT.val)
                    for mc in range(DC):
                        P.wait("vector", sT, w2_done[mc])
                        dstr = a32(R_RES + mc * T + tgb, 512)
                        ev = P.do("vector", lambda e, d=dstr, p_=bank(mc),
                                  bb=a32(sm["b2col"] + mc, 1):
                                  e.scalar_tensor_tensor(d, p_, bb, d, ALU.add, ALU.add),
                                  dep=(mc == 0))
                        bank_read_done(mc, sV, ev)
            state["res_tok"] = (sV, sV.val)
            state["smalls_free"] = (sV, sV.val)
            for sl4 in range(4):
                state["v_free"][sl4] = (sT, sT.val)

        for ell in range(nlayers):
            emit_inner(ell)

        if dump:
            P.wait("sync", sV, sV.val)
            P.wait("sync", sS, sS.val)
            P.do("sync", lambda e: e.dma_start(out=ydbg_d, in_=a32(R_RES, DC * T)),
                 sO, 16)
            P.do("sync", lambda e: e.wait_ge(sO.h, sO.val))
        elif mode == "full":
            P.wait("sync", sV, sV.val)
            P.wait("sync", sS, sS.val)
            P.wait("tensor", sV, sV.val)
            P.wait("tensor", sS, sS.val)
            P.wait("scalar", sV, sV.val)
            P.wait("vector", sS, sS.val)
            P.wait("sync", qST, qST.val)
            for qt in range(NT):
                xd = P.do("sync", lambda e, d=a32(O_X0R[0], 1024),
                          s=x1s_d[qt * 128:(qt + 1) * 128, :]:
                          e.dma_start(out=d, in_=s), qX0[0], 16)
                for dcg in range(2):
                    b = dcg
                    P.wait_t("tensor", P.bank_free[b])
                    for k in range(4):
                        dc = dcg * 4 + k
                        src = a32(R_RES + dc * T + qt * 128, 128)
                        P.do("tensor", lambda e, o=bank(b, 128, off=k * 128), s=src:
                             e.matmul(o, s, sb_idF[:], start=True, stop=True),
                             sT if k == 3 else None, 1)
                    mmv = sT.val
                    P.wait("vector", sT, mmv)
                    P.wait("vector", qX0[0], xd)
                    ev = P.do("vector", lambda e, o=a32(O_ST + dcg * 512, 512),
                              p_=bank(b), xs=a32(O_X0R[0] + dcg * 512, 512):
                              e.tensor_tensor(o, p_, xs, ALU.add), sV, 1)
                    bank_read_done(b, sV, ev)
                P.wait("vector", sV, sV.val)
                stile = a32(O_ST, 1024)
                negmu = a32(sm["t1"], 1)
                P.do("vector", lambda e, o=negmu, i=stile: e.tensor_reduce(
                    o, i, AX.X, ALU.add, negate=True), sV, 1)
                P.wait("vector", sV, sV.val)
                P.do("vector", lambda e, o=negmu, i=negmu: e.tensor_scalar(
                    o, i, 1.0 / 1024.0, None, ALU.mult), sV, 1)
                mu_v = sV.val
                P.wait("scalar", sV, mu_v)
                ssq = a32(sm["t2"], 1)
                P.do("scalar", lambda e, o=a32(O_X1T, 1024), i=stile, b=negmu, sx=ssq:
                     e.activation(o, i, AF.Square, bias=b, accum_out=sx), sS, 1)
                P.do("scalar", lambda e, o=ssq, i=ssq: e.activation(
                    o, i, AF.Ln, scale=1.0 / 1024.0, bias=sb_eps[:]), sS, 1)
                P.do("scalar", lambda e, o=ssq, i=ssq: e.activation(
                    o, i, AF.Exp, scale=-0.5), sS, 1)
                rstd_s = sS.val
                P.wait("vector", sS, rstd_s)
                P.do("vector", lambda e, o=a32(sm["t0"], 1), i=negmu: e.tensor_tensor(
                    o, i, ssq, ALU.mult), sV, 1)
                m2_v = sV.val
                P.wait("vector", sV, m2_v)
                yv = P.do("vector", lambda e, o=a32(O_X1T, 1024), i=stile:
                          e.tensor_scalar(o, i, ssq, a32(sm["t0"], 1), ALU.mult, ALU.add),
                          sV, 1)
                P.wait("sync", sV, yv)
                P.do("sync", lambda e, s=a32(O_X1T, 1024),
                     d=y_d[qt * 128:(qt + 1) * 128, :]: e.dma_start(out=d, in_=s),
                     sO, 16)
                P.wait("tensor", sV, yv)
                P.wait("scalar", sO, sO.val)
                P.wait("sync", sV, sV.val)
            P.do("sync", lambda e: e.wait_ge(sO.h, sO.val))
        else:
            P.do("sync", lambda e: e.wait_ge(sO.h, sO.val))

        P.flush(block)
    return nc



def _in_maps(inputs, consts):
    x = np.ascontiguousarray(np.asarray(inputs["L0_attn_embeddings"], dtype=np.float32))
    base = {
        "wq": np.ascontiguousarray(np.asarray(inputs["L0_attn_W_Q"], np.float32)),
        "wk": np.ascontiguousarray(np.asarray(inputs["L0_attn_W_K"], np.float32)),
        "wv": np.ascontiguousarray(np.asarray(inputs["L0_attn_W_V"], np.float32)),
        "wqkv": np.ascontiguousarray(np.asarray(inputs["L0_ffn_Wqkv"], np.float32)),
        "qsc": np.ascontiguousarray(np.asarray(inputs["L0_ffn_q_scale"], np.float32)),
        "ksc": np.ascontiguousarray(np.asarray(inputs["L0_ffn_k_scale"], np.float32)),
        "wo": np.ascontiguousarray(np.asarray(inputs["L0_ffn_Wo"], np.float32)),
        "lng": np.ascontiguousarray(np.asarray(inputs["L0_ffn_ln_g"], np.float32)),
        "lnb": np.ascontiguousarray(np.asarray(inputs["L0_ffn_ln_b"], np.float32)),
        "w1": np.ascontiguousarray(np.asarray(inputs["L0_ffn_W1"], np.float32)),
        "b1": np.ascontiguousarray(np.asarray(inputs["L0_ffn_b1"], np.float32)),
        "w2": np.ascontiguousarray(np.asarray(inputs["L0_ffn_W2"], np.float32)),
        "b2": np.ascontiguousarray(np.asarray(inputs["L0_ffn_b2"], np.float32)),
    }
    base.update(consts)
    maps = []
    for c in range(NB):
        m = dict(base)
        m["x0"] = np.ascontiguousarray(x[c])
        maps.append(m)
    return maps


def kernel(**inputs):
    consts = _host_consts()
    nc = build_program(depth=DEPTH, mode="full")
    maps = _in_maps(inputs, consts)
    res = run_bass_kernel_spmd(nc, maps, core_ids=list(range(NB)))
    out = np.stack([np.asarray(r["y"], dtype=np.float32) for r in res.results])
    return out
